# version 15
# baseline (speedup 1.0000x reference)
from contextlib import ExitStack

import numpy as np

import concourse.bass as bass
import concourse.mybir as mybir
from concourse.bass_utils import run_bass_kernel_spmd

F32 = mybir.dt.float32
BF16 = mybir.dt.bfloat16
I32 = mybir.dt.int32
AF = mybir.ActivationFunctionType
ALU = mybir.AluOpType
AX = mybir.AxisListType

D = 1024
T = 4096
TQ = 2048
NEC = 8
TC = 256
NTC = T // TC
NH = 16
DHID = 4096
NHC = DHID // 128
QCW = 512
NQC = TQ // QCW
NKT = T // 128
KCW = 512
NKC = T // KCW
VROW = 1040


class SemC:
    def __init__(self, h):
        self.h = h
        self.n = 0

    def inc(self, k=1):
        self.n += k
        return self.n


def build():
    nc = bass.Bass("TRN2", target_bir_lowering=False)
    nc.detect_race_conditions = False

    xT = nc.dram_tensor("xT", [D, T], F32, kind="ExternalInput")
    qflag = nc.dram_tensor("qflag", [1, 1], I32, kind="ExternalInput")
    ident_in = nc.dram_tensor("ident_in", [128, 128], F32, kind="ExternalInput")
    oneslo_in = nc.dram_tensor("oneslo_in", [128, 1], F32, kind="ExternalInput")
    oneshi_in = nc.dram_tensor("oneshi_in", [128, 1], F32, kind="ExternalInput")
    l0_wkT = nc.dram_tensor("l0_wkT", [D, D], F32, kind="ExternalInput")
    l0_wvT = nc.dram_tensor("l0_wvT", [D, D], F32, kind="ExternalInput")
    l0_wrT = nc.dram_tensor("l0_wrT", [D, D], F32, kind="ExternalInput")
    l0_woT = nc.dram_tensor("l0_woT", [D, D], F32, kind="ExternalInput")
    f0_w = nc.dram_tensor("f0_w", [D, D], F32, kind="ExternalInput")
    wq1 = nc.dram_tensor("wq1", [8, 128, D], F32, kind="ExternalInput")
    wk1 = nc.dram_tensor("wk1", [8, 128, D], F32, kind="ExternalInput")
    wv1 = nc.dram_tensor("wv1", [8, 128, D], F32, kind="ExternalInput")
    wo1 = nc.dram_tensor("wo1", [8, 128, D], F32, kind="ExternalInput")
    w1t = nc.dram_tensor("w1t", [NHC, 128, D], F32, kind="ExternalInput")
    w2t = nc.dram_tensor("w2t", [NHC, 128, D], F32, kind="ExternalInput")
    h1t = nc.dram_tensor("h1t", [NHC, 128, DHID], F32, kind="ExternalInput")
    h2t = nc.dram_tensor("h2t", [NHC, 128, DHID], F32, kind="ExternalInput")
    w3t = nc.dram_tensor("w3t", [NEC, 128, DHID], F32, kind="ExternalInput")
    vin = {}
    for nm, k in (("u", 8), ("w", 8), ("tmk", 8), ("tmv", 8), ("tmr", 8),
                  ("bo", 8), ("b0f", 8), ("w1s", 32), ("w2s", 32), ("h1s", 32),
                  ("h2s", 32), ("w3s", 8)):
        vin[nm] = nc.dram_tensor("vec_" + nm, [128, k], F32, kind="ExternalInput")
    y = nc.dram_tensor("y", [D, TQ], F32, kind="ExternalOutput")

    x1_d = nc.dram_tensor("x1_d", [D, T], F32, kind="Internal")
    xn1_d = nc.dram_tensor("xn1_d", [D, T], BF16, kind="Internal")
    K_d = nc.dram_tensor("K_d", [D, T], BF16, kind="Internal")
    Q_d = nc.dram_tensor("Q_d", [D, TQ], BF16, kind="Internal")
    o_d = nc.dram_tensor("o_d", [D, TQ], BF16, kind="Internal")
    kn2_d = nc.dram_tensor("kn2_d", [NH, T], F32, kind="Internal")
    rows_d = nc.dram_tensor("rows_d", [8, T], F32, kind="Internal")
    wq1_b = nc.dram_tensor("wq1_b", [8, 128, D], BF16, kind="Internal")
    wk1_b = nc.dram_tensor("wk1_b", [8, 128, D], BF16, kind="Internal")
    wv1_b = nc.dram_tensor("wv1_b", [8, 128, D], BF16, kind="Internal")
    wo1_b = nc.dram_tensor("wo1_b", [8, 128, D], BF16, kind="Internal")
    w1t_b = nc.dram_tensor("w1t_b", [NHC, 128, D], BF16, kind="Internal")
    w2t_b = nc.dram_tensor("w2t_b", [NHC, 128, D], BF16, kind="Internal")
    h1t_b = nc.dram_tensor("h1t_b", [NHC, 128, DHID], BF16, kind="Internal")
    h2t_b = nc.dram_tensor("h2t_b", [NHC, 128, DHID], BF16, kind="Internal")
    w3t_b = nc.dram_tensor("w3t_b", [NEC, 128, DHID], BF16, kind="Internal")

    ctx = ExitStack()
    with ctx:
        mksem = lambda name: SemC(ctx.enter_context(nc.semaphore(name)))
        sLD = mksem("sLD")
        sLDo = mksem("sLDo")
        sWL = [mksem(f"sWL{i}") for i in range(6)]
        sWS = [mksem(f"sWS{i}") for i in range(6)]
        sCC = mksem("sCC")
        sLB1 = mksem("sLB1")
        sLB2 = mksem("sLB2")
        sST = mksem("sST")
        sSC = mksem("sSC")
        sSP = mksem("sSP")
        sSX = mksem("sSX")
        sSR = [mksem("sSRe"), mksem("sSRo")]
        sMM = mksem("sMM")
        sACT = mksem("sACT")
        sDVE = mksem("sDVE")

        def sb(st, name, shape, dt):
            return st.enter_context(nc.sbuf_tensor(name, shape, dt))

        ident = sb(ctx, "ident", [128, 128], F32)
        epsc = sb(ctx, "epsc", [1, 1], F32)
        ln127c = sb(ctx, "ln127c", [128, 1], F32)
        eps8c = sb(ctx, "eps8c", [128, 1], F32)
        neps8c = sb(ctx, "neps8c", [128, 1], F32)
        ones1 = sb(ctx, "ones1", [128, 1], F32)
        oneslo = sb(ctx, "oneslo", [128, 1], F32)
        oneshi = sb(ctx, "oneshi", [128, 1], F32)
        V = {nm: sb(ctx, "v_" + nm, [128, k], F32)
             for nm, k in (("u", 8), ("w", 8), ("negw", 8), ("umw", 8),
                           ("wmu", 8), ("m2w", 8), ("tmk", 8), ("tmv", 8),
                           ("tmr", 8), ("bo", 8), ("b0f", 8), ("w1s", 32),
                           ("w2s", 32), ("h1s", 32), ("h2s", 32), ("w3s", 8),
                           ("wscale0", 8), ("stA", 8), ("stB", 8), ("prevcol", 8))}
        ps = [ctx.enter_context(nc.psum_tensor(f"ps{i}", [128, 512], F32))
              for i in range(8)]
        flag_reg = ctx.enter_context(nc.sync.register("flag_reg"))

        with nc.Block() as blk:
            @blk.sync
            def _(sync):
                sync.dma_start(ident[:], ident_in.ap()).then_inc(sLD.h, 16)
                sync.dma_start(oneslo[:], oneslo_in.ap()).then_inc(sLD.h, 16)
                sync.dma_start(oneshi[:], oneshi_in.ap()).then_inc(sLD.h, 16)
                sLD.inc(48)
                for nm in ("u", "w", "tmk", "tmv", "tmr", "bo", "b0f", "w1s",
                           "w2s", "h1s", "h2s", "w3s"):
                    sync.dma_start(V[nm][:], vin[nm].ap()).then_inc(sLD.h, 16)
                    sLD.inc(16)
                sync.wait_ge(sLD.h, sLD.n)
                sync.reg_load(flag_reg, qflag.ap().to_broadcast((1, 1)))

            @blk.vector
            def _(v):
                v.memset(ones1[:], 1.0)
                v.memset(V["stA"][:], 0.0)
                v.memset(V["stB"][:], 0.0)
                v.memset(V["prevcol"][:], 0.0)
                v.memset(epsc[:], 1e-5)
                v.memset(ln127c[:], 4.844187086458591)
                v.memset(eps8c[:], 1e-8)
                v.memset(neps8c[:], -1e-8)
        with nc.Block() as blk:
            @blk.vector
            def _(v):
                v.tensor_scalar_mul(V["negw"][:], V["w"][:], -1.0)
                v.tensor_sub(V["umw"][:], V["u"][:], V["w"][:])
                v.tensor_sub(V["wmu"][:], V["w"][:], V["u"][:])
                v.tensor_scalar_mul(V["m2w"][:], V["w"][:], -2.0)

        with ExitStack() as pc:
            wk_sb = [sb(pc, f"wk_sb{i}", [128, D], BF16) for i in range(NEC)]
            wv_sb = [sb(pc, f"wv_sb{i}", [128, D], BF16) for i in range(NEC)]
            wr_sb = [sb(pc, f"wr_sb{i}", [128, D], BF16) for i in range(NEC)]
            wo_sb = [sb(pc, f"wo_sb{i}", [128, D], BF16) for i in range(NEC)]
            wq0_sb = [sb(pc, f"wq0_sb{i}", [128, D], BF16) for i in range(NEC)]
            wld = [sb(pc, f"wld{i}", [128, D], F32) for i in range(2)]

            for mat, dstl in ((l0_wkT, wk_sb), (l0_wvT, wv_sb), (l0_wrT, wr_sb),
                              (l0_woT, wo_sb)):
                ld0, dv0 = sLD.n, sDVE.n
                with nc.Block() as blk:
                    @blk.sync
                    def _(sync, mat=mat, ld0=ld0, dv0=dv0):
                        for dc in range(NEC):
                            if dc >= 2:
                                sync.wait_ge(sDVE.h, dv0 + dc - 1)
                            sync.dma_start(wld[dc % 2][:],
                                           mat.ap()[dc * 128:(dc + 1) * 128, :]
                                           ).then_inc(sLD.h, 16)

                    @blk.vector
                    def _(v, dstl=dstl, ld0=ld0):
                        for dc in range(NEC):
                            v.wait_ge(sLD.h, ld0 + 16 * min(dc + 2, NEC))
                            v.tensor_copy(dstl[dc][:], wld[dc % 2][:]
                                          ).then_inc(sDVE.h, 1)
                sLD.inc(16 * NEC)
                sDVE.inc(NEC)

            with ExitStack() as p2bs:
                wqf = sb(p2bs, "wqf", [128, D], F32)
                wamax = sb(p2bs, "wamax", [128, 1], F32)
                wr127 = sb(p2bs, "wr127", [128, 1], F32)
                rnd_i = sb(p2bs, "rnd_i", [128, D], I32)
                rnd_f = sb(p2bs, "rnd_f", [128, D], F32)
                rnd_d = sb(p2bs, "rnd_d", [128, D], F32)
                rnd_c = sb(p2bs, "rnd_c", [128, D], F32)
                for ec in range(NEC):
                    with nc.Block() as blk:
                        @blk.sync
                        def _(sync, ec=ec):
                            sync.dma_start(wld[0][:], f0_w.ap()[ec * 128:(ec + 1) * 128, :]
                                           ).then_inc(sLD.h, 16)
                            sLD.inc(16)
                            sync.wait_ge(sLD.h, sLD.n)
                    with nc.Block() as blk:
                        @blk.vector
                        def _(v, ec=ec):
                            v.reduce_max(wamax[:, 0:1], wld[0][:], axis=AX.X,
                                         apply_absolute_value=True
                                         ).then_inc(sDVE.h, 1)
                            sDVE.inc(1)
                            v.wait_ge(sDVE.h, sDVE.n)
                            v.tensor_scalar_max(wamax[:, 0:1], wamax[:, 0:1], 1e-8)
                            v.tensor_scalar_mul(V["wscale0"][:, ec:ec + 1],
                                                wamax[:, 0:1], 1.0 / 127.0)
                    with nc.Block() as blk:
                        @blk.vector
                        def _(v, ec=ec):
                            v.reciprocal(wr127[:, 0:1], V["wscale0"][:, ec:ec + 1])
                    with nc.Block() as blk:
                        @blk.vector
                        def _(v):
                            v.tensor_scalar_mul(wqf[:], wld[0][:], wr127[:, 0:1])
                            v.tensor_copy(rnd_i[:], wqf[:])
                            v.tensor_copy(rnd_f[:], rnd_i[:])
                            v.tensor_sub(rnd_d[:], wqf[:], rnd_f[:])
                            v.scalar_tensor_tensor(rnd_f[:], rnd_d[:], 0.5, rnd_f[:],
                                                   ALU.is_gt, ALU.add)
                            v.tensor_scalar(rnd_c[:], rnd_d[:], -0.5, None, ALU.is_lt)
                            v.tensor_sub(wqf[:], rnd_f[:], rnd_c[:])
                    mm0, act0 = sMM.n, sACT.n
                    with nc.Block() as blk:
                        @blk.tensor
                        def _(t, mm0=mm0, act0=act0):
                            for dc in range(NEC):
                                if dc >= 2:
                                    t.wait_ge(sACT.h, act0 + dc - 1)
                                t.transpose(ps[dc % 2][0:128, 0:128],
                                            wqf[:, dc * 128:(dc + 1) * 128], ident[:]
                                            ).then_inc(sMM.h, 1)

                        @blk.scalar
                        def _(s, ec=ec, mm0=mm0):
                            for dc in range(NEC):
                                s.wait_ge(sMM.h, mm0 + dc + 1)
                                s.copy(wq0_sb[dc][:, ec * 128:(ec + 1) * 128],
                                       ps[dc % 2][0:128, 0:128]).then_inc(sACT.h, 1)
                    sMM.inc(NEC)
                    sACT.inc(NEC)

            with ExitStack() as pa:
                wtmp = [sb(pa, f"wtmp{i}", [128, 2048], F32) for i in range(6)]
                wtmpb = [sb(pa, f"wtmpb{i}", [128, 2048], BF16) for i in range(6)]
                x_a = [[sb(pa, f"x_a{j}_{i}", [128, TC], F32) for i in range(NEC)]
                       for j in range(2)]
                x2a = [sb(pa, f"x2a{i}", [128, TC], F32) for i in range(2)]
                x2d = sb(pa, "x2d", [128, TC], F32)
                mrow = [sb(pa, f"mrow{i}", [1, TC], F32) for i in range(2)]
                vrow = [sb(pa, f"vrow{i}", [1, TC], F32) for i in range(2)]
                rrow = [sb(pa, f"rrow{i}", [1, TC], F32) for i in range(2)]
                nrow = [sb(pa, f"nrow{i}", [1, TC], F32) for i in range(2)]
                tiles = []
                for srcw, dstw, noc, kk in (
                        (wk1, wk1_b, 8, D), (wv1, wv1_b, 8, D),
                        (wq1, wq1_b, 8, D), (wo1, wo1_b, 8, D),
                        (w1t, w1t_b, NHC, D), (w2t, w2t_b, NHC, D),
                        (w3t, w3t_b, NEC, DHID),
                        (h1t, h1t_b, NHC, DHID), (h2t, h2t_b, NHC, DHID)):
                    for oc in range(noc):
                        for off in range(0, kk, 2048):
                            tiles.append((srcw, dstw, oc, off,
                                          min(2048, kk - off)))
                ncast = len(tiles)
                nbsem = [len([c for c in range(ncast) if c % 6 == b])
                         for b in range(6)]
                cbnd = [(i * ncast) // NTC for i in range(NTC + 1)]
                csplit = [list(range(cbnd[i], cbnd[i + 1]))
                          for i in range(NTC)]
                mm0, act0, dv0 = sMM.n, sACT.n, sDVE.n
                xsem = [sLD, sLDo]
                xb = [sLD.n, sLDo.n]
                xtot = [xb[tc % 2] + 128 * (tc // 2 + 1) for tc in range(NTC)]
                lqs = sWL
                sqs = sWS
                lq0 = [s.n for s in lqs]
                sq0 = [s.n for s in sqs]
                cc0 = sCC.n
                NAC = 10
                dvb = [dv0 + 2 * i for i in range(NTC + 1)]
                sr0 = [sSR[0].n, sSR[1].n]
                rrcnt = [0, 0]
                rrtot = [0] * (NTC + 1)
                for i in range(1, NTC + 1):
                    rrcnt[i % 2] += 32
                    rrtot[i] = sr0[i % 2] + rrcnt[i % 2]
                with nc.Block() as blk:
                    @blk.sync
                    def _(sync):
                        for tc in range(NTC):
                            if tc >= 2:
                                sync.wait_ge(sMM.h, mm0 + 8 * (tc - 1))
                            for ec in range(NEC):
                                sync.dma_start(
                                    x_a[tc % 2][ec][:],
                                    xT.ap()[ec * 128:(ec + 1) * 128,
                                            tc * TC:(tc + 1) * TC]
                                ).then_inc(xsem[tc % 2].h, 16)
                            for ci in csplit[tc]:
                                if ci % 2 != 0:
                                    continue
                                srcw, dstw, oc, off, w = tiles[ci]
                                if ci >= 6:
                                    sync.wait_ge(sCC.h, cc0 + ci - 5)
                                sync.dma_start(wtmp[ci % 6][:, 0:w],
                                               srcw.ap()[oc][:, off:off + w]
                                               ).then_inc(lqs[ci % 6].h, 16)

                    @blk.scalar
                    def _(s):
                        for tc in range(NTC):
                            for ec in range(NEC):
                                if ec == 0:
                                    s.wait_ge(xsem[tc % 2].h, xtot[tc])
                                if ec >= 2:
                                    s.wait_ge(sMM.h, mm0 + 8 * tc + ec - 1)
                                elif tc >= 1:
                                    s.wait_ge(sMM.h, mm0 + 8 * (tc - 1) + 7 + ec)
                                ins = s.activation(x2a[ec % 2][:],
                                                   x_a[tc % 2][ec][:], AF.Square)
                                if ec >= 1:
                                    ins.then_inc(sACT.h, 1)
                            s.activation(x2d[:], x_a[tc % 2][7][:], AF.Square
                                         ).then_inc(sACT.h, 1)
                            s.wait_ge(sDVE.h, dvb[tc] + 1)
                            if tc >= 2:
                                s.wait_ge(sSR[(tc - 1) % 2].h,
                                          rrtot[tc - 1])
                            s.activation(rrow[tc % 2][:], vrow[tc % 2][:], AF.Ln,
                                         bias=epsc[0:1, 0:1]).then_inc(sACT.h, 1)
                            s.activation(rrow[tc % 2][:], rrow[tc % 2][:], AF.Exp,
                                         scale=-0.5).then_inc(sACT.h, 1)
                            for ci in csplit[tc]:
                                if ci % 2 != 1:
                                    continue
                                srcw, dstw, oc, off, w = tiles[ci]
                                if ci >= 6:
                                    s.wait_ge(sCC.h, cc0 + ci - 5)
                                s.dma_start(wtmp[ci % 6][:, 0:w],
                                            srcw.ap()[oc][:, off:off + w]
                                            ).then_inc(lqs[ci % 6].h, 16)

                    @blk.tensor
                    def _(t):
                        for tc in range(NTC):
                            if tc >= 1:
                                t.wait_ge(sDVE.h, dvb[tc - 1] + 1)
                            t.wait_ge(xsem[tc % 2].h, xtot[tc])
                            for ec in range(NEC):
                                t.matmul(ps[6][0:1, 0:TC], ones1[:],
                                         x_a[tc % 2][ec][:],
                                         start=(ec == 0), stop=(ec == NEC - 1))
                                t.wait_ge(sACT.h, act0 + NAC * tc + ec + 1)
                                t.matmul(ps[7][0:1, 0:TC], ones1[:],
                                         x2a[ec % 2][:], start=(ec == 0),
                                         stop=(ec == NEC - 1)).then_inc(sMM.h, 1)

                    @blk.vector
                    def _(v):
                        for tc in range(NTC):
                            v.wait_ge(sMM.h, mm0 + 8 * (tc + 1))
                            if tc >= 2:
                                v.wait_ge(sSR[(tc - 1) % 2].h,
                                          rrtot[tc - 1])
                            v.tensor_scalar_mul(mrow[tc % 2][:], ps[6][0:1, 0:TC],
                                                1.0 / D)
                            v.tensor_mul(vrow[tc % 2][:], mrow[tc % 2][:],
                                         mrow[tc % 2][:])
                            v.scalar_tensor_tensor(vrow[tc % 2][:],
                                                   ps[7][0:1, 0:TC], 1.0 / D,
                                                   vrow[tc % 2][:], ALU.mult,
                                                   ALU.subtract).then_inc(sDVE.h, 1)
                            for ci in csplit[tc]:
                                srcw, dstw, oc, off, w = tiles[ci]
                                v.wait_ge(lqs[ci % 6].h,
                                          lq0[ci % 6] + 16 * (ci // 6 + 1))
                                if ci >= 6:
                                    v.wait_ge(sqs[ci % 6].h,
                                              sq0[ci % 6] + 16 * ((ci - 6) // 6 + 1))
                                v.tensor_copy(wtmpb[ci % 6][:, 0:w],
                                              wtmp[ci % 6][:, 0:w]
                                              ).then_inc(sCC.h, 1)
                            v.wait_ge(sACT.h, act0 + NAC * (tc + 1))
                            v.scalar_tensor_tensor(nrow[tc % 2][:],
                                                   mrow[tc % 2][:], -1.0,
                                                   rrow[tc % 2][:], ALU.mult,
                                                   ALU.mult).then_inc(sDVE.h, 1)

                    @blk.gpsimd
                    def _(g):
                        for tc in range(NTC):
                            if tc >= 1:
                                g.wait_ge(sACT.h, act0 + NAC * tc)
                                g.dma_start(
                                    rows_d.ap()[0:1, (tc - 1) * TC:tc * TC],
                                    rrow[(tc - 1) % 2][:]
                                ).then_inc(sSR[tc % 2].h, 16)
                                g.wait_ge(sDVE.h, dvb[tc])
                                g.dma_start(
                                    rows_d.ap()[1:2, (tc - 1) * TC:tc * TC],
                                    nrow[(tc - 1) % 2][:]
                                ).then_inc(sSR[tc % 2].h, 16)
                            for ci in csplit[tc]:
                                srcw, dstw, oc, off, w = tiles[ci]
                                g.wait_ge(sCC.h, cc0 + ci + 1)
                                g.dma_start(dstw.ap()[oc][:, off:off + w],
                                            wtmpb[ci % 6][:, 0:w]
                                            ).then_inc(sqs[ci % 6].h, 16)
                        g.wait_ge(sACT.h, act0 + NAC * NTC)
                        g.dma_start(
                            rows_d.ap()[0:1, (NTC - 1) * TC:NTC * TC],
                            rrow[(NTC - 1) % 2][:]).then_inc(sSR[0].h, 16)
                        g.wait_ge(sDVE.h, dvb[NTC])
                        g.dma_start(
                            rows_d.ap()[1:2, (NTC - 1) * TC:NTC * TC],
                            nrow[(NTC - 1) % 2][:]).then_inc(sSR[0].h, 16)
                sLD.n = xb[0] + 128 * 8
                sLDo.n = xb[1] + 128 * 8
                for b in range(6):
                    lqs[b].n = lq0[b] + 16 * nbsem[b]
                    sqs[b].n = sq0[b] + 16 * nbsem[b]
                sCC.n = cc0 + ncast
                sSR[0].n = sr0[0] + rrcnt[0]
                sSR[1].n = sr0[1] + rrcnt[1]
                sMM.inc(8 * NTC)
                sACT.inc(NAC * NTC)
                sDVE.n = dvb[NTC]
                with nc.Block() as blk:
                    @blk.sync
                    def _(sync):
                        sync.wait_ge(sLD.h, sLD.n)
                        sync.wait_ge(sLDo.h, sLDo.n)
                        for b in range(6):
                            sync.wait_ge(lqs[b].h, lqs[b].n)
                            sync.wait_ge(sqs[b].h, sqs[b].n)
                        sync.wait_ge(sSR[0].h, sSR[0].n)
                        sync.wait_ge(sSR[1].h, sSR[1].n)

            x_in = [[sb(pc, f"x_in{j}_{i}", [128, TC], F32) for i in range(NEC)]
                    for j in range(2)]
            xn = [sb(pc, f"xn{i}", [128, TC + 1], F32) for i in range(NEC)]
            dif = [sb(pc, f"dif{i}", [128, TC], F32) for i in range(NEC)]
            x1 = [sb(pc, f"x1_{i}", [128, TC], F32) for i in range(NEC)]
            kb = [sb(pc, f"kb{i}", [128, TC], BF16) for i in range(NEC)]
            vb = [sb(pc, f"vb{i}", [128, TC], BF16) for i in range(NEC)]
            rb = [sb(pc, f"rb{i}", [128, TC], BF16) for i in range(NEC)]
            wkvr = [sb(pc, f"wkvr{i}", [128, TC], BF16) for i in range(NEC)]
            xq = [sb(pc, f"xq{i}", [128, TC], BF16) for i in range(NEC)]
            x2t = [sb(pc, f"x2t{i}", [128, TC], F32) for i in range(2)]
            rm_t = [sb(pc, f"rm_t{i}", [128, TC], F32) for i in range(2)]
            att_t = [sb(pc, f"att_t{i}", [128, TC], F32) for i in range(2)]
            kf = sb(pc, "kf", [128, TC], F32)
            vf = [sb(pc, f"vf{i}", [128, TC], F32) for i in range(4)]
            rfp = [sb(pc, f"rfp{i}", [128, TC], F32) for i in range(4)]
            e1 = [sb(pc, f"e1_{i}", [128, TC], F32) for i in range(4)]
            e2 = [sb(pc, f"e2_{i}", [128, TC], F32) for i in range(4)]
            rcb = [sb(pc, f"rcb{i}", [128, TC], F32) for i in range(4)]
            sc_b = [sb(pc, f"sc_b{i}", [128, TC], F32) for i in range(4)]
            rr_t = sb(pc, "rr_t", [128, TC], F32)
            s1_t = sb(pc, "s1_t", [128, TC], F32)
            e2v = sb(pc, "e2v", [128, TC], F32)
            sc_a = sb(pc, "sc_a", [128, TC], F32)
            lnb = sb(pc, "lnb", [128, TC], F32)
            wkv = sb(pc, "wkv", [128, TC], F32)
            run_amax = sb(pc, "run_amax", [128, TC], F32)
            amax_tm = sb(pc, "amax_tm", [128, 2], F32)
            pack_tm = sb(pc, "pack_tm", [128, 4], F32)
            pack_row = sb(pc, "pack_row", [4, 128], F32)
            rnd_qi = sb(pc, "rnd_qi", [128, TC], I32)
            m_row = sb(pc, "m_row", [1, TC], F32)
            var_row = sb(pc, "var_row", [1, TC], F32)
            rstd_row = sb(pc, "rstd_row", [1, TC], F32)
            nmr_row = sb(pc, "nmr_row", [1, TC], F32)
            rstd_b = [sb(pc, f"rstd_b{i}", [128, TC], F32) for i in range(2)]
            nmr_b = [sb(pc, f"nmr_b{i}", [128, TC], F32) for i in range(2)]
            rstd1_b = sb(pc, "rstd1_b", [128, TC], F32)
            nmr1_b = sb(pc, "nmr1_b", [128, TC], F32)
            r127_b = sb(pc, "r127_b", [128, TC], F32)
            s_b = sb(pc, "s_b", [128, TC], F32)
            xn1_t = [sb(pc, f"xn1_t{i}", [128, TC], BF16) for i in range(2)]

            def ln_stats1(src_tiles, row_a, row_b, seg):
                mm0, act0 = sMM.n, sACT.n
                with nc.Block() as blk:
                    @blk.scalar
                    def _(s, mm0=mm0):
                        for ec in range(NEC):
                            if ec >= 2:
                                s.wait_ge(sMM.h, mm0 + 2 * (ec - 1))
                            s.activation(x2t[ec % 2][:], src_tiles[ec][:, 0:TC],
                                         AF.Square).then_inc(sACT.h, 1)

                    @blk.tensor
                    def _(t, mm0=mm0, act0=act0):
                        for ec in range(NEC):
                            t.matmul(ps[6][0:1, 0:TC], ones1[:],
                                     src_tiles[ec][:, 0:TC],
                                     start=(ec == 0), stop=(ec == NEC - 1))
                            t.wait_ge(sACT.h, act0 + ec + 1)
                            t.matmul(ps[7][0:1, 0:TC], ones1[:], x2t[ec % 2][:],
                                     start=(ec == 0), stop=(ec == NEC - 1)
                                     ).then_inc(sMM.h, 2)
                sACT.inc(NEC)
                sMM.inc(2 * NEC)
                with nc.Block() as blk:
                    @blk.vector
                    def _(v):
                        v.tensor_scalar_mul(m_row[:], ps[6][0:1, 0:TC], 1.0 / D)
                        v.tensor_mul(var_row[:], m_row[:], m_row[:])
                        v.scalar_tensor_tensor(var_row[:], ps[7][0:1, 0:TC],
                                               1.0 / D, var_row[:], ALU.mult,
                                               ALU.subtract).then_inc(sDVE.h, 1)
                    sDVE.inc(1)

                    @blk.scalar
                    def _(s):
                        s.wait_ge(sDVE.h, sDVE.n)
                        s.activation(rstd_row[:], var_row[:], AF.Ln,
                                     bias=epsc[0:1, 0:1])
                        s.activation(rstd_row[:], rstd_row[:], AF.Exp, scale=-0.5
                                     ).then_inc(sACT.h, 1)
                    sACT.inc(1)

                    @blk.vector
                    def _(v):
                        v.wait_ge(sACT.h, sACT.n)
                        v.scalar_tensor_tensor(nmr_row[:], m_row[:], -1.0,
                                               rstd_row[:], ALU.mult, ALU.mult)
                with nc.Block() as blk:
                    @blk.gpsimd
                    def _(g):
                        g.dma_start(rows_d.ap()[row_a:row_a + 1, seg:seg + TC],
                                    rstd_row[:]).then_inc(sSP.h, 16)
                        g.dma_start(rows_d.ap()[row_b:row_b + 1, seg:seg + TC],
                                    nmr_row[:]).then_inc(sSP.h, 16)
                    sSP.inc(32)

                    @blk.sync
                    def _(sync):
                        sync.wait_ge(sSP.h, sSP.n)
                        sync.dma_start(
                            rstd1_b[:], rows_d.ap()[row_a:row_a + 1, seg:seg + TC]
                            .to_broadcast((128, TC))).then_inc(sLB1.h, 16)
                        sync.dma_start(
                            nmr1_b[:], rows_d.ap()[row_b:row_b + 1, seg:seg + TC]
                            .to_broadcast((128, TC))).then_inc(sLB1.h, 16)
                        sLB1.inc(32)
                        sync.wait_ge(sLB1.h, sLB1.n)

            xcnt = [sLD.n, sLDo.n]
            xint = [0] * (NTC + 1)
            b2tot = [0] * (NTC + 1)
            with nc.Block() as blk:
                @blk.sync
                def _(sync):
                    for ec in range(NEC):
                        sync.dma_start(x_in[0][ec][:],
                                       xT.ap()[ec * 128:(ec + 1) * 128, 0:TC]
                                       ).then_inc(sLD.h, 16)
                        sLD.inc(16)
                    sync.dma_start(rstd_b[0][:], rows_d.ap()[0:1, 0:TC]
                                   .to_broadcast((128, TC))).then_inc(sLB2.h, 16)
                    sync.dma_start(nmr_b[0][:], rows_d.ap()[1:2, 0:TC]
                                   .to_broadcast((128, TC))).then_inc(sLB2.h, 16)
                    sLB2.inc(32)
                    sync.wait_ge(sLD.h, sLD.n)
                    sync.wait_ge(sLB2.h, sLB2.n)
            xcnt[0] += 128
            xint[0] = xcnt[0]
            b2tot[0] = sLB2.n

            VBl, ABl, TBl = [], [], []
            for tc in range(NTC):
                seg = tc * TC
                cp = tc % 2
                v0, a0, t0 = sDVE.n, sACT.n, sMM.n
                b1t = sLB1.n + 32
                spt = sSP.n + 32
                x1p = sST.n
                sxp = sSX.n
                VBl.append(v0); ABl.append(a0); TBl.append(t0)
                pv0 = VBl[tc - 1] if tc >= 1 else None
                pa0 = ABl[tc - 1] if tc >= 1 else None
                pt0 = TBl[tc - 1] if tc >= 1 else None
                Vdif = lambda e, b=v0: b + e + 1
                Vrm = lambda e, b=v0: b + 8 + e + 1
                Vkb = lambda e, b=v0: b + 16 + e + 1
                Vvb = lambda e, b=v0: b + 24 + e + 1
                Vb_ = lambda e, b=v0: b + 32 + 2 * e + 1
                Vwkvr = lambda e, b=v0: b + 32 + 2 * e + 2
                Vabsm = lambda e, b=v0: b + 48 + e
                Vpack = lambda b=v0: b + 56
                Vxq = lambda e, b=v0: b + 56 + e + 1
                Vx1a = lambda e, b=v0: b + 64 + e + 1
                Vfd = lambda e, b=v0: b + 72 + 2 * e + 1
                Vx1b = lambda e, b=v0: b + 72 + 2 * e + 2
                Asig = lambda e, b=a0: b + e + 1
                Akv = lambda e, b=a0: b + 8 + 4 * e + 1
                Ae = lambda e, b=a0: b + 8 + 4 * e + 2
                Ar = lambda e, b=a0: b + 8 + 4 * e + 3
                Arcb = lambda e, b=a0: b + 8 + 4 * e + 4
                Aabs = lambda e, b=a0: b + 41 + e
                Arec = lambda b=a0: b + 49
                Apack = lambda b=a0: b + 50
                Awo = lambda e, b=a0: b + 50 + e + 1
                Afb = lambda e, b=a0: b + 58 + e + 1
                Tk = lambda e, b=t0: b + 3 * e + 1
                Tv = lambda e, b=t0: b + 3 * e + 2
                Tr = lambda e, b=t0: b + 3 * e + 3
                Ttr = lambda g2, b=t0: b + 24 + g2 + 1
                Tpk = lambda b=t0: b + 27
                Two = lambda e, b=t0: b + 27 + e + 1
                Tf = lambda e, b=t0: b + 35 + e + 1

                with nc.Block() as blk:
                    @blk.vector
                    def _(v, tc=tc, seg=seg, cp=cp, v0=v0, a0=a0, t0=t0,
                          b1t=b1t, x1p=x1p,
                          pv0=pv0, pa0=pa0, pt0=pt0, Vdif=Vdif, Asig=Asig,
                          Aabs=Aabs, Ae=Ae, Arcb=Arcb, Ar=Ar, Ttr=Ttr, Awo=Awo,
                          Tf=Tf, Afb=Afb):
                        for ec in range(NEC):
                            if tc >= 1:
                                if ec == 0:
                                    v.wait_ge(xsem[tc % 2].h, xint[tc])
                                    v.wait_ge(sLB2.h, b2tot[tc])
                                v.wait_ge(sACT.h, pa0 + 41 + ec)
                            v.tensor_copy(xn[ec][:, 0:1], V["prevcol"][:, ec:ec + 1])
                            v.tensor_mul(xn[ec][:, 1:TC + 1], x_in[cp][ec][:],
                                         rstd_b[cp][:])
                            v.tensor_add(xn[ec][:, 1:TC + 1], xn[ec][:, 1:TC + 1],
                                         nmr_b[cp][:])
                            v.tensor_sub(dif[ec][:], xn[ec][:, 1:TC + 1],
                                         xn[ec][:, 0:TC]).then_inc(sDVE.h, 1)
                        for ec in range(NEC):
                            if ec >= 2:
                                v.wait_ge(sACT.h, Asig(ec - 2))
                            elif tc >= 1:
                                v.wait_ge(sACT.h, pa0 + 58 + 7 + ec)
                            v.scalar_tensor_tensor(rm_t[ec % 2][:], dif[ec][:],
                                                   V["tmr"][:, ec:ec + 1],
                                                   xn[ec][:, 0:TC], ALU.mult,
                                                   ALU.add).then_inc(sDVE.h, 1)
                        if tc >= 1:
                            v.wait_ge(sMM.h, pt0 + 24)
                        for ec in range(NEC):
                            v.scalar_tensor_tensor(kb[ec][:], dif[ec][:],
                                                   V["tmk"][:, ec:ec + 1],
                                                   xn[ec][:, 0:TC], ALU.mult,
                                                   ALU.add).then_inc(sDVE.h, 1)
                        for ec in range(NEC):
                            v.scalar_tensor_tensor(vb[ec][:], dif[ec][:],
                                                   V["tmv"][:, ec:ec + 1],
                                                   xn[ec][:, 0:TC], ALU.mult,
                                                   ALU.add).then_inc(sDVE.h, 1)
                        for ec in range(NEC):
                            v.tensor_copy(V["prevcol"][:, ec:ec + 1],
                                          xn[ec][:, TC:TC + 1])
                        for ec in range(NEC):
                            v.wait_ge(sACT.h, Ae(ec))
                            if ec >= 4:
                                v.wait_ge(sACT.h, Arcb(ec - 4))
                            elif tc >= 1:
                                v.wait_ge(sACT.h, pa0 + 8 + 4 * (4 + ec) + 4)
                            v.tensor_mul(e2v[:], e2[ec % 4][:], vf[ec % 4][:])
                            v.tensor_tensor_scan(sc_a[:], e1[ec % 4][:], e2v[:],
                                                 V["stA"][:, ec:ec + 1],
                                                 ALU.mult, ALU.add)
                            v.tensor_tensor_scan(sc_b[ec % 4][:], e1[ec % 4][:],
                                                 e2[ec % 4][:],
                                                 V["stB"][:, ec:ec + 1],
                                                 ALU.mult, ALU.add
                                                 ).then_inc(sDVE.h, 1)
                            v.tensor_copy(V["stA"][:, ec:ec + 1],
                                          sc_a[:, TC - 1:TC])
                            v.tensor_copy(V["stB"][:, ec:ec + 1],
                                          sc_b[ec % 4][:, TC - 1:TC])
                            v.wait_ge(sACT.h, Arcb(ec))
                            v.tensor_mul(wkv[:], sc_a[:], rcb[ec % 4][:])
                            v.wait_ge(sACT.h, Ar(ec))
                            if tc >= 1 and ec == 0:
                                v.wait_ge(sMM.h, pt0 + 35)
                            v.tensor_mul(wkvr[ec][:], wkv[:], rfp[ec % 4][:]
                                         ).then_inc(sDVE.h, 1)
                        for ec in range(1, NEC):
                            v.wait_ge(sACT.h, Aabs(ec))
                            v.tensor_max(run_amax[:], run_amax[:], x2t[ec % 2][:]
                                         ).then_inc(sDVE.h, 1)
                        for g2 in range(2):
                            v.wait_ge(sMM.h, Ttr(g2))
                            ins = v.reduce_max(amax_tm[:, g2:g2 + 1],
                                               ps[6 + g2][0:128, 0:128],
                                               axis=AX.X)
                            if g2 == 1:
                                ins.then_inc(sDVE.h, 1)
                        for ec in range(NEC):
                            if ec == 0:
                                v.wait_ge(sLB1.h, b1t)
                                if tc >= 1:
                                    v.wait_ge(sMM.h, pt0 + 43)
                            v.tensor_mul(dif[0][:], xn[ec][:, 1:TC + 1], r127_b[:])
                            v.tensor_copy(rnd_qi[:], dif[0][:])
                            v.tensor_copy(dif[1][:], rnd_qi[:])
                            v.tensor_sub(dif[2][:], dif[0][:], dif[1][:])
                            v.scalar_tensor_tensor(dif[1][:], dif[2][:], 0.5,
                                                   dif[1][:], ALU.is_gt, ALU.add)
                            v.tensor_scalar(dif[3][:], dif[2][:], -0.5, None,
                                            ALU.is_lt)
                            v.tensor_sub(xq[ec][:], dif[1][:], dif[3][:]
                                         ).then_inc(sDVE.h, 1)
                        for ec in range(NEC):
                            if ec == 0 and tc >= 1:
                                v.wait_ge(sST.h, x1p)
                            v.wait_ge(sACT.h, Awo(ec))
                            v.tensor_add(x1[ec][:], x_in[cp][ec][:],
                                         att_t[ec % 2][:]).then_inc(sDVE.h, 1)
                        for ec in range(NEC):
                            v.wait_ge(sMM.h, Tf(ec))
                            v.scalar_tensor_tensor(
                                rm_t[ec % 2][:],
                                ps[2 if ec % 2 == 0 else 5][:, 0:TC],
                                V["wscale0"][:, ec:ec + 1], s_b[:], ALU.mult,
                                ALU.mult).then_inc(sDVE.h, 1)
                            v.wait_ge(sACT.h, Afb(ec))
                            v.tensor_add(x1[ec][:], x1[ec][:], att_t[ec % 2][:]
                                         ).then_inc(sDVE.h, 1)

                    @blk.scalar
                    def _(s, tc=tc, cp=cp, v0=v0, a0=a0, t0=t0, pv0=pv0, pa0=pa0,
                          Tk=Tk, Tv=Tv, Tr=Tr, Vb_=Vb_, Vrm=Vrm, Vwkvr=Vwkvr,
                          Vdif=Vdif, Vabsm=Vabsm, Tpk=Tpk, Two=Two, Vx1a=Vx1a,
                          Vfd=Vfd, Vx1b=Vx1b, Vpack=Vpack):
                        for ec in range(NEC):
                            s.wait_ge(sDVE.h, Vrm(ec))
                            s.activation(rb[ec][:], rm_t[ec % 2][:], AF.Sigmoid
                                         ).then_inc(sACT.h, 1)
                        for ec in range(NEC):
                            p = ec % 2
                            s.wait_ge(sMM.h, Tk(ec))
                            s.copy(kf[:], ps[0 if p == 0 else 3][:, 0:TC])
                            s.wait_ge(sMM.h, Tv(ec))
                            if ec >= 4:
                                s.wait_ge(sDVE.h, Vb_(ec - 4))
                            elif tc >= 1:
                                s.wait_ge(sDVE.h, pv0 + 32 + 2 * (4 + ec) + 1)
                            s.copy(vf[ec % 4][:], ps[1 if p == 0 else 4][:, 0:TC]
                                   ).then_inc(sACT.h, 1)
                            s.activation(rr_t[:], kf[:], AF.Relu,
                                         bias=V["umw"][:, ec:ec + 1])
                            s.activation(e1[ec % 4][:], rr_t[:], AF.Exp,
                                         scale=-1.0, bias=V["m2w"][:, ec:ec + 1])
                            s.activation(s1_t[:], kf[:], AF.Relu, scale=-1.0,
                                         bias=V["wmu"][:, ec:ec + 1])
                            s.activation(e2[ec % 4][:], s1_t[:], AF.Exp,
                                         scale=-1.0).then_inc(sACT.h, 1)
                            s.wait_ge(sMM.h, Tr(ec))
                            if ec >= 4:
                                s.wait_ge(sDVE.h, Vwkvr(ec - 4))
                            elif tc >= 1:
                                s.wait_ge(sDVE.h, pv0 + 32 + 2 * (4 + ec) + 2)
                            s.copy(rfp[ec % 4][:], ps[2 if p == 0 else 5][:, 0:TC]
                                   ).then_inc(sACT.h, 1)
                            s.wait_ge(sDVE.h, Vb_(ec))
                            s.activation(lnb[:], sc_b[ec % 4][:], AF.Ln)
                            s.activation(rcb[ec % 4][:], lnb[:], AF.Exp,
                                         scale=-1.0).then_inc(sACT.h, 1)
                        s.wait_ge(sDVE.h, Vdif(0))
                        s.activation(run_amax[:], xn[0][:, 1:TC + 1], AF.Abs
                                     ).then_inc(sACT.h, 1)
                        for ec in range(1, NEC):
                            s.wait_ge(sDVE.h, Vdif(ec))
                            if ec >= 3:
                                s.wait_ge(sDVE.h, Vabsm(ec - 2))
                            s.activation(x2t[ec % 2][:], xn[ec][:, 1:TC + 1],
                                         AF.Abs).then_inc(sACT.h, 1)
                        s.wait_ge(sDVE.h, Vpack())
                        s.activation(rr_t[:, 0:2], amax_tm[:], AF.Relu,
                                     bias=neps8c[:, 0:1])
                        s.activation(s1_t[:, 0:2], rr_t[:, 0:2], AF.Identity,
                                     bias=eps8c[:, 0:1])
                        s.activation(pack_tm[:, 2:4], s1_t[:, 0:2], AF.Identity,
                                     scale=1.0 / 127.0)
                        s.activation(rr_t[:, 0:2], s1_t[:, 0:2], AF.Ln)
                        s.activation(pack_tm[:, 0:2], rr_t[:, 0:2], AF.Exp,
                                     scale=-1.0, bias=ln127c[:, 0:1]
                                     ).then_inc(sACT.h, 1)
                        s.wait_ge(sMM.h, Tpk())
                        s.copy(pack_row[:], ps[6][0:4, 0:128]).then_inc(sACT.h, 1)
                        for ec in range(NEC):
                            s.wait_ge(sMM.h, Two(ec))
                            if ec >= 2:
                                s.wait_ge(sDVE.h, Vx1a(ec - 2))
                            elif tc >= 1:
                                s.wait_ge(sDVE.h, pv0 + 72 + 2 * (6 + ec) + 2)
                            s.activation(att_t[ec % 2][:],
                                         ps[6 if ec % 2 == 0 else 7][:, 0:TC],
                                         AF.Identity, bias=V["bo"][:, ec:ec + 1]
                                         ).then_inc(sACT.h, 1)
                        for ec in range(NEC):
                            s.wait_ge(sDVE.h, Vfd(ec))
                            if ec >= 2:
                                s.wait_ge(sDVE.h, Vx1b(ec - 2))
                            s.activation(att_t[ec % 2][:], rm_t[ec % 2][:],
                                         AF.Identity, bias=V["b0f"][:, ec:ec + 1]
                                         ).then_inc(sACT.h, 1)

                    @blk.tensor
                    def _(t, tc=tc, v0=v0, a0=a0, t0=t0, pv0=pv0, pa0=pa0,
                          Vkb=Vkb, Vvb=Vvb, Akv=Akv, Asig=Asig, Ar=Ar,
                          Vabsm=Vabsm, Vpack=Vpack, Vwkvr=Vwkvr, Apack=Apack,
                          Arec=Arec,
                          Awo=Awo, Vxq=Vxq, Vfd=Vfd):
                        for ec in range(NEC):
                            p = ec % 2
                            if ec == 0:
                                t.wait_ge(sDVE.h, Vkb(7))
                            if ec >= 2:
                                t.wait_ge(sACT.h, Akv(ec - 2))
                            for dc in range(NEC):
                                ins = t.matmul(
                                    ps[0 if p == 0 else 3][:, 0:TC],
                                    wk_sb[dc][:, ec * 128:(ec + 1) * 128],
                                    kb[dc][:], start=(dc == 0), stop=(dc == 7))
                                if dc == 7:
                                    ins.then_inc(sMM.h, 1)
                            if ec == 0:
                                t.wait_ge(sDVE.h, Vvb(7))
                            for dc in range(NEC):
                                ins = t.matmul(
                                    ps[1 if p == 0 else 4][:, 0:TC],
                                    wv_sb[dc][:, ec * 128:(ec + 1) * 128],
                                    vb[dc][:], start=(dc == 0), stop=(dc == 7))
                                if dc == 7:
                                    ins.then_inc(sMM.h, 1)
                            if ec == 0:
                                t.wait_ge(sACT.h, Asig(7))
                            if ec >= 2:
                                t.wait_ge(sACT.h, Ar(ec - 2))
                            elif tc >= 1:
                                t.wait_ge(sDVE.h, pv0 + 72 + 2 * (6 + ec) + 1)
                            for dc in range(NEC):
                                ins = t.matmul(
                                    ps[2 if p == 0 else 5][:, 0:TC],
                                    wr_sb[dc][:, ec * 128:(ec + 1) * 128],
                                    rb[dc][:], start=(dc == 0), stop=(dc == 7))
                                if dc == 7:
                                    ins.then_inc(sMM.h, 1)
                        t.wait_ge(sDVE.h, Vabsm(7))
                        for g2 in range(2):
                            t.transpose(ps[6 + g2][0:128, 0:128],
                                        run_amax[:, g2 * 128:(g2 + 1) * 128],
                                        ident[:]).then_inc(sMM.h, 1)
                        t.wait_ge(sACT.h, Arec())
                        t.transpose(ps[6][0:4, 0:128], pack_tm[:], ident[:]
                                    ).then_inc(sMM.h, 1)
                        for ec in range(NEC):
                            if ec == 0:
                                t.wait_ge(sDVE.h, Vwkvr(7))
                                t.wait_ge(sACT.h, Apack())
                            if ec >= 2:
                                t.wait_ge(sACT.h, Awo(ec - 2))
                            for dc in range(NEC):
                                ins = t.matmul(
                                    ps[6 if ec % 2 == 0 else 7][:, 0:TC],
                                    wo_sb[dc][:, ec * 128:(ec + 1) * 128],
                                    wkvr[dc][:], start=(dc == 0), stop=(dc == 7))
                                if dc == 7:
                                    ins.then_inc(sMM.h, 1)
                        for ec in range(NEC):
                            if ec == 0:
                                t.wait_ge(sDVE.h, Vxq(7))
                            if ec >= 2:
                                t.wait_ge(sDVE.h, Vfd(ec - 2))
                            else:
                                t.wait_ge(sACT.h, Ar(6 + ec))
                            for dc in range(NEC):
                                ins = t.matmul(
                                    ps[2 if ec % 2 == 0 else 5][:, 0:TC],
                                    wq0_sb[dc][:, ec * 128:(ec + 1) * 128],
                                    xq[dc][:], start=(dc == 0), stop=(dc == 7))
                                if dc == 7:
                                    ins.then_inc(sMM.h, 1)

                    @blk.sync
                    def _(sync, tc=tc, seg=seg, spt=spt, pv0=pv0):
                        sync.wait_ge(sSP.h, spt)
                        if tc >= 1:
                            sync.wait_ge(sDVE.h, pv0 + 72 + 15)
                        sync.dma_start(r127_b[:], rows_d.ap()[2:3, seg:seg + TC]
                                       .to_broadcast((128, TC))).then_inc(sLB1.h, 16)
                        sync.dma_start(s_b[:], rows_d.ap()[3:4, seg:seg + TC]
                                       .to_broadcast((128, TC))).then_inc(sLB1.h, 16)
                        if tc < NTC - 1:
                            np_ = (tc + 1) % 2
                            if tc >= 1:
                                sync.wait_ge(sDVE.h, pv0 + 64 + 8)
                            for ec in range(NEC):
                                sync.dma_start(
                                    x_in[np_][ec][:],
                                    xT.ap()[ec * 128:(ec + 1) * 128,
                                            seg + TC:seg + 2 * TC]
                                ).then_inc(xsem[np_].h, 16)
                            sync.dma_start(
                                rstd_b[np_][:],
                                rows_d.ap()[0:1, seg + TC:seg + 2 * TC]
                                .to_broadcast((128, TC))).then_inc(sLB2.h, 16)
                            sync.dma_start(
                                nmr_b[np_][:],
                                rows_d.ap()[1:2, seg + TC:seg + 2 * TC]
                                .to_broadcast((128, TC))).then_inc(sLB2.h, 16)

                    @blk.gpsimd
                    def _(g, seg=seg, a0=a0, v0=v0, Apack=Apack, Vx1b=Vx1b):
                        g.wait_ge(sACT.h, Apack())
                        g.dma_start(rows_d.ap()[2:3, seg:seg + TC],
                                    pack_row[0:2, :]).then_inc(sSP.h, 16)
                        g.dma_start(rows_d.ap()[3:4, seg:seg + TC],
                                    pack_row[2:4, :]).then_inc(sSP.h, 16)
                        for ec in range(NEC):
                            g.wait_ge(sDVE.h, Vx1b(ec))
                            g.dma_start(x1_d.ap()[ec * 128:(ec + 1) * 128,
                                        seg:seg + TC], x1[ec][:]
                                        ).then_inc(sST.h, 16)
                sDVE.inc(88)
                sACT.inc(66)
                sMM.inc(43)
                sLB1.inc(32)
                sSP.inc(32)
                sST.inc(128)
                if tc < NTC - 1:
                    xsem[(tc + 1) % 2].inc(128)
                    xcnt[(tc + 1) % 2] += 128
                    xint[tc + 1] = xcnt[(tc + 1) % 2]
                    sLB2.inc(32)
                    b2tot[tc + 1] = sLB2.n
                ln_stats1(x1, 4, 5, seg)
                dv0b = sDVE.n
                with nc.Block() as blk:
                    @blk.vector
                    def _(v, dv0b=dv0b, sxp=sxp):
                        for ec in range(NEC):
                            if ec >= 2:
                                v.wait_ge(sSX.h, sxp + 16 * ec)
                            else:
                                v.wait_ge(sSX.h, sxp)
                            v.tensor_mul(dif[0][:], x1[ec][:], rstd1_b[:])
                            v.tensor_add(xn1_t[ec % 2][:], dif[0][:], nmr1_b[:]
                                         ).then_inc(sDVE.h, 1)

                    @blk.gpsimd
                    def _(g, seg=seg, dv0b=dv0b):
                        for ec in range(NEC):
                            g.wait_ge(sDVE.h, dv0b + ec + 1)
                            g.dma_start(xn1_d.ap()[ec * 128:(ec + 1) * 128,
                                        seg:seg + TC], xn1_t[ec % 2][:]
                                        ).then_inc(sSX.h, 16)
                sDVE.inc(NEC)
                sSX.inc(16 * NEC)
            with nc.Block() as blk:
                @blk.sync
                def _(sync):
                    sync.wait_ge(sLD.h, sLD.n)
                    sync.wait_ge(sLDo.h, sLDo.n)
                    sync.wait_ge(sLB1.h, sLB1.n)
                    sync.wait_ge(sLB2.h, sLB2.n)
                    sync.wait_ge(sST.h, sST.n)
                    sync.wait_ge(sSP.h, sSP.n)
                    sync.wait_ge(sSX.h, sSX.n)

        with ExitStack() as pc:
            V_sb = sb(pc, "V_sb", [128, NKT * VROW], BF16)
            cbias = sb(pc, "cbias", [128, 64], F32)
            with nc.Block() as blk:
                @blk.vector
                def _(v):
                    v.memset(V_sb[:], 1.0)

            with ExitStack() as p4:
                kw_sb = [sb(p4, f"kw1sb{i}", [128, D], BF16) for i in range(NEC)]
                vw_sb = [sb(p4, f"vw1sb{i}", [128, D], BF16) for i in range(NEC)]
                xn1c = [sb(p4, f"xn1c{i}", [128, KCW], BF16) for i in range(NEC)]
                ktile = [sb(p4, f"ktile{i}", [128, KCW], BF16) for i in range(2)]
                k2t = [sb(p4, f"k2t{i}", [128, KCW], F32) for i in range(2)]
                knrow = [sb(p4, f"knrow{i}", [1, KCW], F32) for i in range(4)]
                with nc.Block() as blk:
                    @blk.sync
                    def _(sync):
                        for ec in range(NEC):
                            sync.dma_start(kw_sb[ec][:], wk1_b.ap()[ec]
                                           ).then_inc(sLD.h, 16)
                            sync.dma_start(vw_sb[ec][:], wv1_b.ap()[ec]
                                           ).then_inc(sLD.h, 16)
                            sLD.inc(32)
                        sync.wait_ge(sLD.h, sLD.n)
                for kc in range(NKC):
                    seg = kc * KCW
                    with nc.Block() as blk:
                        @blk.sync
                        def _(sync, seg=seg):
                            for ec in range(NEC):
                                sync.dma_start(xn1c[ec][:],
                                               xn1_d.ap()[ec * 128:(ec + 1) * 128,
                                               seg:seg + KCW]).then_inc(sLD.h, 16)
                                sLD.inc(16)
                            sync.wait_ge(sLD.h, sLD.n)
                    mm0, act0, st0 = sMM.n, sACT.n, sST.n
                    with nc.Block() as blk:
                        @blk.tensor
                        def _(t, mm0=mm0, act0=act0):
                            for ec in range(NEC):
                                if ec >= 2:
                                    t.wait_ge(sACT.h, act0 + 4 * (ec - 2) + 2)
                                pk = ps[ec % 2]
                                for dc in range(NEC):
                                    ins = t.matmul(
                                        pk[:, 0:KCW],
                                        kw_sb[ec][:, dc * 128:(dc + 1) * 128],
                                        xn1c[dc][:], start=(dc == 0), stop=(dc == 7))
                                    if dc == 7:
                                        ins.then_inc(sMM.h, 1)
                                t.wait_ge(sACT.h, act0 + 4 * ec + 2)
                                if ec >= 1:
                                    t.wait_ge(sACT.h, act0 + 4 * (ec - 1) + 4)
                                t.matmul(ps[2][0:1, 0:KCW], oneslo[:],
                                         k2t[ec % 2][:], start=True, stop=True)
                                t.matmul(ps[3][0:1, 0:KCW], oneshi[:],
                                         k2t[ec % 2][:], start=True, stop=True
                                         ).then_inc(sMM.h, 1)

                        @blk.scalar
                        def _(s, mm0=mm0, act0=act0, st0=st0):
                            for ec in range(NEC):
                                pk = ps[ec % 2]
                                s.wait_ge(sMM.h, mm0 + 2 * ec + 1)
                                if ec >= 2:
                                    s.wait_ge(sST.h, st0 + 48 * ec)
                                s.copy(ktile[ec % 2][:], pk[:, 0:KCW])
                                s.activation(k2t[ec % 2][:], pk[:, 0:KCW],
                                             AF.Square).then_inc(sACT.h, 2)
                                s.wait_ge(sMM.h, mm0 + 2 * ec + 2)
                                s.copy(knrow[(2 * ec) % 4][:], ps[2][0:1, 0:KCW])
                                s.copy(knrow[(2 * ec + 1) % 4][:],
                                       ps[3][0:1, 0:KCW]).then_inc(sACT.h, 2)

                        @blk.gpsimd
                        def _(g, seg=seg, act0=act0):
                            for ec in range(NEC):
                                g.wait_ge(sACT.h, act0 + 4 * ec + 2)
                                g.dma_start(K_d.ap()[ec * 128:(ec + 1) * 128,
                                            seg:seg + KCW],
                                            ktile[ec % 2][:]).then_inc(sST.h, 16)
                                g.wait_ge(sACT.h, act0 + 4 * ec + 4)
                                g.dma_start(kn2_d.ap()[2 * ec:2 * ec + 1,
                                            seg:seg + KCW],
                                            knrow[(2 * ec) % 4][:]
                                            ).then_inc(sST.h, 16)
                                g.dma_start(kn2_d.ap()[2 * ec + 1:2 * ec + 2,
                                            seg:seg + KCW],
                                            knrow[(2 * ec + 1) % 4][:]
                                            ).then_inc(sST.h, 16)
                    sMM.inc(2 * NEC)
                    sACT.inc(4 * NEC)
                    sST.inc(48 * NEC)
                    mm0, act0 = sMM.n, sACT.n
                    with nc.Block() as blk:
                        @blk.tensor
                        def _(t, mm0=mm0, act0=act0):
                            for idx in range(2 * (KCW // 128)):
                                g2, nh2 = divmod(idx, 2)
                                if idx >= 2:
                                    t.wait_ge(sACT.h, act0 + idx - 1)
                                pv = ps[4 + idx % 2]
                                for dc in range(NEC):
                                    ins = t.matmul(
                                        pv[:, 0:512],
                                        xn1c[dc][:, g2 * 128:(g2 + 1) * 128],
                                        vw_sb[dc][:, nh2 * 512:(nh2 + 1) * 512],
                                        start=(dc == 0), stop=(dc == 7))
                                    if dc == 7:
                                        ins.then_inc(sMM.h, 1)

                        @blk.scalar
                        def _(s, kc=kc, mm0=mm0):
                            for idx in range(2 * (KCW // 128)):
                                g2, nh2 = divmod(idx, 2)
                                gg = kc * (KCW // 128) + g2
                                base = gg * VROW
                                pv = ps[4 + idx % 2]
                                s.wait_ge(sMM.h, mm0 + idx + 1)
                                dst = V_sb[:, base:base + VROW].rearrange(
                                    "p (h c) -> p h c", h=16)[:, 8 * nh2:8 * nh2 + 8,
                                                              0:64]
                                src = pv[:, 0:512].rearrange("p (h c) -> p h c", h=8)
                                s.copy(dst, src).then_inc(sACT.h, 1)
                    sMM.inc(2 * (KCW // 128))
                    sACT.inc(2 * (KCW // 128))
                with nc.Block() as blk:
                    @blk.sync
                    def _(sync):
                        sync.wait_ge(sST.h, sST.n)

            with ExitStack() as p5:
                qw_sb = [sb(p5, f"qw1sb{i}", [128, D], BF16) for i in range(NEC)]
                xn1q = [sb(p5, f"xn1q{i}", [128, TQ], BF16) for i in range(NEC)]
                qtile = [sb(p5, f"qtile{i}", [128, QCW], BF16) for i in range(2)]
                knrow_l = sb(p5, "knrow_l", [1, T], F32)
                kmax_row = sb(p5, "kmax_row", [1, NH], F32)
                kmaxd8 = sb(p5, "kmaxd8", [1, NH], F32)
                qh_l = [sb(p5, f"qh_l{i}", [64, TQ], BF16) for i in range(2)]
                qh_sq = [sb(p5, f"qh_sq{i}", [64, TQ], F32) for i in range(2)]
                qn_row = sb(p5, "qn_row", [1, TQ], F32)
                qm4 = sb(p5, "qm4", [1, NQC], F32)
                cneg = sb(p5, "cneg", [1, 64], F32)
                with nc.Block() as blk:
                    @blk.sync
                    def _(sync):
                        for ec in range(NEC):
                            sync.dma_start(qw_sb[ec][:], wq1_b.ap()[ec]
                                           ).then_inc(sLD.h, 16)
                            sLD.inc(16)
                        with sync.If_eq(flag_reg, 0):
                            for ec in range(NEC):
                                sync.dma_start(xn1q[ec][:],
                                               xn1_d.ap()[ec * 128:(ec + 1) * 128,
                                               0:TQ]).then_inc(sLD.h, 16)
                        with sync.Else():
                            for ec in range(NEC):
                                sync.dma_start(xn1q[ec][:],
                                               xn1_d.ap()[ec * 128:(ec + 1) * 128,
                                               TQ:T]).then_inc(sLD.h, 16)
                        sLD.inc(16 * NEC)
                        sync.wait_ge(sLD.h, sLD.n)
                ld0, dv0 = sLD.n, sDVE.n
                with nc.Block() as blk:
                    @blk.sync
                    def _(sync, ld0=ld0, dv0=dv0):
                        for h in range(NH):
                            if h >= 1:
                                sync.wait_ge(sDVE.h, dv0 + h)
                            sync.dma_start(knrow_l[:], kn2_d.ap()[h:h + 1, :]
                                           ).then_inc(sLD.h, 16)

                    @blk.vector
                    def _(v, ld0=ld0):
                        for h in range(NH):
                            v.wait_ge(sLD.h, ld0 + 16 * (h + 1))
                            v.reduce_max(kmax_row[:, h:h + 1], knrow_l[:],
                                         axis=AX.X).then_inc(sDVE.h, 1)
                sLD.inc(16 * NH)
                sDVE.inc(NH)
                with nc.Block() as blk:
                    @blk.scalar
                    def _(s):
                        s.activation(kmaxd8[:], kmax_row[:], AF.Sqrt
                                     ).then_inc(sACT.h, 1)
                    sACT.inc(1)

                    @blk.vector
                    def _(v):
                        v.wait_ge(sACT.h, sACT.n)
                        v.tensor_scalar_mul(kmaxd8[:], kmaxd8[:], 1.0 / 8.0)
                for ec in range(NEC):
                    mm0, act0, st0 = sMM.n, sACT.n, sST.n
                    with nc.Block() as blk:
                        @blk.tensor
                        def _(t, ec=ec, act0=act0):
                            for qc in range(NQC):
                                if qc >= 2:
                                    t.wait_ge(sACT.h, act0 + qc - 1)
                                pq = ps[qc % 2]
                                for dc in range(NEC):
                                    ins = t.matmul(
                                        pq[:, 0:QCW],
                                        qw_sb[ec][:, dc * 128:(dc + 1) * 128],
                                        xn1q[dc][:, qc * QCW:(qc + 1) * QCW],
                                        start=(dc == 0), stop=(dc == 7))
                                    if dc == 7:
                                        ins.then_inc(sMM.h, 1)

                        @blk.scalar
                        def _(s, mm0=mm0, st0=st0):
                            for qc in range(NQC):
                                pq = ps[qc % 2]
                                s.wait_ge(sMM.h, mm0 + qc + 1)
                                if qc >= 2:
                                    s.wait_ge(sST.h, st0 + 16 * qc)
                                s.copy(qtile[qc % 2][:], pq[:, 0:QCW]
                                       ).then_inc(sACT.h, 1)

                        @blk.gpsimd
                        def _(g, ec=ec, act0=act0):
                            for qc in range(NQC):
                                g.wait_ge(sACT.h, act0 + qc + 1)
                                g.dma_start(Q_d.ap()[ec * 128:(ec + 1) * 128,
                                            qc * QCW:(qc + 1) * QCW],
                                            qtile[qc % 2][:]).then_inc(sST.h, 16)
                    sMM.inc(NQC)
                    sACT.inc(NQC)
                    sST.inc(16 * NQC)
                with nc.Block() as blk:
                    @blk.sync
                    def _(sync):
                        sync.wait_ge(sST.h, sST.n)
                ld0, mm0, act0, dv0 = sLD.n, sMM.n, sACT.n, sDVE.n
                with nc.Block() as blk:
                    @blk.sync
                    def _(sync, ld0=ld0, act0=act0):
                        for h in range(NH):
                            if h >= 2:
                                sync.wait_ge(sACT.h, act0 + 6 * (h - 2) + 1)
                            sync.dma_start(qh_l[h % 2][:],
                                           Q_d.ap()[h * 64:(h + 1) * 64, :]
                                           ).then_inc(sLD.h, 16)

                    @blk.scalar
                    def _(s, ld0=ld0, mm0=mm0, dv0=dv0):
                        for h in range(NH):
                            s.wait_ge(sLD.h, ld0 + 16 * (h + 1))
                            s.activation(qh_sq[h % 2][:], qh_l[h % 2][:], AF.Square
                                         ).then_inc(sACT.h, 1)
                            for qc in range(NQC):
                                s.wait_ge(sMM.h, mm0 + 4 * h + qc + 1)
                                s.copy(qn_row[:, qc * QCW:(qc + 1) * QCW],
                                       ps[4 + (h % 2) * 0][0:1, 0:QCW] if False
                                       else ps[4 + (h + qc) % 2][0:1, 0:QCW]
                                       ).then_inc(sACT.h, 1)
                            s.wait_ge(sDVE.h, dv0 + 2 * h + 1)
                            s.activation(qm4[:], qm4[:], AF.Sqrt).then_inc(sACT.h, 1)

                    @blk.tensor
                    def _(t, act0=act0):
                        for h in range(NH):
                            t.wait_ge(sACT.h, act0 + 6 * h + 1)
                            for qc in range(NQC):
                                if 4 * h + qc >= 2:
                                    t.wait_ge(sACT.h, act0 + 6 * ((4 * h + qc - 2)
                                              // 4) + ((4 * h + qc - 2) % 4) + 2)
                                t.matmul(ps[4 + (h + qc) % 2][0:1, 0:QCW],
                                         ones1[0:64, :],
                                         qh_sq[h % 2][:, qc * QCW:(qc + 1) * QCW],
                                         start=True, stop=True).then_inc(sMM.h, 1)

                    @blk.vector
                    def _(v, act0=act0):
                        for h in range(NH):
                            v.wait_ge(sACT.h, act0 + 6 * h + 5)
                            for qc in range(NQC):
                                ins = v.reduce_max(qm4[:, qc:qc + 1],
                                                   qn_row[:, qc * QCW:(qc + 1) * QCW],
                                                   axis=AX.X)
                                if qc == NQC - 1:
                                    ins.then_inc(sDVE.h, 1)
                            v.wait_ge(sDVE.h, dv0 + 2 * h + 1)
                            v.wait_ge(sACT.h, act0 + 6 * h + 6)
                            v.tensor_scalar_mul(qm4[:], qm4[:],
                                                kmaxd8[0:1, h:h + 1])
                            v.tensor_scalar_mul(cneg[:, 4 * h:4 * h + 4], qm4[:],
                                                -1.0).then_inc(sDVE.h, 1)
                sLD.inc(16 * NH)
                sMM.inc(4 * NH)
                sACT.inc(6 * NH)
                sDVE.inc(2 * NH)
                with nc.Block() as blk:
                    @blk.gpsimd
                    def _(g):
                        g.dma_start(rows_d.ap()[6:7, 0:64], cneg[:]
                                    ).then_inc(sST.h, 16)
                    sST.inc(16)

                    @blk.sync
                    def _(sync):
                        sync.wait_ge(sST.h, sST.n)
                        sync.dma_start(cbias[:], rows_d.ap()[6:7, 0:64]
                                       .to_broadcast((128, 64))).then_inc(sLD.h, 16)
                        sLD.inc(16)
                        sync.wait_ge(sLD.h, sLD.n)

            with ExitStack() as p6:
                khh2 = [sb(p6, f"khh2_{i}", [128, T // 2], BF16) for i in range(2)]
                qh2 = [sb(p6, f"qh2_{i}", [128, TQ], BF16) for i in range(2)]
                pt = [sb(p6, f"pt{i}", [128, QCW], BF16) for i in range(4)]
                oraw = sb(p6, "oraw", [65, TQ], F32)
                ob_t = sb(p6, "ob_t", [65, QCW], F32)
                reciprow = sb(p6, "reciprow", [1, TQ], F32)
                rb64 = sb(p6, "rb64", [64, TQ], F32)
                o_store = sb(p6, "o_store", [64, TQ], BF16)
                ld0, st0, mm0, act0, dv0 = sLD.n, sST.n, sMM.n, sACT.n, sDVE.n
                lb0, sp0 = sLB1.n, sSP.n
                NJP = NKT // 2

                def cE(h, qc, j):
                    return act0 + 128 * h + 32 * qc + 2 * (j + 1)

                def cS(h, qc, j):
                    return mm0 + 128 * h + 32 * qc + (1 if j == 0 else 2 * j)

                def cAV(h, qc, i):
                    return (mm0 + 128 * h + 32 * qc
                            + (2 * i + 3 if i < NJP - 1 else 32))

                def gV(h, qc):
                    return dv0 + 6 * h + qc + 1

                def prev_jq(h, qc, j):
                    if j >= 2:
                        return (h, qc, j - 2)
                    if qc >= 1:
                        return (h, qc - 1, j + NJP - 2)
                    if h >= 1:
                        return (h - 1, NQC - 1, j + NJP - 2)
                    return None

                with nc.Block() as blk:
                    @blk.sync
                    def _(sync, ld0=ld0, sp0=sp0, mm0=mm0):
                        for h in range(NH):
                            if h >= 1:
                                sync.wait_ge(sSP.h, sp0 + 16 * h)
                                sync.dma_start(rb64[:], rows_d.ap()[7:8, 0:TQ]
                                               .to_broadcast((64, TQ))
                                               ).then_inc(sLB1.h, 16)
                            if h >= 2:
                                sync.wait_ge(sMM.h, mm0 + 128 * (h - 1))
                            sync.dma_start(khh2[h % 2][0:64, :],
                                           K_d.ap()[h * 64:(h + 1) * 64, 0:T // 2]
                                           ).then_inc(sLD.h, 16)
                            sync.dma_start(khh2[h % 2][64:128, :],
                                           K_d.ap()[h * 64:(h + 1) * 64, T // 2:T]
                                           ).then_inc(sLD.h, 16)
                            sync.dma_start(qh2[h % 2][0:64, :],
                                           Q_d.ap()[h * 64:(h + 1) * 64, :]
                                           ).then_inc(sLD.h, 16)
                            sync.dma_start(qh2[h % 2][64:128, :],
                                           Q_d.ap()[h * 64:(h + 1) * 64, :]
                                           ).then_inc(sLD.h, 16)
                        sync.wait_ge(sSP.h, sp0 + 16 * NH)
                        sync.dma_start(rb64[:], rows_d.ap()[7:8, 0:TQ]
                                       .to_broadcast((64, TQ))).then_inc(sLB1.h, 16)

                    @blk.tensor
                    def _(t, ld0=ld0, mm0=mm0, act0=act0, dv0=dv0):
                        for h in range(NH):
                            t.wait_ge(sLD.h, ld0 + 64 * (h + 1))
                            for qc in range(NQC):
                                qs = slice(qc * QCW, (qc + 1) * QCW)

                                def spair(j, t=t, h=h, qc=qc, qs=qs):
                                    pv = prev_jq(h, qc, j)
                                    if pv is not None:
                                        t.wait_ge(sACT.h, cE(*pv))
                                    bs = 2 * (j % 2)
                                    t.matmul(ps[bs][:, 0:QCW],
                                             khh2[h % 2][0:64,
                                                         j * 128:(j + 1) * 128],
                                             qh2[h % 2][0:64, qs],
                                             start=True, stop=True)
                                    t.matmul(ps[bs + 1][:, 0:QCW],
                                             khh2[h % 2][64:128,
                                                         j * 128:(j + 1) * 128],
                                             qh2[h % 2][64:128, qs],
                                             start=True, stop=True
                                             ).then_inc(sMM.h, 1)

                                def avg(i, t=t, h=h, qc=qc, qs=qs):
                                    t.wait_ge(sACT.h, cE(h, qc, i))
                                    if i == 0:
                                        if qc >= 1:
                                            t.wait_ge(sDVE.h, gV(h, qc - 1))
                                        elif h >= 1:
                                            t.wait_ge(sDVE.h, gV(h - 1, NQC - 1))
                                    for half, ktk in ((0, i), (1, i + NJP)):
                                        vb = V_sb[:, ktk * VROW + 65 * h:
                                                  ktk * VROW + 65 * h + 65]
                                        pm = pt[(2 * i + half) % 4]
                                        t.matmul(
                                            ps[4][0:65, 0:QCW], vb[0:64, :],
                                            pm[0:64, :],
                                            start=(i == 0 and half == 0),
                                            stop=(i == NJP - 1 and half == 1))
                                        ins = t.matmul(
                                            ps[5][0:65, 0:QCW], vb[64:128, :],
                                            pm[64:128, :],
                                            start=(i == 0 and half == 0),
                                            stop=(i == NJP - 1 and half == 1))
                                        if half == 1:
                                            ins.then_inc(sMM.h, 1)

                                for j in range(NJP):
                                    spair(j)
                                    if j >= 1:
                                        avg(j - 1)
                                avg(NJP - 1)

                    @blk.scalar
                    def _(s, sp0=sp0, mm0=mm0, act0=act0, dv0=dv0):
                        for h in range(NH):
                            for qc in range(NQC):
                                for j in range(NJP):
                                    s.wait_ge(sMM.h, cS(h, qc, j))
                                    pv = prev_jq(h, qc, j)
                                    if pv is not None:
                                        s.wait_ge(sMM.h, cAV(*pv))
                                    bs = 2 * (j % 2)
                                    s.activation(pt[(2 * j) % 4][:],
                                                 ps[bs][:, 0:QCW], AF.Exp,
                                                 scale=0.125,
                                                 bias=cbias[:, 4 * h + qc:
                                                            4 * h + qc + 1]
                                                 ).then_inc(sACT.h, 1)
                                    s.activation(pt[(2 * j + 1) % 4][:],
                                                 ps[bs + 1][:, 0:QCW], AF.Exp,
                                                 scale=0.125,
                                                 bias=cbias[:, 4 * h + qc:
                                                            4 * h + qc + 1]
                                                 ).then_inc(sACT.h, 1)

                    @blk.gpsimd
                    def _(g, act0=act0, dv0=dv0):
                        for h in range(NH):
                            g.wait_ge(sDVE.h, dv0 + 6 * h + 5)
                            g.dma_start(rows_d.ap()[7:8, 0:TQ], reciprow[:]
                                        ).then_inc(sSP.h, 16)
                            g.wait_ge(sDVE.h, dv0 + 6 * h + 6)
                            g.dma_start(o_d.ap()[h * 64:(h + 1) * 64, :],
                                        o_store[:]).then_inc(sST.h, 16)

                    @blk.vector
                    def _(v, lb0=lb0, st0=st0, mm0=mm0, sp0=sp0):
                        for h in range(NH):
                            for qc in range(NQC):
                                v.wait_ge(sMM.h, cAV(h, qc, NJP - 1))
                                v.tensor_copy(ob_t[:], ps[5][0:65, 0:QCW])
                                v.tensor_add(oraw[0:65, qc * QCW:(qc + 1) * QCW],
                                             ps[4][0:65, 0:QCW], ob_t[:]
                                             ).then_inc(sDVE.h, 1)
                            if h >= 1:
                                v.wait_ge(sSP.h, sp0 + 16 * h)
                            v.reciprocal(reciprow[:], oraw[64:65, :]
                                         ).then_inc(sDVE.h, 1)
                            v.wait_ge(sLB1.h, lb0 + 16 * (h + 1))
                            if h >= 1:
                                v.wait_ge(sST.h, st0 + 16 * h)
                            v.tensor_mul(o_store[:], oraw[0:64, :], rb64[:]
                                         ).then_inc(sDVE.h, 1)
                sLD.inc(64 * NH)
                sLB1.inc(16 * NH)
                sSP.inc(16 * NH)
                sST.inc(16 * NH)
                sMM.inc(128 * NH)
                sACT.inc(128 * NH)
                sDVE.inc(6 * NH)
                with nc.Block() as blk:
                    @blk.sync
                    def _(sync):
                        sync.wait_ge(sST.h, sST.n)
                        sync.wait_ge(sLD.h, sLD.n)
                        sync.wait_ge(sLB1.h, sLB1.n)
                        sync.wait_ge(sSP.h, sSP.n)

        with ExitStack() as pc:
            acc = [sb(pc, f"acc{i}", [128, TQ], F32) for i in range(NEC)]
            with nc.Block() as blk:
                @blk.sync
                def _(sync):
                    with sync.If_eq(flag_reg, 0):
                        for ec in range(NEC):
                            sync.dma_start(acc[ec][:],
                                           x1_d.ap()[ec * 128:(ec + 1) * 128, 0:TQ]
                                           ).then_inc(sLD.h, 16)
                    with sync.Else():
                        for ec in range(NEC):
                            sync.dma_start(acc[ec][:],
                                           x1_d.ap()[ec * 128:(ec + 1) * 128, TQ:T]
                                           ).then_inc(sLD.h, 16)
                    sLD.inc(16 * NEC)
                    sync.wait_ge(sLD.h, sLD.n)
            with ExitStack() as p7:
                o_sb = [sb(p7, f"o_sb{i}", [128, TQ], BF16) for i in range(NEC)]
                ow_sb = [sb(p7, f"owsb{i}", [128, D], BF16) for i in range(NEC)]
                wo_ev = [sb(p7, f"wo_ev{i}", [128, QCW], F32) for i in range(2)]
                with nc.Block() as blk:
                    @blk.sync
                    def _(sync):
                        for dc in range(NEC):
                            sync.dma_start(o_sb[dc][:],
                                           o_d.ap()[dc * 128:(dc + 1) * 128, :]
                                           ).then_inc(sLD.h, 16)
                            sLD.inc(16)
                        for ec in range(NEC):
                            sync.dma_start(ow_sb[ec][:], wo1_b.ap()[ec]
                                           ).then_inc(sLD.h, 16)
                            sLD.inc(16)
                        sync.wait_ge(sLD.h, sLD.n)
                mm0, act0, dv0 = sMM.n, sACT.n, sDVE.n
                with nc.Block() as blk:
                    @blk.tensor
                    def _(t, act0=act0):
                        i = 0
                        for ec in range(NEC):
                            for qc in range(NQC):
                                if i >= 2:
                                    t.wait_ge(sACT.h, act0 + i - 1)
                                pw = ps[i % 2]
                                for dc in range(NEC):
                                    ins = t.matmul(
                                        pw[:, 0:QCW],
                                        ow_sb[ec][:, dc * 128:(dc + 1) * 128],
                                        o_sb[dc][:, qc * QCW:(qc + 1) * QCW],
                                        start=(dc == 0), stop=(dc == 7))
                                    if dc == 7:
                                        ins.then_inc(sMM.h, 1)
                                i += 1

                    @blk.scalar
                    def _(s, mm0=mm0, dv0=dv0):
                        i = 0
                        for ec in range(NEC):
                            for qc in range(NQC):
                                s.wait_ge(sMM.h, mm0 + i + 1)
                                if i >= 2:
                                    s.wait_ge(sDVE.h, dv0 + i - 1)
                                s.copy(wo_ev[i % 2][:], ps[i % 2][:, 0:QCW]
                                       ).then_inc(sACT.h, 1)
                                i += 1

                    @blk.vector
                    def _(v, act0=act0):
                        i = 0
                        for ec in range(NEC):
                            for qc in range(NQC):
                                v.wait_ge(sACT.h, act0 + i + 1)
                                v.tensor_add(acc[ec][:, qc * QCW:(qc + 1) * QCW],
                                             acc[ec][:, qc * QCW:(qc + 1) * QCW],
                                             wo_ev[i % 2][:]).then_inc(sDVE.h, 1)
                                i += 1
                sMM.inc(NEC * NQC)
                sACT.inc(NEC * NQC)
                sDVE.inc(NEC * NQC)
            with ExitStack() as p8:
                xq1 = [sb(p8, f"xq1_{i}", [128, QCW], BF16) for i in range(NEC)]
                ha = [sb(p8, f"ha{i}", [128, QCW], BF16) for i in range(NHC)]
                hb = [sb(p8, f"hb{i}", [128, QCW], BF16) for i in range(NHC)]
                wt1 = [sb(p8, f"wt1_{i}", [128, D], BF16) for i in range(2)]
                wt2 = [sb(p8, f"wt2_{i}", [128, D], BF16) for i in range(2)]
                wth = [sb(p8, f"wth{i}", [128, NHC * 128], BF16) for i in range(2)]
                sil_t = sb(p8, "sil_t", [128, QCW], BF16)
                u_t = sb(p8, "u_t", [128, QCW], BF16)
                w3_ev = [sb(p8, f"w3_ev{i}", [128, QCW], F32) for i in range(2)]
                for qc in range(NQC):
                    with nc.Block() as blk:
                        @blk.sync
                        def _(sync, qc=qc):
                            with sync.If_eq(flag_reg, 0):
                                for ec in range(NEC):
                                    sync.dma_start(
                                        xq1[ec][:],
                                        xn1_d.ap()[ec * 128:(ec + 1) * 128,
                                                   qc * QCW:(qc + 1) * QCW]
                                    ).then_inc(sLD.h, 16)
                            with sync.Else():
                                for ec in range(NEC):
                                    sync.dma_start(
                                        xq1[ec][:],
                                        xn1_d.ap()[ec * 128:(ec + 1) * 128,
                                                   TQ + qc * QCW:
                                                   TQ + (qc + 1) * QCW]
                                    ).then_inc(sLD.h, 16)
                            sLD.inc(16 * NEC)
                            sync.wait_ge(sLD.h, sLD.n)
                    ld0, mm0, act0, dv0 = sLD.n, sMM.n, sACT.n, sDVE.n
                    with nc.Block() as blk:
                        @blk.sync
                        def _(sync, ld0=ld0, mm0=mm0):
                            for hc in range(NHC):
                                if hc >= 2:
                                    sync.wait_ge(sMM.h, mm0 + 2 * (hc - 1))
                                sync.dma_start(wt1[hc % 2][:], w1t_b.ap()[hc]
                                               ).then_inc(sLD.h, 16)
                                sync.dma_start(wt2[hc % 2][:], w2t_b.ap()[hc]
                                               ).then_inc(sLD.h, 16)

                        @blk.tensor
                        def _(t, ld0=ld0, act0=act0):
                            for hc in range(NHC):
                                t.wait_ge(sLD.h, ld0 + 32 * (hc + 1))
                                if hc >= 2:
                                    t.wait_ge(sACT.h, act0 + 2 * (hc - 1))
                                p1 = ps[2 * (hc % 2)]
                                p2 = ps[2 * (hc % 2) + 1]
                                for dc in range(NEC):
                                    ins = t.matmul(
                                        p1[:, 0:QCW],
                                        wt1[hc % 2][:, dc * 128:(dc + 1) * 128],
                                        xq1[dc][:], start=(dc == 0), stop=(dc == 7))
                                    if dc == 7:
                                        ins.then_inc(sMM.h, 1)
                                for dc in range(NEC):
                                    ins = t.matmul(
                                        p2[:, 0:QCW],
                                        wt2[hc % 2][:, dc * 128:(dc + 1) * 128],
                                        xq1[dc][:], start=(dc == 0), stop=(dc == 7))
                                    if dc == 7:
                                        ins.then_inc(sMM.h, 1)

                        @blk.scalar
                        def _(s, mm0=mm0, dv0=dv0):
                            for hc in range(NHC):
                                p1 = ps[2 * (hc % 2)]
                                p2 = ps[2 * (hc % 2) + 1]
                                s.wait_ge(sMM.h, mm0 + 2 * hc + 1)
                                if hc >= 1:
                                    s.wait_ge(sDVE.h, dv0 + hc)
                                s.activation(sil_t[:], p1[:, 0:QCW], AF.Silu,
                                             scale=V["w1s"][:, hc:hc + 1])
                                s.wait_ge(sMM.h, mm0 + 2 * hc + 2)
                                s.activation(u_t[:], p2[:, 0:QCW], AF.Identity,
                                             scale=V["w2s"][:, hc:hc + 1]
                                             ).then_inc(sACT.h, 2)

                        @blk.vector
                        def _(v, act0=act0):
                            for hc in range(NHC):
                                v.wait_ge(sACT.h, act0 + 2 * hc + 2)
                                v.tensor_mul(ha[hc][:], sil_t[:], u_t[:]
                                             ).then_inc(sDVE.h, 1)
                    sLD.inc(32 * NHC)
                    sMM.inc(2 * NHC)
                    sACT.inc(2 * NHC)
                    sDVE.inc(NHC)
                    for src_t, dst_t, wmat, svec in ((ha, hb, h1t_b, "h1s"),
                                                     (hb, ha, h2t_b, "h2s")):
                        ld0, mm0, act0 = sLD.n, sMM.n, sACT.n
                        with nc.Block() as blk:
                            @blk.sync
                            def _(sync, wmat=wmat, ld0=ld0, mm0=mm0):
                                for oc in range(NHC):
                                    if oc >= 2:
                                        sync.wait_ge(sMM.h, mm0 + oc - 1)
                                    sync.dma_start(wth[oc % 2][:], wmat.ap()[oc]
                                                   ).then_inc(sLD.h, 16)

                            @blk.tensor
                            def _(t, src_t=src_t, ld0=ld0, act0=act0):
                                for oc in range(NHC):
                                    t.wait_ge(sLD.h, ld0 + 16 * (oc + 1))
                                    if oc >= 2:
                                        t.wait_ge(sACT.h, act0 + oc - 1)
                                    po = ps[4 + oc % 2]
                                    for ic in range(NHC):
                                        ins = t.matmul(
                                            po[:, 0:QCW],
                                            wth[oc % 2][:, ic * 128:(ic + 1) * 128],
                                            src_t[ic][:], start=(ic == 0),
                                            stop=(ic == NHC - 1))
                                        if ic == NHC - 1:
                                            ins.then_inc(sMM.h, 1)

                            @blk.scalar
                            def _(s, dst_t=dst_t, svec=svec, mm0=mm0):
                                for oc in range(NHC):
                                    po = ps[4 + oc % 2]
                                    s.wait_ge(sMM.h, mm0 + oc + 1)
                                    s.activation(dst_t[oc][:], po[:, 0:QCW],
                                                 AF.Gelu,
                                                 scale=V[svec][:, oc:oc + 1]
                                                 ).then_inc(sACT.h, 1)
                        sLD.inc(16 * NHC)
                        sMM.inc(NHC)
                        sACT.inc(NHC)
                    ld0, mm0, act0, dv0 = sLD.n, sMM.n, sACT.n, sDVE.n
                    with nc.Block() as blk:
                        @blk.sync
                        def _(sync, ld0=ld0, mm0=mm0):
                            for ec in range(NEC):
                                if ec >= 2:
                                    sync.wait_ge(sMM.h, mm0 + ec - 1)
                                sync.dma_start(wth[ec % 2][:], w3t_b.ap()[ec]
                                               ).then_inc(sLD.h, 16)

                        @blk.tensor
                        def _(t, ld0=ld0, act0=act0):
                            for ec in range(NEC):
                                t.wait_ge(sLD.h, ld0 + 16 * (ec + 1))
                                if ec >= 2:
                                    t.wait_ge(sACT.h, act0 + ec - 1)
                                po = ps[6 + ec % 2]
                                for ic in range(NHC):
                                    ins = t.matmul(
                                        po[:, 0:QCW],
                                        wth[ec % 2][:, ic * 128:(ic + 1) * 128],
                                        ha[ic][:], start=(ic == 0),
                                        stop=(ic == NHC - 1))
                                    if ic == NHC - 1:
                                        ins.then_inc(sMM.h, 1)

                        @blk.scalar
                        def _(s, mm0=mm0, dv0=dv0):
                            for ec in range(NEC):
                                po = ps[6 + ec % 2]
                                s.wait_ge(sMM.h, mm0 + ec + 1)
                                if ec >= 2:
                                    s.wait_ge(sDVE.h, dv0 + ec - 1)
                                s.activation(w3_ev[ec % 2][:], po[:, 0:QCW],
                                             AF.Identity,
                                             scale=V["w3s"][:, ec:ec + 1]
                                             ).then_inc(sACT.h, 1)

                        @blk.vector
                        def _(v, qc=qc, act0=act0):
                            for ec in range(NEC):
                                v.wait_ge(sACT.h, act0 + ec + 1)
                                v.tensor_add(acc[ec][:, qc * QCW:(qc + 1) * QCW],
                                             acc[ec][:, qc * QCW:(qc + 1) * QCW],
                                             w3_ev[ec % 2][:]).then_inc(sDVE.h, 1)
                    sLD.inc(16 * NEC)
                    sMM.inc(NEC)
                    sACT.inc(NEC)
                    sDVE.inc(NEC)
            with nc.Block() as blk:
                @blk.gpsimd
                def _(g):
                    for ec in range(NEC):
                        g.dma_start(y.ap()[ec * 128:(ec + 1) * 128, :], acc[ec][:]
                                    ).then_inc(sST.h, 16)
                    sST.inc(16 * NEC)
                    g.wait_ge(sST.h, sST.n)

                @blk.sync
                def _(sync):
                    sync.wait_ge(sLD.h, sLD.n)
    return nc


def _pack_vec(v, k):
    return np.ascontiguousarray(np.asarray(v, np.float32).reshape(k, 128).T)


def _tile_T(w, noc):
    wT = np.asarray(w, np.float32).T
    nk = wT.shape[0] // 128
    A = wT.reshape(nk, 128, noc, 128).transpose(2, 1, 0, 3)
    return np.ascontiguousarray(A.reshape(noc, 128, nk * 128))


def _rowchunk_T(w):
    wT = np.ascontiguousarray(np.asarray(w, np.float32).T)
    return wT.reshape(wT.shape[0] // 128, 128, wT.shape[1])


def kernel(**inputs):
    xin = np.asarray(inputs["L0_attn_x"], np.float32)
    base = {
        "ident_in": np.eye(128, dtype=np.float32),
        "oneslo_in": np.concatenate([np.ones(64, np.float32),
                                     np.zeros(64, np.float32)]).reshape(128, 1),
        "oneshi_in": np.concatenate([np.zeros(64, np.float32),
                                     np.ones(64, np.float32)]).reshape(128, 1),
        "l0_wkT": np.ascontiguousarray(
            np.asarray(inputs["L0_attn_Wk"], np.float32).T),
        "l0_wvT": np.ascontiguousarray(
            np.asarray(inputs["L0_attn_Wv"], np.float32).T),
        "l0_wrT": np.ascontiguousarray(
            np.asarray(inputs["L0_attn_Wr"], np.float32).T),
        "l0_woT": np.ascontiguousarray(
            np.asarray(inputs["L0_attn_Wo"], np.float32).T),
        "f0_w": np.ascontiguousarray(np.asarray(inputs["L0_ffn_weight"],
                                                np.float32)),
        "wq1": _tile_T(inputs["L1_attn_wq"], 8),
        "wk1": _tile_T(inputs["L1_attn_wk"], 8),
        "wv1": _rowchunk_T(inputs["L1_attn_wv"]),
        "wo1": _tile_T(inputs["L1_attn_wo"], 8),
        "w1t": _tile_T(inputs["L1_ffn_w1_w"], NHC),
        "w2t": _tile_T(inputs["L1_ffn_w2_w"], NHC),
        "h1t": _tile_T(inputs["L1_ffn_h1_w"], NHC),
        "h2t": _tile_T(inputs["L1_ffn_h2_w"], NHC),
        "w3t": _tile_T(inputs["L1_ffn_w3_w"], NEC),
        "vec_u": _pack_vec(inputs["L0_attn_u"], 8),
        "vec_w": _pack_vec(inputs["L0_attn_w"], 8),
        "vec_tmk": _pack_vec(inputs["L0_attn_tmk"], 8),
        "vec_tmv": _pack_vec(inputs["L0_attn_tmv"], 8),
        "vec_tmr": _pack_vec(inputs["L0_attn_tmr"], 8),
        "vec_bo": _pack_vec(inputs["L0_attn_bo"], 8),
        "vec_b0f": _pack_vec(inputs["L0_ffn_bias"], 8),
        "vec_w1s": _pack_vec(inputs["L1_ffn_w1_s"], 32),
        "vec_w2s": _pack_vec(inputs["L1_ffn_w2_s"], 32),
        "vec_h1s": _pack_vec(inputs["L1_ffn_h1_s"], 32),
        "vec_h2s": _pack_vec(inputs["L1_ffn_h2_s"], 32),
        "vec_w3s": _pack_vec(inputs["L1_ffn_w3_s"], 8),
    }
    in_maps = []
    for c in range(8):
        b, qh = divmod(c, 2)
        m = dict(base)
        m["xT"] = np.ascontiguousarray(xin[b].T)
        m["qflag"] = np.array([[qh]], np.int32)
        in_maps.append(m)
    nc = build()
    res = run_bass_kernel_spmd(nc, in_maps, core_ids=list(range(8)))
    out = np.empty((4, T, D), np.float32)
    for c in range(8):
        b, qh = divmod(c, 2)
        out[b, qh * TQ:(qh + 1) * TQ, :] = res.results[c]["y"].T
    return out




# version 16
# speedup vs baseline: 1.0168x; 1.0168x over previous
from contextlib import ExitStack

import numpy as np

import concourse.bass as bass
import concourse.mybir as mybir
from concourse.bass_utils import run_bass_kernel_spmd

F32 = mybir.dt.float32
BF16 = mybir.dt.bfloat16
I32 = mybir.dt.int32
AF = mybir.ActivationFunctionType
ALU = mybir.AluOpType
AX = mybir.AxisListType

D = 1024
T = 4096
TQ = 2048
NEC = 8
TC = 256
NTC = T // TC
NH = 16
DHID = 4096
NHC = DHID // 128
QCW = 512
NQC = TQ // QCW
NKT = T // 128
KCW = 512
NKC = T // KCW
VROW = 1040


class SemC:
    def __init__(self, h):
        self.h = h
        self.n = 0

    def inc(self, k=1):
        self.n += k
        return self.n


def build():
    nc = bass.Bass("TRN2", target_bir_lowering=False)
    nc.detect_race_conditions = False

    xT = nc.dram_tensor("xT", [D, T], F32, kind="ExternalInput")
    qflag = nc.dram_tensor("qflag", [1, 1], I32, kind="ExternalInput")
    ident_in = nc.dram_tensor("ident_in", [128, 128], F32, kind="ExternalInput")
    oneslo_in = nc.dram_tensor("oneslo_in", [128, 1], F32, kind="ExternalInput")
    oneshi_in = nc.dram_tensor("oneshi_in", [128, 1], F32, kind="ExternalInput")
    l0_wkT = nc.dram_tensor("l0_wkT", [D, D], F32, kind="ExternalInput")
    l0_wvT = nc.dram_tensor("l0_wvT", [D, D], F32, kind="ExternalInput")
    l0_wrT = nc.dram_tensor("l0_wrT", [D, D], F32, kind="ExternalInput")
    l0_woT = nc.dram_tensor("l0_woT", [D, D], F32, kind="ExternalInput")
    f0_w = nc.dram_tensor("f0_w", [D, D], F32, kind="ExternalInput")
    wq1 = nc.dram_tensor("wq1", [8, 128, D], F32, kind="ExternalInput")
    wk1 = nc.dram_tensor("wk1", [8, 128, D], F32, kind="ExternalInput")
    wv1 = nc.dram_tensor("wv1", [8, 128, D], F32, kind="ExternalInput")
    wo1 = nc.dram_tensor("wo1", [8, 128, D], F32, kind="ExternalInput")
    w1t = nc.dram_tensor("w1t", [NHC, 128, D], F32, kind="ExternalInput")
    w2t = nc.dram_tensor("w2t", [NHC, 128, D], F32, kind="ExternalInput")
    h1t = nc.dram_tensor("h1t", [NHC, 128, DHID], F32, kind="ExternalInput")
    h2t = nc.dram_tensor("h2t", [NHC, 128, DHID], F32, kind="ExternalInput")
    w3t = nc.dram_tensor("w3t", [NEC, 128, DHID], F32, kind="ExternalInput")
    vin = {}
    for nm, k in (("u", 8), ("w", 8), ("tmk", 8), ("tmv", 8), ("tmr", 8),
                  ("bo", 8), ("b0f", 8), ("w1s", 32), ("w2s", 32), ("h1s", 32),
                  ("h2s", 32), ("w3s", 8)):
        vin[nm] = nc.dram_tensor("vec_" + nm, [128, k], F32, kind="ExternalInput")
    y = nc.dram_tensor("y", [D, TQ], F32, kind="ExternalOutput")

    x1_d = nc.dram_tensor("x1_d", [D, T], F32, kind="Internal")
    xn1_d = nc.dram_tensor("xn1_d", [D, T], BF16, kind="Internal")
    K_d = nc.dram_tensor("K_d", [D, T], BF16, kind="Internal")
    Q_d = nc.dram_tensor("Q_d", [D, TQ], BF16, kind="Internal")
    o_d = nc.dram_tensor("o_d", [D, TQ], BF16, kind="Internal")
    kn2_d = nc.dram_tensor("kn2_d", [NH, T], F32, kind="Internal")
    rows_d = nc.dram_tensor("rows_d", [8, T], F32, kind="Internal")
    wq1_b = nc.dram_tensor("wq1_b", [8, 128, D], BF16, kind="Internal")
    wk1_b = nc.dram_tensor("wk1_b", [8, 128, D], BF16, kind="Internal")
    wv1_b = nc.dram_tensor("wv1_b", [8, 128, D], BF16, kind="Internal")
    wo1_b = nc.dram_tensor("wo1_b", [8, 128, D], BF16, kind="Internal")
    w1t_b = nc.dram_tensor("w1t_b", [NHC, 128, D], BF16, kind="Internal")
    w2t_b = nc.dram_tensor("w2t_b", [NHC, 128, D], BF16, kind="Internal")
    h1t_b = nc.dram_tensor("h1t_b", [NHC, 128, DHID], BF16, kind="Internal")
    h2t_b = nc.dram_tensor("h2t_b", [NHC, 128, DHID], BF16, kind="Internal")
    w3t_b = nc.dram_tensor("w3t_b", [NEC, 128, DHID], BF16, kind="Internal")

    ctx = ExitStack()
    with ctx:
        mksem = lambda name: SemC(ctx.enter_context(nc.semaphore(name)))
        sLD = mksem("sLD")
        sLDo = mksem("sLDo")
        sWL = [mksem(f"sWL{i}") for i in range(6)]
        sWS = [mksem(f"sWS{i}") for i in range(6)]
        sCC = mksem("sCC")
        sLB1 = mksem("sLB1")
        sLB2 = mksem("sLB2")
        sST = mksem("sST")
        sSC = mksem("sSC")
        sSP = mksem("sSP")
        sSX = mksem("sSX")
        sSR = [mksem("sSRe"), mksem("sSRo")]
        sMM = mksem("sMM")
        sACT = mksem("sACT")
        sDVE = mksem("sDVE")

        def sb(st, name, shape, dt):
            return st.enter_context(nc.sbuf_tensor(name, shape, dt))

        ident = sb(ctx, "ident", [128, 128], F32)
        epsc = sb(ctx, "epsc", [1, 1], F32)
        ln127c = sb(ctx, "ln127c", [128, 1], F32)
        eps8c = sb(ctx, "eps8c", [128, 1], F32)
        neps8c = sb(ctx, "neps8c", [128, 1], F32)
        ones1 = sb(ctx, "ones1", [128, 1], F32)
        oneslo = sb(ctx, "oneslo", [128, 1], F32)
        oneshi = sb(ctx, "oneshi", [128, 1], F32)
        V = {nm: sb(ctx, "v_" + nm, [128, k], F32)
             for nm, k in (("u", 8), ("w", 8), ("negw", 8), ("umw", 8),
                           ("wmu", 8), ("m2w", 8), ("tmk", 8), ("tmv", 8),
                           ("tmr", 8), ("bo", 8), ("b0f", 8), ("w1s", 32),
                           ("w2s", 32), ("h1s", 32), ("h2s", 32), ("w3s", 8),
                           ("wscale0", 8), ("stA", 8), ("stB", 8), ("prevcol", 8))}
        ps = [ctx.enter_context(nc.psum_tensor(f"ps{i}", [128, 512], F32))
              for i in range(8)]
        flag_reg = ctx.enter_context(nc.sync.register("flag_reg"))

        with nc.Block() as blk:
            @blk.sync
            def _(sync):
                sync.dma_start(ident[:], ident_in.ap()).then_inc(sLD.h, 16)
                sync.dma_start(oneslo[:], oneslo_in.ap()).then_inc(sLD.h, 16)
                sync.dma_start(oneshi[:], oneshi_in.ap()).then_inc(sLD.h, 16)
                sLD.inc(48)
                for nm in ("u", "w", "tmk", "tmv", "tmr", "bo", "b0f", "w1s",
                           "w2s", "h1s", "h2s", "w3s"):
                    sync.dma_start(V[nm][:], vin[nm].ap()).then_inc(sLD.h, 16)
                    sLD.inc(16)
                sync.wait_ge(sLD.h, sLD.n)
                sync.reg_load(flag_reg, qflag.ap().to_broadcast((1, 1)))

            @blk.vector
            def _(v):
                v.memset(ones1[:], 1.0)
                v.memset(V["stA"][:], 0.0)
                v.memset(V["stB"][:], 0.0)
                v.memset(V["prevcol"][:], 0.0)
                v.memset(epsc[:], 1e-5)
                v.memset(ln127c[:], 4.844187086458591)
                v.memset(eps8c[:], 1e-8)
                v.memset(neps8c[:], -1e-8)
        with nc.Block() as blk:
            @blk.vector
            def _(v):
                v.tensor_scalar_mul(V["negw"][:], V["w"][:], -1.0)
                v.tensor_sub(V["umw"][:], V["u"][:], V["w"][:])
                v.tensor_sub(V["wmu"][:], V["w"][:], V["u"][:])
                v.tensor_scalar_mul(V["m2w"][:], V["w"][:], -2.0)

        with ExitStack() as pc:
            wk_sb = [sb(pc, f"wk_sb{i}", [128, D], BF16) for i in range(NEC)]
            wv_sb = [sb(pc, f"wv_sb{i}", [128, D], BF16) for i in range(NEC)]
            wr_sb = [sb(pc, f"wr_sb{i}", [128, D], BF16) for i in range(NEC)]
            wo_sb = [sb(pc, f"wo_sb{i}", [128, D], BF16) for i in range(NEC)]
            wq0_sb = [sb(pc, f"wq0_sb{i}", [128, D], BF16) for i in range(NEC)]
            wld = [sb(pc, f"wld{i}", [128, D], F32) for i in range(2)]

            for mat, dstl in ((l0_wkT, wk_sb), (l0_wvT, wv_sb), (l0_wrT, wr_sb),
                              (l0_woT, wo_sb)):
                ld0, dv0 = sLD.n, sDVE.n
                with nc.Block() as blk:
                    @blk.sync
                    def _(sync, mat=mat, ld0=ld0, dv0=dv0):
                        for dc in range(NEC):
                            if dc >= 2:
                                sync.wait_ge(sDVE.h, dv0 + dc - 1)
                            sync.dma_start(wld[dc % 2][:],
                                           mat.ap()[dc * 128:(dc + 1) * 128, :]
                                           ).then_inc(sLD.h, 16)

                    @blk.vector
                    def _(v, dstl=dstl, ld0=ld0):
                        for dc in range(NEC):
                            v.wait_ge(sLD.h, ld0 + 16 * min(dc + 2, NEC))
                            v.tensor_copy(dstl[dc][:], wld[dc % 2][:]
                                          ).then_inc(sDVE.h, 1)
                sLD.inc(16 * NEC)
                sDVE.inc(NEC)

            with ExitStack() as p2bs:
                wqf = sb(p2bs, "wqf", [128, D], F32)
                wamax = sb(p2bs, "wamax", [128, 1], F32)
                wr127 = sb(p2bs, "wr127", [128, 1], F32)
                rnd_i = sb(p2bs, "rnd_i", [128, D], I32)
                rnd_f = sb(p2bs, "rnd_f", [128, D], F32)
                rnd_d = sb(p2bs, "rnd_d", [128, D], F32)
                rnd_c = sb(p2bs, "rnd_c", [128, D], F32)
                for ec in range(NEC):
                    with nc.Block() as blk:
                        @blk.sync
                        def _(sync, ec=ec):
                            sync.dma_start(wld[0][:], f0_w.ap()[ec * 128:(ec + 1) * 128, :]
                                           ).then_inc(sLD.h, 16)
                            sLD.inc(16)
                            sync.wait_ge(sLD.h, sLD.n)
                    with nc.Block() as blk:
                        @blk.vector
                        def _(v, ec=ec):
                            v.reduce_max(wamax[:, 0:1], wld[0][:], axis=AX.X,
                                         apply_absolute_value=True
                                         ).then_inc(sDVE.h, 1)
                            sDVE.inc(1)
                            v.wait_ge(sDVE.h, sDVE.n)
                            v.tensor_scalar_max(wamax[:, 0:1], wamax[:, 0:1], 1e-8)
                            v.tensor_scalar_mul(V["wscale0"][:, ec:ec + 1],
                                                wamax[:, 0:1], 1.0 / 127.0)
                    with nc.Block() as blk:
                        @blk.vector
                        def _(v, ec=ec):
                            v.reciprocal(wr127[:, 0:1], V["wscale0"][:, ec:ec + 1])
                    with nc.Block() as blk:
                        @blk.vector
                        def _(v):
                            v.tensor_scalar_mul(wqf[:], wld[0][:], wr127[:, 0:1])
                            v.tensor_copy(rnd_i[:], wqf[:])
                            v.tensor_copy(rnd_f[:], rnd_i[:])
                            v.tensor_sub(rnd_d[:], wqf[:], rnd_f[:])
                            v.scalar_tensor_tensor(rnd_f[:], rnd_d[:], 0.5, rnd_f[:],
                                                   ALU.is_gt, ALU.add)
                            v.tensor_scalar(rnd_c[:], rnd_d[:], -0.5, None, ALU.is_lt)
                            v.tensor_sub(wqf[:], rnd_f[:], rnd_c[:])
                    mm0, act0 = sMM.n, sACT.n
                    with nc.Block() as blk:
                        @blk.tensor
                        def _(t, mm0=mm0, act0=act0):
                            for dc in range(NEC):
                                if dc >= 2:
                                    t.wait_ge(sACT.h, act0 + dc - 1)
                                t.transpose(ps[dc % 2][0:128, 0:128],
                                            wqf[:, dc * 128:(dc + 1) * 128], ident[:]
                                            ).then_inc(sMM.h, 1)

                        @blk.scalar
                        def _(s, ec=ec, mm0=mm0):
                            for dc in range(NEC):
                                s.wait_ge(sMM.h, mm0 + dc + 1)
                                s.copy(wq0_sb[dc][:, ec * 128:(ec + 1) * 128],
                                       ps[dc % 2][0:128, 0:128]).then_inc(sACT.h, 1)
                    sMM.inc(NEC)
                    sACT.inc(NEC)

            with ExitStack() as pa:
                wtmp = [sb(pa, f"wtmp{i}", [128, 2048], F32) for i in range(6)]
                wtmpb = [sb(pa, f"wtmpb{i}", [128, 2048], BF16) for i in range(6)]
                x_a = [[sb(pa, f"x_a{j}_{i}", [128, TC], F32) for i in range(NEC)]
                       for j in range(2)]
                x2a = [sb(pa, f"x2a{i}", [128, TC], F32) for i in range(2)]
                x2d = sb(pa, "x2d", [128, TC], F32)
                mrow = [sb(pa, f"mrow{i}", [1, TC], F32) for i in range(2)]
                vrow = [sb(pa, f"vrow{i}", [1, TC], F32) for i in range(2)]
                rrow = [sb(pa, f"rrow{i}", [1, TC], F32) for i in range(2)]
                nrow = [sb(pa, f"nrow{i}", [1, TC], F32) for i in range(2)]
                tiles = []
                for srcw, dstw, noc, kk in (
                        (wk1, wk1_b, 8, D), (wv1, wv1_b, 8, D),
                        (wq1, wq1_b, 8, D), (wo1, wo1_b, 8, D),
                        (w1t, w1t_b, NHC, D), (w2t, w2t_b, NHC, D),
                        (w3t, w3t_b, NEC, DHID),
                        (h1t, h1t_b, NHC, DHID), (h2t, h2t_b, NHC, DHID)):
                    for oc in range(noc):
                        for off in range(0, kk, 2048):
                            tiles.append((srcw, dstw, oc, off,
                                          min(2048, kk - off)))
                ncast = len(tiles)
                nbsem = [len([c for c in range(ncast) if c % 6 == b])
                         for b in range(6)]
                cbnd = [(i * ncast) // NTC for i in range(NTC + 1)]
                csplit = [list(range(cbnd[i], cbnd[i + 1]))
                          for i in range(NTC)]
                mm0, act0, dv0 = sMM.n, sACT.n, sDVE.n
                xsem = [sLD, sLDo]
                xb = [sLD.n, sLDo.n]
                xtot = [xb[tc % 2] + 128 * (tc // 2 + 1) for tc in range(NTC)]
                lqs = sWL
                sqs = sWS
                lq0 = [s.n for s in lqs]
                sq0 = [s.n for s in sqs]
                cc0 = sCC.n
                NAC = 10
                dvb = [dv0 + 2 * i for i in range(NTC + 1)]
                sr0 = [sSR[0].n, sSR[1].n]
                rrcnt = [0, 0]
                rrtot = [0] * (NTC + 1)
                for i in range(1, NTC + 1):
                    rrcnt[i % 2] += 32
                    rrtot[i] = sr0[i % 2] + rrcnt[i % 2]
                with nc.Block() as blk:
                    @blk.sync
                    def _(sync):
                        for tc in range(NTC):
                            if tc >= 2:
                                sync.wait_ge(sMM.h, mm0 + 8 * (tc - 1))
                            for ec in range(NEC):
                                sync.dma_start(
                                    x_a[tc % 2][ec][:],
                                    xT.ap()[ec * 128:(ec + 1) * 128,
                                            tc * TC:(tc + 1) * TC]
                                ).then_inc(xsem[tc % 2].h, 16)
                            for ci in csplit[tc]:
                                if ci % 2 != 0:
                                    continue
                                srcw, dstw, oc, off, w = tiles[ci]
                                if ci >= 6:
                                    sync.wait_ge(sCC.h, cc0 + ci - 5)
                                sync.dma_start(wtmp[ci % 6][:, 0:w],
                                               srcw.ap()[oc][:, off:off + w]
                                               ).then_inc(lqs[ci % 6].h, 16)

                    @blk.scalar
                    def _(s):
                        for tc in range(NTC):
                            for ec in range(NEC):
                                if ec == 0:
                                    s.wait_ge(xsem[tc % 2].h, xtot[tc])
                                if ec >= 2:
                                    s.wait_ge(sMM.h, mm0 + 8 * tc + ec - 1)
                                elif tc >= 1:
                                    s.wait_ge(sMM.h, mm0 + 8 * (tc - 1) + 7 + ec)
                                ins = s.activation(x2a[ec % 2][:],
                                                   x_a[tc % 2][ec][:], AF.Square)
                                if ec >= 1:
                                    ins.then_inc(sACT.h, 1)
                            s.activation(x2d[:], x_a[tc % 2][7][:], AF.Square
                                         ).then_inc(sACT.h, 1)
                            s.wait_ge(sDVE.h, dvb[tc] + 1)
                            if tc >= 2:
                                s.wait_ge(sSR[(tc - 1) % 2].h,
                                          rrtot[tc - 1])
                            s.activation(rrow[tc % 2][:], vrow[tc % 2][:], AF.Ln,
                                         bias=epsc[0:1, 0:1]).then_inc(sACT.h, 1)
                            s.activation(rrow[tc % 2][:], rrow[tc % 2][:], AF.Exp,
                                         scale=-0.5).then_inc(sACT.h, 1)
                            for ci in csplit[tc]:
                                if ci % 2 != 1:
                                    continue
                                srcw, dstw, oc, off, w = tiles[ci]
                                if ci >= 6:
                                    s.wait_ge(sCC.h, cc0 + ci - 5)
                                s.dma_start(wtmp[ci % 6][:, 0:w],
                                            srcw.ap()[oc][:, off:off + w]
                                            ).then_inc(lqs[ci % 6].h, 16)

                    @blk.tensor
                    def _(t):
                        for tc in range(NTC):
                            if tc >= 1:
                                t.wait_ge(sDVE.h, dvb[tc - 1] + 1)
                            t.wait_ge(xsem[tc % 2].h, xtot[tc])
                            for ec in range(NEC):
                                t.matmul(ps[6][0:1, 0:TC], ones1[:],
                                         x_a[tc % 2][ec][:],
                                         start=(ec == 0), stop=(ec == NEC - 1))
                                t.wait_ge(sACT.h, act0 + NAC * tc + ec + 1)
                                t.matmul(ps[7][0:1, 0:TC], ones1[:],
                                         x2a[ec % 2][:], start=(ec == 0),
                                         stop=(ec == NEC - 1)).then_inc(sMM.h, 1)

                    @blk.vector
                    def _(v):
                        for tc in range(NTC):
                            v.wait_ge(sMM.h, mm0 + 8 * (tc + 1))
                            if tc >= 2:
                                v.wait_ge(sSR[(tc - 1) % 2].h,
                                          rrtot[tc - 1])
                            v.tensor_scalar_mul(mrow[tc % 2][:], ps[6][0:1, 0:TC],
                                                1.0 / D)
                            v.tensor_mul(vrow[tc % 2][:], mrow[tc % 2][:],
                                         mrow[tc % 2][:])
                            v.scalar_tensor_tensor(vrow[tc % 2][:],
                                                   ps[7][0:1, 0:TC], 1.0 / D,
                                                   vrow[tc % 2][:], ALU.mult,
                                                   ALU.subtract).then_inc(sDVE.h, 1)
                            for ci in csplit[tc]:
                                srcw, dstw, oc, off, w = tiles[ci]
                                v.wait_ge(lqs[ci % 6].h,
                                          lq0[ci % 6] + 16 * (ci // 6 + 1))
                                if ci >= 6:
                                    v.wait_ge(sqs[ci % 6].h,
                                              sq0[ci % 6] + 16 * ((ci - 6) // 6 + 1))
                                v.tensor_copy(wtmpb[ci % 6][:, 0:w],
                                              wtmp[ci % 6][:, 0:w]
                                              ).then_inc(sCC.h, 1)
                            v.wait_ge(sACT.h, act0 + NAC * (tc + 1))
                            v.scalar_tensor_tensor(nrow[tc % 2][:],
                                                   mrow[tc % 2][:], -1.0,
                                                   rrow[tc % 2][:], ALU.mult,
                                                   ALU.mult).then_inc(sDVE.h, 1)

                    @blk.gpsimd
                    def _(g):
                        for tc in range(NTC):
                            if tc >= 1:
                                g.wait_ge(sACT.h, act0 + NAC * tc)
                                g.dma_start(
                                    rows_d.ap()[0:1, (tc - 1) * TC:tc * TC],
                                    rrow[(tc - 1) % 2][:]
                                ).then_inc(sSR[tc % 2].h, 16)
                                g.wait_ge(sDVE.h, dvb[tc])
                                g.dma_start(
                                    rows_d.ap()[1:2, (tc - 1) * TC:tc * TC],
                                    nrow[(tc - 1) % 2][:]
                                ).then_inc(sSR[tc % 2].h, 16)
                            for ci in csplit[tc]:
                                srcw, dstw, oc, off, w = tiles[ci]
                                g.wait_ge(sCC.h, cc0 + ci + 1)
                                g.dma_start(dstw.ap()[oc][:, off:off + w],
                                            wtmpb[ci % 6][:, 0:w]
                                            ).then_inc(sqs[ci % 6].h, 16)
                        g.wait_ge(sACT.h, act0 + NAC * NTC)
                        g.dma_start(
                            rows_d.ap()[0:1, (NTC - 1) * TC:NTC * TC],
                            rrow[(NTC - 1) % 2][:]).then_inc(sSR[0].h, 16)
                        g.wait_ge(sDVE.h, dvb[NTC])
                        g.dma_start(
                            rows_d.ap()[1:2, (NTC - 1) * TC:NTC * TC],
                            nrow[(NTC - 1) % 2][:]).then_inc(sSR[0].h, 16)
                sLD.n = xb[0] + 128 * 8
                sLDo.n = xb[1] + 128 * 8
                for b in range(6):
                    lqs[b].n = lq0[b] + 16 * nbsem[b]
                    sqs[b].n = sq0[b] + 16 * nbsem[b]
                sCC.n = cc0 + ncast
                sSR[0].n = sr0[0] + rrcnt[0]
                sSR[1].n = sr0[1] + rrcnt[1]
                sMM.inc(8 * NTC)
                sACT.inc(NAC * NTC)
                sDVE.n = dvb[NTC]
                with nc.Block() as blk:
                    @blk.sync
                    def _(sync):
                        sync.wait_ge(sLD.h, sLD.n)
                        sync.wait_ge(sLDo.h, sLDo.n)
                        for b in range(6):
                            sync.wait_ge(lqs[b].h, lqs[b].n)
                            sync.wait_ge(sqs[b].h, sqs[b].n)
                        sync.wait_ge(sSR[0].h, sSR[0].n)
                        sync.wait_ge(sSR[1].h, sSR[1].n)

            x_in = [[sb(pc, f"x_in{j}_{i}", [128, TC], F32) for i in range(NEC)]
                    for j in range(2)]
            xn = [sb(pc, f"xn{i}", [128, TC + 1], F32) for i in range(NEC)]
            dif = [sb(pc, f"dif{i}", [128, TC], F32) for i in range(NEC)]
            x1 = [sb(pc, f"x1_{i}", [128, TC], F32) for i in range(NEC)]
            kb = [sb(pc, f"kb{i}", [128, TC], BF16) for i in range(NEC)]
            vb = [sb(pc, f"vb{i}", [128, TC], BF16) for i in range(NEC)]
            rb = [sb(pc, f"rb{i}", [128, TC], BF16) for i in range(NEC)]
            wkvr = [sb(pc, f"wkvr{i}", [128, TC], BF16) for i in range(NEC)]
            xq = [sb(pc, f"xq{i}", [128, TC], BF16) for i in range(NEC)]
            x2t = [sb(pc, f"x2t{i}", [128, TC], F32) for i in range(2)]
            rm_t = [sb(pc, f"rm_t{i}", [128, TC], F32) for i in range(2)]
            att_t = [sb(pc, f"att_t{i}", [128, TC], F32) for i in range(2)]
            kf = sb(pc, "kf", [128, TC], F32)
            vf = [sb(pc, f"vf{i}", [128, TC], F32) for i in range(4)]
            rfp = [sb(pc, f"rfp{i}", [128, TC], F32) for i in range(4)]
            e1 = [sb(pc, f"e1_{i}", [128, TC], F32) for i in range(4)]
            e2 = [sb(pc, f"e2_{i}", [128, TC], F32) for i in range(4)]
            rcb = [sb(pc, f"rcb{i}", [128, TC], F32) for i in range(4)]
            sc_b = [sb(pc, f"sc_b{i}", [128, TC], F32) for i in range(4)]
            rr_t = sb(pc, "rr_t", [128, TC], F32)
            s1_t = sb(pc, "s1_t", [128, TC], F32)
            e2v = sb(pc, "e2v", [128, TC], F32)
            sc_a = sb(pc, "sc_a", [128, TC], F32)
            lnb = sb(pc, "lnb", [128, TC], F32)
            wkv = sb(pc, "wkv", [128, TC], F32)
            run_amax = sb(pc, "run_amax", [128, TC], F32)
            amax_tm = sb(pc, "amax_tm", [128, 2], F32)
            pack_tm = sb(pc, "pack_tm", [128, 4], F32)
            pack_row = sb(pc, "pack_row", [4, 128], F32)
            rnd_qi = sb(pc, "rnd_qi", [128, TC], I32)
            m_row = sb(pc, "m_row", [1, TC], F32)
            var_row = sb(pc, "var_row", [1, TC], F32)
            rstd_row = sb(pc, "rstd_row", [1, TC], F32)
            nmr_row = sb(pc, "nmr_row", [1, TC], F32)
            rstd_b = [sb(pc, f"rstd_b{i}", [128, TC], F32) for i in range(2)]
            nmr_b = [sb(pc, f"nmr_b{i}", [128, TC], F32) for i in range(2)]
            rstd1_b = sb(pc, "rstd1_b", [128, TC], F32)
            nmr1_b = sb(pc, "nmr1_b", [128, TC], F32)
            r127_b = sb(pc, "r127_b", [128, TC], F32)
            s_b = sb(pc, "s_b", [128, TC], F32)
            xn1_t = [sb(pc, f"xn1_t{i}", [128, TC], BF16) for i in range(2)]

            def ln_stats1(src_tiles, row_a, row_b, seg):
                mm0, act0 = sMM.n, sACT.n
                with nc.Block() as blk:
                    @blk.scalar
                    def _(s, mm0=mm0):
                        for ec in range(NEC):
                            if ec >= 2:
                                s.wait_ge(sMM.h, mm0 + 2 * (ec - 1))
                            s.activation(x2t[ec % 2][:], src_tiles[ec][:, 0:TC],
                                         AF.Square).then_inc(sACT.h, 1)

                    @blk.tensor
                    def _(t, mm0=mm0, act0=act0):
                        for ec in range(NEC):
                            t.matmul(ps[6][0:1, 0:TC], ones1[:],
                                     src_tiles[ec][:, 0:TC],
                                     start=(ec == 0), stop=(ec == NEC - 1))
                            t.wait_ge(sACT.h, act0 + ec + 1)
                            t.matmul(ps[7][0:1, 0:TC], ones1[:], x2t[ec % 2][:],
                                     start=(ec == 0), stop=(ec == NEC - 1)
                                     ).then_inc(sMM.h, 2)
                sACT.inc(NEC)
                sMM.inc(2 * NEC)
                with nc.Block() as blk:
                    @blk.vector
                    def _(v):
                        v.tensor_scalar_mul(m_row[:], ps[6][0:1, 0:TC], 1.0 / D)
                        v.tensor_mul(var_row[:], m_row[:], m_row[:])
                        v.scalar_tensor_tensor(var_row[:], ps[7][0:1, 0:TC],
                                               1.0 / D, var_row[:], ALU.mult,
                                               ALU.subtract).then_inc(sDVE.h, 1)
                    sDVE.inc(1)

                    @blk.scalar
                    def _(s):
                        s.wait_ge(sDVE.h, sDVE.n)
                        s.activation(rstd_row[:], var_row[:], AF.Ln,
                                     bias=epsc[0:1, 0:1])
                        s.activation(rstd_row[:], rstd_row[:], AF.Exp, scale=-0.5
                                     ).then_inc(sACT.h, 1)
                    sACT.inc(1)

                    @blk.vector
                    def _(v):
                        v.wait_ge(sACT.h, sACT.n)
                        v.scalar_tensor_tensor(nmr_row[:], m_row[:], -1.0,
                                               rstd_row[:], ALU.mult, ALU.mult)
                with nc.Block() as blk:
                    @blk.gpsimd
                    def _(g):
                        g.dma_start(rows_d.ap()[row_a:row_a + 1, seg:seg + TC],
                                    rstd_row[:]).then_inc(sSP.h, 16)
                        g.dma_start(rows_d.ap()[row_b:row_b + 1, seg:seg + TC],
                                    nmr_row[:]).then_inc(sSP.h, 16)
                    sSP.inc(32)

                    @blk.sync
                    def _(sync):
                        sync.wait_ge(sSP.h, sSP.n)
                        sync.dma_start(
                            rstd1_b[:], rows_d.ap()[row_a:row_a + 1, seg:seg + TC]
                            .to_broadcast((128, TC))).then_inc(sLB1.h, 16)
                        sync.dma_start(
                            nmr1_b[:], rows_d.ap()[row_b:row_b + 1, seg:seg + TC]
                            .to_broadcast((128, TC))).then_inc(sLB1.h, 16)
                        sLB1.inc(32)
                        sync.wait_ge(sLB1.h, sLB1.n)

            xcnt = [sLD.n, sLDo.n]
            xint = [0] * (NTC + 1)
            b2tot = [0] * (NTC + 1)
            with nc.Block() as blk:
                @blk.sync
                def _(sync):
                    for ec in range(NEC):
                        sync.dma_start(x_in[0][ec][:],
                                       xT.ap()[ec * 128:(ec + 1) * 128, 0:TC]
                                       ).then_inc(sLD.h, 16)
                        sLD.inc(16)
                    sync.dma_start(rstd_b[0][:], rows_d.ap()[0:1, 0:TC]
                                   .to_broadcast((128, TC))).then_inc(sLB2.h, 16)
                    sync.dma_start(nmr_b[0][:], rows_d.ap()[1:2, 0:TC]
                                   .to_broadcast((128, TC))).then_inc(sLB2.h, 16)
                    sLB2.inc(32)
                    sync.wait_ge(sLD.h, sLD.n)
                    sync.wait_ge(sLB2.h, sLB2.n)
            xcnt[0] += 128
            xint[0] = xcnt[0]
            b2tot[0] = sLB2.n

            VBl, ABl, TBl = [], [], []
            for tc in range(NTC):
                seg = tc * TC
                cp = tc % 2
                v0, a0, t0 = sDVE.n, sACT.n, sMM.n
                b1t = sLB1.n + 32
                spt = sSP.n + 32
                x1p = sST.n
                sxp = sSX.n
                VBl.append(v0); ABl.append(a0); TBl.append(t0)
                pv0 = VBl[tc - 1] if tc >= 1 else None
                pa0 = ABl[tc - 1] if tc >= 1 else None
                pt0 = TBl[tc - 1] if tc >= 1 else None
                Vdif = lambda e, b=v0: b + e + 1
                Vrm = lambda e, b=v0: b + 8 + e + 1
                Vkb = lambda e, b=v0: b + 16 + e + 1
                Vvb = lambda e, b=v0: b + 24 + e + 1
                Vb_ = lambda e, b=v0: b + 32 + 2 * e + 1
                Vwkvr = lambda e, b=v0: b + 32 + 2 * e + 2
                Vabsm = lambda e, b=v0: b + 48 + e
                Vpack = lambda b=v0: b + 56
                Vxq = lambda e, b=v0: b + 56 + e + 1
                Vx1a = lambda e, b=v0: b + 64 + e + 1
                Vfd = lambda e, b=v0: b + 72 + 2 * e + 1
                Vx1b = lambda e, b=v0: b + 72 + 2 * e + 2
                Asig = lambda e, b=a0: b + e + 1
                Akv = lambda e, b=a0: b + 8 + 4 * e + 1
                Ae = lambda e, b=a0: b + 8 + 4 * e + 2
                Ar = lambda e, b=a0: b + 8 + 4 * e + 3
                Arcb = lambda e, b=a0: b + 8 + 4 * e + 4
                Aabs = lambda e, b=a0: b + 41 + e
                Arec = lambda b=a0: b + 49
                Apack = lambda b=a0: b + 50
                Awo = lambda e, b=a0: b + 50 + e + 1
                Afb = lambda e, b=a0: b + 58 + e + 1
                Tk = lambda e, b=t0: b + 3 * e + 1
                Tv = lambda e, b=t0: b + 3 * e + 2
                Tr = lambda e, b=t0: b + 3 * e + 3
                Ttr = lambda g2, b=t0: b + 24 + g2 + 1
                Tpk = lambda b=t0: b + 27
                Two = lambda e, b=t0: b + 27 + e + 1
                Tf = lambda e, b=t0: b + 35 + e + 1

                with nc.Block() as blk:
                    @blk.vector
                    def _(v, tc=tc, seg=seg, cp=cp, v0=v0, a0=a0, t0=t0,
                          b1t=b1t, x1p=x1p,
                          pv0=pv0, pa0=pa0, pt0=pt0, Vdif=Vdif, Asig=Asig,
                          Aabs=Aabs, Ae=Ae, Arcb=Arcb, Ar=Ar, Ttr=Ttr, Awo=Awo,
                          Tf=Tf, Afb=Afb):
                        for ec in range(NEC):
                            if tc >= 1:
                                if ec == 0:
                                    v.wait_ge(xsem[tc % 2].h, xint[tc])
                                    v.wait_ge(sLB2.h, b2tot[tc])
                                v.wait_ge(sACT.h, pa0 + 41 + ec)
                            v.tensor_copy(xn[ec][:, 0:1], V["prevcol"][:, ec:ec + 1])
                            v.tensor_mul(xn[ec][:, 1:TC + 1], x_in[cp][ec][:],
                                         rstd_b[cp][:])
                            v.tensor_add(xn[ec][:, 1:TC + 1], xn[ec][:, 1:TC + 1],
                                         nmr_b[cp][:])
                            v.tensor_sub(dif[ec][:], xn[ec][:, 1:TC + 1],
                                         xn[ec][:, 0:TC]).then_inc(sDVE.h, 1)
                        for ec in range(NEC):
                            if ec >= 2:
                                v.wait_ge(sACT.h, Asig(ec - 2))
                            elif tc >= 1:
                                v.wait_ge(sACT.h, pa0 + 58 + 7 + ec)
                            v.scalar_tensor_tensor(rm_t[ec % 2][:], dif[ec][:],
                                                   V["tmr"][:, ec:ec + 1],
                                                   xn[ec][:, 0:TC], ALU.mult,
                                                   ALU.add).then_inc(sDVE.h, 1)
                        if tc >= 1:
                            v.wait_ge(sMM.h, pt0 + 24)
                        for ec in range(NEC):
                            v.scalar_tensor_tensor(kb[ec][:], dif[ec][:],
                                                   V["tmk"][:, ec:ec + 1],
                                                   xn[ec][:, 0:TC], ALU.mult,
                                                   ALU.add).then_inc(sDVE.h, 1)
                        for ec in range(NEC):
                            v.scalar_tensor_tensor(vb[ec][:], dif[ec][:],
                                                   V["tmv"][:, ec:ec + 1],
                                                   xn[ec][:, 0:TC], ALU.mult,
                                                   ALU.add).then_inc(sDVE.h, 1)
                        for ec in range(NEC):
                            v.tensor_copy(V["prevcol"][:, ec:ec + 1],
                                          xn[ec][:, TC:TC + 1])
                        for ec in range(NEC):
                            v.wait_ge(sACT.h, Ae(ec))
                            if ec >= 4:
                                v.wait_ge(sACT.h, Arcb(ec - 4))
                            elif tc >= 1:
                                v.wait_ge(sACT.h, pa0 + 8 + 4 * (4 + ec) + 4)
                            v.tensor_mul(e2v[:], e2[ec % 4][:], vf[ec % 4][:])
                            v.tensor_tensor_scan(sc_a[:], e1[ec % 4][:], e2v[:],
                                                 V["stA"][:, ec:ec + 1],
                                                 ALU.mult, ALU.add)
                            v.tensor_tensor_scan(sc_b[ec % 4][:], e1[ec % 4][:],
                                                 e2[ec % 4][:],
                                                 V["stB"][:, ec:ec + 1],
                                                 ALU.mult, ALU.add
                                                 ).then_inc(sDVE.h, 1)
                            v.tensor_copy(V["stA"][:, ec:ec + 1],
                                          sc_a[:, TC - 1:TC])
                            v.tensor_copy(V["stB"][:, ec:ec + 1],
                                          sc_b[ec % 4][:, TC - 1:TC])
                            v.wait_ge(sACT.h, Arcb(ec))
                            v.tensor_mul(wkv[:], sc_a[:], rcb[ec % 4][:])
                            v.wait_ge(sACT.h, Ar(ec))
                            if tc >= 1 and ec == 0:
                                v.wait_ge(sMM.h, pt0 + 35)
                            v.tensor_mul(wkvr[ec][:], wkv[:], rfp[ec % 4][:]
                                         ).then_inc(sDVE.h, 1)
                        for ec in range(1, NEC):
                            v.wait_ge(sACT.h, Aabs(ec))
                            v.tensor_max(run_amax[:], run_amax[:], x2t[ec % 2][:]
                                         ).then_inc(sDVE.h, 1)
                        for g2 in range(2):
                            v.wait_ge(sMM.h, Ttr(g2))
                            ins = v.reduce_max(amax_tm[:, g2:g2 + 1],
                                               ps[6 + g2][0:128, 0:128],
                                               axis=AX.X)
                            if g2 == 1:
                                ins.then_inc(sDVE.h, 1)
                        for ec in range(NEC):
                            if ec == 0:
                                v.wait_ge(sLB1.h, b1t)
                                if tc >= 1:
                                    v.wait_ge(sMM.h, pt0 + 43)
                            v.tensor_mul(dif[0][:], xn[ec][:, 1:TC + 1], r127_b[:])
                            v.tensor_copy(rnd_qi[:], dif[0][:])
                            v.tensor_copy(dif[1][:], rnd_qi[:])
                            v.tensor_sub(dif[2][:], dif[0][:], dif[1][:])
                            v.scalar_tensor_tensor(dif[1][:], dif[2][:], 0.5,
                                                   dif[1][:], ALU.is_gt, ALU.add)
                            v.tensor_scalar(dif[3][:], dif[2][:], -0.5, None,
                                            ALU.is_lt)
                            v.tensor_sub(xq[ec][:], dif[1][:], dif[3][:]
                                         ).then_inc(sDVE.h, 1)
                        for ec in range(NEC):
                            if ec == 0 and tc >= 1:
                                v.wait_ge(sST.h, x1p)
                            v.wait_ge(sACT.h, Awo(ec))
                            v.tensor_add(x1[ec][:], x_in[cp][ec][:],
                                         att_t[ec % 2][:]).then_inc(sDVE.h, 1)
                        for ec in range(NEC):
                            v.wait_ge(sMM.h, Tf(ec))
                            v.scalar_tensor_tensor(
                                rm_t[ec % 2][:],
                                ps[2 if ec % 2 == 0 else 5][:, 0:TC],
                                V["wscale0"][:, ec:ec + 1], s_b[:], ALU.mult,
                                ALU.mult).then_inc(sDVE.h, 1)
                            v.wait_ge(sACT.h, Afb(ec))
                            v.tensor_add(x1[ec][:], x1[ec][:], att_t[ec % 2][:]
                                         ).then_inc(sDVE.h, 1)

                    @blk.scalar
                    def _(s, tc=tc, cp=cp, v0=v0, a0=a0, t0=t0, pv0=pv0, pa0=pa0,
                          Tk=Tk, Tv=Tv, Tr=Tr, Vb_=Vb_, Vrm=Vrm, Vwkvr=Vwkvr,
                          Vdif=Vdif, Vabsm=Vabsm, Tpk=Tpk, Two=Two, Vx1a=Vx1a,
                          Vfd=Vfd, Vx1b=Vx1b, Vpack=Vpack):
                        for ec in range(NEC):
                            s.wait_ge(sDVE.h, Vrm(ec))
                            s.activation(rb[ec][:], rm_t[ec % 2][:], AF.Sigmoid
                                         ).then_inc(sACT.h, 1)
                        for ec in range(NEC):
                            p = ec % 2
                            s.wait_ge(sMM.h, Tk(ec))
                            s.copy(kf[:], ps[0 if p == 0 else 3][:, 0:TC])
                            s.wait_ge(sMM.h, Tv(ec))
                            if ec >= 4:
                                s.wait_ge(sDVE.h, Vb_(ec - 4))
                            elif tc >= 1:
                                s.wait_ge(sDVE.h, pv0 + 32 + 2 * (4 + ec) + 1)
                            s.copy(vf[ec % 4][:], ps[1 if p == 0 else 4][:, 0:TC]
                                   ).then_inc(sACT.h, 1)
                            s.activation(rr_t[:], kf[:], AF.Relu,
                                         bias=V["umw"][:, ec:ec + 1])
                            s.activation(e1[ec % 4][:], rr_t[:], AF.Exp,
                                         scale=-1.0, bias=V["m2w"][:, ec:ec + 1])
                            s.activation(s1_t[:], kf[:], AF.Relu, scale=-1.0,
                                         bias=V["wmu"][:, ec:ec + 1])
                            s.activation(e2[ec % 4][:], s1_t[:], AF.Exp,
                                         scale=-1.0).then_inc(sACT.h, 1)
                            s.wait_ge(sMM.h, Tr(ec))
                            if ec >= 4:
                                s.wait_ge(sDVE.h, Vwkvr(ec - 4))
                            elif tc >= 1:
                                s.wait_ge(sDVE.h, pv0 + 32 + 2 * (4 + ec) + 2)
                            s.copy(rfp[ec % 4][:], ps[2 if p == 0 else 5][:, 0:TC]
                                   ).then_inc(sACT.h, 1)
                            s.wait_ge(sDVE.h, Vb_(ec))
                            s.activation(lnb[:], sc_b[ec % 4][:], AF.Ln)
                            s.activation(rcb[ec % 4][:], lnb[:], AF.Exp,
                                         scale=-1.0).then_inc(sACT.h, 1)
                        s.wait_ge(sDVE.h, Vdif(0))
                        s.activation(run_amax[:], xn[0][:, 1:TC + 1], AF.Abs
                                     ).then_inc(sACT.h, 1)
                        for ec in range(1, NEC):
                            s.wait_ge(sDVE.h, Vdif(ec))
                            if ec >= 3:
                                s.wait_ge(sDVE.h, Vabsm(ec - 2))
                            s.activation(x2t[ec % 2][:], xn[ec][:, 1:TC + 1],
                                         AF.Abs).then_inc(sACT.h, 1)
                        s.wait_ge(sDVE.h, Vpack())
                        s.activation(rr_t[:, 0:2], amax_tm[:], AF.Relu,
                                     bias=neps8c[:, 0:1])
                        s.activation(s1_t[:, 0:2], rr_t[:, 0:2], AF.Identity,
                                     bias=eps8c[:, 0:1])
                        s.activation(pack_tm[:, 2:4], s1_t[:, 0:2], AF.Identity,
                                     scale=1.0 / 127.0)
                        s.activation(rr_t[:, 0:2], s1_t[:, 0:2], AF.Ln)
                        s.activation(pack_tm[:, 0:2], rr_t[:, 0:2], AF.Exp,
                                     scale=-1.0, bias=ln127c[:, 0:1]
                                     ).then_inc(sACT.h, 1)
                        s.wait_ge(sMM.h, Tpk())
                        s.copy(pack_row[:], ps[6][0:4, 0:128]).then_inc(sACT.h, 1)
                        for ec in range(NEC):
                            s.wait_ge(sMM.h, Two(ec))
                            if ec >= 2:
                                s.wait_ge(sDVE.h, Vx1a(ec - 2))
                            elif tc >= 1:
                                s.wait_ge(sDVE.h, pv0 + 72 + 2 * (6 + ec) + 2)
                            s.activation(att_t[ec % 2][:],
                                         ps[6 if ec % 2 == 0 else 7][:, 0:TC],
                                         AF.Identity, bias=V["bo"][:, ec:ec + 1]
                                         ).then_inc(sACT.h, 1)
                        for ec in range(NEC):
                            s.wait_ge(sDVE.h, Vfd(ec))
                            if ec >= 2:
                                s.wait_ge(sDVE.h, Vx1b(ec - 2))
                            s.activation(att_t[ec % 2][:], rm_t[ec % 2][:],
                                         AF.Identity, bias=V["b0f"][:, ec:ec + 1]
                                         ).then_inc(sACT.h, 1)

                    @blk.tensor
                    def _(t, tc=tc, v0=v0, a0=a0, t0=t0, pv0=pv0, pa0=pa0,
                          Vkb=Vkb, Vvb=Vvb, Akv=Akv, Asig=Asig, Ar=Ar,
                          Vabsm=Vabsm, Vpack=Vpack, Vwkvr=Vwkvr, Apack=Apack,
                          Arec=Arec,
                          Awo=Awo, Vxq=Vxq, Vfd=Vfd):
                        for ec in range(NEC):
                            p = ec % 2
                            if ec == 0:
                                t.wait_ge(sDVE.h, Vkb(7))
                            if ec >= 2:
                                t.wait_ge(sACT.h, Akv(ec - 2))
                            for dc in range(NEC):
                                ins = t.matmul(
                                    ps[0 if p == 0 else 3][:, 0:TC],
                                    wk_sb[dc][:, ec * 128:(ec + 1) * 128],
                                    kb[dc][:], start=(dc == 0), stop=(dc == 7))
                                if dc == 7:
                                    ins.then_inc(sMM.h, 1)
                            if ec == 0:
                                t.wait_ge(sDVE.h, Vvb(7))
                            for dc in range(NEC):
                                ins = t.matmul(
                                    ps[1 if p == 0 else 4][:, 0:TC],
                                    wv_sb[dc][:, ec * 128:(ec + 1) * 128],
                                    vb[dc][:], start=(dc == 0), stop=(dc == 7))
                                if dc == 7:
                                    ins.then_inc(sMM.h, 1)
                            if ec == 0:
                                t.wait_ge(sACT.h, Asig(7))
                            if ec >= 2:
                                t.wait_ge(sACT.h, Ar(ec - 2))
                            elif tc >= 1:
                                t.wait_ge(sDVE.h, pv0 + 72 + 2 * (6 + ec) + 1)
                            for dc in range(NEC):
                                ins = t.matmul(
                                    ps[2 if p == 0 else 5][:, 0:TC],
                                    wr_sb[dc][:, ec * 128:(ec + 1) * 128],
                                    rb[dc][:], start=(dc == 0), stop=(dc == 7))
                                if dc == 7:
                                    ins.then_inc(sMM.h, 1)
                        t.wait_ge(sDVE.h, Vabsm(7))
                        for g2 in range(2):
                            t.transpose(ps[6 + g2][0:128, 0:128],
                                        run_amax[:, g2 * 128:(g2 + 1) * 128],
                                        ident[:]).then_inc(sMM.h, 1)
                        t.wait_ge(sACT.h, Arec())
                        t.transpose(ps[6][0:4, 0:128], pack_tm[:], ident[:]
                                    ).then_inc(sMM.h, 1)
                        for ec in range(NEC):
                            if ec == 0:
                                t.wait_ge(sDVE.h, Vwkvr(7))
                                t.wait_ge(sACT.h, Apack())
                            if ec >= 2:
                                t.wait_ge(sACT.h, Awo(ec - 2))
                            for dc in range(NEC):
                                ins = t.matmul(
                                    ps[6 if ec % 2 == 0 else 7][:, 0:TC],
                                    wo_sb[dc][:, ec * 128:(ec + 1) * 128],
                                    wkvr[dc][:], start=(dc == 0), stop=(dc == 7))
                                if dc == 7:
                                    ins.then_inc(sMM.h, 1)
                        for ec in range(NEC):
                            if ec == 0:
                                t.wait_ge(sDVE.h, Vxq(7))
                            if ec >= 2:
                                t.wait_ge(sDVE.h, Vfd(ec - 2))
                            else:
                                t.wait_ge(sACT.h, Ar(6 + ec))
                            for dc in range(NEC):
                                ins = t.matmul(
                                    ps[2 if ec % 2 == 0 else 5][:, 0:TC],
                                    wq0_sb[dc][:, ec * 128:(ec + 1) * 128],
                                    xq[dc][:], start=(dc == 0), stop=(dc == 7))
                                if dc == 7:
                                    ins.then_inc(sMM.h, 1)

                    @blk.sync
                    def _(sync, tc=tc, seg=seg, spt=spt, pv0=pv0):
                        sync.wait_ge(sSP.h, spt)
                        if tc >= 1:
                            sync.wait_ge(sDVE.h, pv0 + 72 + 15)
                        sync.dma_start(r127_b[:], rows_d.ap()[2:3, seg:seg + TC]
                                       .to_broadcast((128, TC))).then_inc(sLB1.h, 16)
                        sync.dma_start(s_b[:], rows_d.ap()[3:4, seg:seg + TC]
                                       .to_broadcast((128, TC))).then_inc(sLB1.h, 16)
                        if tc < NTC - 1:
                            np_ = (tc + 1) % 2
                            if tc >= 1:
                                sync.wait_ge(sDVE.h, pv0 + 64 + 8)
                            for ec in range(NEC):
                                sync.dma_start(
                                    x_in[np_][ec][:],
                                    xT.ap()[ec * 128:(ec + 1) * 128,
                                            seg + TC:seg + 2 * TC]
                                ).then_inc(xsem[np_].h, 16)
                            sync.dma_start(
                                rstd_b[np_][:],
                                rows_d.ap()[0:1, seg + TC:seg + 2 * TC]
                                .to_broadcast((128, TC))).then_inc(sLB2.h, 16)
                            sync.dma_start(
                                nmr_b[np_][:],
                                rows_d.ap()[1:2, seg + TC:seg + 2 * TC]
                                .to_broadcast((128, TC))).then_inc(sLB2.h, 16)

                    @blk.gpsimd
                    def _(g, seg=seg, a0=a0, v0=v0, Apack=Apack, Vx1b=Vx1b):
                        g.wait_ge(sACT.h, Apack())
                        g.dma_start(rows_d.ap()[2:3, seg:seg + TC],
                                    pack_row[0:2, :]).then_inc(sSP.h, 16)
                        g.dma_start(rows_d.ap()[3:4, seg:seg + TC],
                                    pack_row[2:4, :]).then_inc(sSP.h, 16)
                        for ec in range(NEC):
                            g.wait_ge(sDVE.h, Vx1b(ec))
                            g.dma_start(x1_d.ap()[ec * 128:(ec + 1) * 128,
                                        seg:seg + TC], x1[ec][:]
                                        ).then_inc(sST.h, 16)
                sDVE.inc(88)
                sACT.inc(66)
                sMM.inc(43)
                sLB1.inc(32)
                sSP.inc(32)
                sST.inc(128)
                if tc < NTC - 1:
                    xsem[(tc + 1) % 2].inc(128)
                    xcnt[(tc + 1) % 2] += 128
                    xint[tc + 1] = xcnt[(tc + 1) % 2]
                    sLB2.inc(32)
                    b2tot[tc + 1] = sLB2.n
                ln_stats1(x1, 4, 5, seg)
                dv0b = sDVE.n
                with nc.Block() as blk:
                    @blk.vector
                    def _(v, dv0b=dv0b, sxp=sxp):
                        for ec in range(NEC):
                            if ec >= 2:
                                v.wait_ge(sSX.h, sxp + 16 * ec)
                            else:
                                v.wait_ge(sSX.h, sxp)
                            v.tensor_mul(dif[0][:], x1[ec][:], rstd1_b[:])
                            v.tensor_add(xn1_t[ec % 2][:], dif[0][:], nmr1_b[:]
                                         ).then_inc(sDVE.h, 1)

                    @blk.gpsimd
                    def _(g, seg=seg, dv0b=dv0b):
                        for ec in range(NEC):
                            g.wait_ge(sDVE.h, dv0b + ec + 1)
                            g.dma_start(xn1_d.ap()[ec * 128:(ec + 1) * 128,
                                        seg:seg + TC], xn1_t[ec % 2][:]
                                        ).then_inc(sSX.h, 16)
                sDVE.inc(NEC)
                sSX.inc(16 * NEC)
            with nc.Block() as blk:
                @blk.sync
                def _(sync):
                    sync.wait_ge(sLD.h, sLD.n)
                    sync.wait_ge(sLDo.h, sLDo.n)
                    sync.wait_ge(sLB1.h, sLB1.n)
                    sync.wait_ge(sLB2.h, sLB2.n)
                    sync.wait_ge(sST.h, sST.n)
                    sync.wait_ge(sSP.h, sSP.n)
                    sync.wait_ge(sSX.h, sSX.n)

        with ExitStack() as pc:
            V_sb = sb(pc, "V_sb", [128, NKT * VROW], BF16)
            cbias = sb(pc, "cbias", [128, 64], F32)
            with nc.Block() as blk:
                @blk.vector
                def _(v):
                    v.memset(V_sb[:], 1.0)

            with ExitStack() as p4:
                kw_sb = [sb(p4, f"kw1sb{i}", [128, D], BF16) for i in range(NEC)]
                vw_sb = [sb(p4, f"vw1sb{i}", [128, D], BF16) for i in range(NEC)]
                xn1c = [sb(p4, f"xn1c{i}", [128, KCW], BF16) for i in range(NEC)]
                ktile = [sb(p4, f"ktile{i}", [128, KCW], BF16) for i in range(2)]
                k2t = [sb(p4, f"k2t{i}", [128, KCW], F32) for i in range(2)]
                knrow = [sb(p4, f"knrow{i}", [1, KCW], F32) for i in range(4)]
                with nc.Block() as blk:
                    @blk.sync
                    def _(sync):
                        for ec in range(NEC):
                            sync.dma_start(kw_sb[ec][:], wk1_b.ap()[ec]
                                           ).then_inc(sLD.h, 16)
                            sync.dma_start(vw_sb[ec][:], wv1_b.ap()[ec]
                                           ).then_inc(sLD.h, 16)
                            sLD.inc(32)
                        sync.wait_ge(sLD.h, sLD.n)
                for kc in range(NKC):
                    seg = kc * KCW
                    with nc.Block() as blk:
                        @blk.sync
                        def _(sync, seg=seg):
                            for ec in range(NEC):
                                sync.dma_start(xn1c[ec][:],
                                               xn1_d.ap()[ec * 128:(ec + 1) * 128,
                                               seg:seg + KCW]).then_inc(sLD.h, 16)
                                sLD.inc(16)
                            sync.wait_ge(sLD.h, sLD.n)
                    mm0, act0, st0 = sMM.n, sACT.n, sST.n
                    with nc.Block() as blk:
                        @blk.tensor
                        def _(t, mm0=mm0, act0=act0):
                            for ec in range(NEC):
                                if ec >= 2:
                                    t.wait_ge(sACT.h, act0 + 4 * (ec - 2) + 2)
                                pk = ps[ec % 2]
                                for dc in range(NEC):
                                    ins = t.matmul(
                                        pk[:, 0:KCW],
                                        kw_sb[ec][:, dc * 128:(dc + 1) * 128],
                                        xn1c[dc][:], start=(dc == 0), stop=(dc == 7))
                                    if dc == 7:
                                        ins.then_inc(sMM.h, 1)
                                t.wait_ge(sACT.h, act0 + 4 * ec + 2)
                                if ec >= 1:
                                    t.wait_ge(sACT.h, act0 + 4 * (ec - 1) + 4)
                                t.matmul(ps[2][0:1, 0:KCW], oneslo[:],
                                         k2t[ec % 2][:], start=True, stop=True)
                                t.matmul(ps[3][0:1, 0:KCW], oneshi[:],
                                         k2t[ec % 2][:], start=True, stop=True
                                         ).then_inc(sMM.h, 1)

                        @blk.scalar
                        def _(s, mm0=mm0, act0=act0, st0=st0):
                            for ec in range(NEC):
                                pk = ps[ec % 2]
                                s.wait_ge(sMM.h, mm0 + 2 * ec + 1)
                                if ec >= 2:
                                    s.wait_ge(sST.h, st0 + 48 * ec)
                                s.copy(ktile[ec % 2][:], pk[:, 0:KCW])
                                s.activation(k2t[ec % 2][:], pk[:, 0:KCW],
                                             AF.Square).then_inc(sACT.h, 2)
                                s.wait_ge(sMM.h, mm0 + 2 * ec + 2)
                                s.copy(knrow[(2 * ec) % 4][:], ps[2][0:1, 0:KCW])
                                s.copy(knrow[(2 * ec + 1) % 4][:],
                                       ps[3][0:1, 0:KCW]).then_inc(sACT.h, 2)

                        @blk.gpsimd
                        def _(g, seg=seg, act0=act0):
                            for ec in range(NEC):
                                g.wait_ge(sACT.h, act0 + 4 * ec + 2)
                                g.dma_start(K_d.ap()[ec * 128:(ec + 1) * 128,
                                            seg:seg + KCW],
                                            ktile[ec % 2][:]).then_inc(sST.h, 16)
                                g.wait_ge(sACT.h, act0 + 4 * ec + 4)
                                g.dma_start(kn2_d.ap()[2 * ec:2 * ec + 1,
                                            seg:seg + KCW],
                                            knrow[(2 * ec) % 4][:]
                                            ).then_inc(sST.h, 16)
                                g.dma_start(kn2_d.ap()[2 * ec + 1:2 * ec + 2,
                                            seg:seg + KCW],
                                            knrow[(2 * ec + 1) % 4][:]
                                            ).then_inc(sST.h, 16)
                    sMM.inc(2 * NEC)
                    sACT.inc(4 * NEC)
                    sST.inc(48 * NEC)
                    mm0, act0 = sMM.n, sACT.n
                    with nc.Block() as blk:
                        @blk.tensor
                        def _(t, mm0=mm0, act0=act0):
                            for idx in range(2 * (KCW // 128)):
                                g2, nh2 = divmod(idx, 2)
                                if idx >= 2:
                                    t.wait_ge(sACT.h, act0 + idx - 1)
                                pv = ps[4 + idx % 2]
                                for dc in range(NEC):
                                    ins = t.matmul(
                                        pv[:, 0:512],
                                        xn1c[dc][:, g2 * 128:(g2 + 1) * 128],
                                        vw_sb[dc][:, nh2 * 512:(nh2 + 1) * 512],
                                        start=(dc == 0), stop=(dc == 7))
                                    if dc == 7:
                                        ins.then_inc(sMM.h, 1)

                        @blk.scalar
                        def _(s, kc=kc, mm0=mm0):
                            for idx in range(2 * (KCW // 128)):
                                g2, nh2 = divmod(idx, 2)
                                gg = kc * (KCW // 128) + g2
                                base = gg * VROW
                                pv = ps[4 + idx % 2]
                                s.wait_ge(sMM.h, mm0 + idx + 1)
                                dst = V_sb[:, base:base + VROW].rearrange(
                                    "p (h c) -> p h c", h=16)[:, 8 * nh2:8 * nh2 + 8,
                                                              0:64]
                                src = pv[:, 0:512].rearrange("p (h c) -> p h c", h=8)
                                s.copy(dst, src).then_inc(sACT.h, 1)
                    sMM.inc(2 * (KCW // 128))
                    sACT.inc(2 * (KCW // 128))
                with nc.Block() as blk:
                    @blk.sync
                    def _(sync):
                        sync.wait_ge(sST.h, sST.n)

            with ExitStack() as p5:
                qw_sb = [sb(p5, f"qw1sb{i}", [128, D], BF16) for i in range(NEC)]
                xn1q = [sb(p5, f"xn1q{i}", [128, TQ], BF16) for i in range(NEC)]
                qtile = [sb(p5, f"qtile{i}", [128, QCW], BF16) for i in range(2)]
                knrow_l = sb(p5, "knrow_l", [1, T], F32)
                kmax_row = sb(p5, "kmax_row", [1, NH], F32)
                kmaxd8 = sb(p5, "kmaxd8", [1, NH], F32)
                qh_l = [sb(p5, f"qh_l{i}", [64, TQ], BF16) for i in range(2)]
                qh_sq = [sb(p5, f"qh_sq{i}", [64, TQ], F32) for i in range(2)]
                qn_row = sb(p5, "qn_row", [1, TQ], F32)
                qm4 = sb(p5, "qm4", [1, NQC], F32)
                cneg = sb(p5, "cneg", [1, 64], F32)
                with nc.Block() as blk:
                    @blk.sync
                    def _(sync):
                        for ec in range(NEC):
                            sync.dma_start(qw_sb[ec][:], wq1_b.ap()[ec]
                                           ).then_inc(sLD.h, 16)
                            sLD.inc(16)
                        with sync.If_eq(flag_reg, 0):
                            for ec in range(NEC):
                                sync.dma_start(xn1q[ec][:],
                                               xn1_d.ap()[ec * 128:(ec + 1) * 128,
                                               0:TQ]).then_inc(sLD.h, 16)
                        with sync.Else():
                            for ec in range(NEC):
                                sync.dma_start(xn1q[ec][:],
                                               xn1_d.ap()[ec * 128:(ec + 1) * 128,
                                               TQ:T]).then_inc(sLD.h, 16)
                        sLD.inc(16 * NEC)
                        sync.wait_ge(sLD.h, sLD.n)
                ld0, dv0 = sLD.n, sDVE.n
                with nc.Block() as blk:
                    @blk.sync
                    def _(sync, ld0=ld0, dv0=dv0):
                        for h in range(NH):
                            if h >= 1:
                                sync.wait_ge(sDVE.h, dv0 + h)
                            sync.dma_start(knrow_l[:], kn2_d.ap()[h:h + 1, :]
                                           ).then_inc(sLD.h, 16)

                    @blk.vector
                    def _(v, ld0=ld0):
                        for h in range(NH):
                            v.wait_ge(sLD.h, ld0 + 16 * (h + 1))
                            v.reduce_max(kmax_row[:, h:h + 1], knrow_l[:],
                                         axis=AX.X).then_inc(sDVE.h, 1)
                sLD.inc(16 * NH)
                sDVE.inc(NH)
                with nc.Block() as blk:
                    @blk.scalar
                    def _(s):
                        s.activation(kmaxd8[:], kmax_row[:], AF.Sqrt
                                     ).then_inc(sACT.h, 1)
                    sACT.inc(1)

                    @blk.vector
                    def _(v):
                        v.wait_ge(sACT.h, sACT.n)
                        v.tensor_scalar_mul(kmaxd8[:], kmaxd8[:], 1.0 / 8.0)
                for ec in range(NEC):
                    mm0, act0, st0 = sMM.n, sACT.n, sST.n
                    with nc.Block() as blk:
                        @blk.tensor
                        def _(t, ec=ec, act0=act0):
                            for qc in range(NQC):
                                if qc >= 2:
                                    t.wait_ge(sACT.h, act0 + qc - 1)
                                pq = ps[qc % 2]
                                for dc in range(NEC):
                                    ins = t.matmul(
                                        pq[:, 0:QCW],
                                        qw_sb[ec][:, dc * 128:(dc + 1) * 128],
                                        xn1q[dc][:, qc * QCW:(qc + 1) * QCW],
                                        start=(dc == 0), stop=(dc == 7))
                                    if dc == 7:
                                        ins.then_inc(sMM.h, 1)

                        @blk.scalar
                        def _(s, mm0=mm0, st0=st0):
                            for qc in range(NQC):
                                pq = ps[qc % 2]
                                s.wait_ge(sMM.h, mm0 + qc + 1)
                                if qc >= 2:
                                    s.wait_ge(sST.h, st0 + 16 * qc)
                                s.copy(qtile[qc % 2][:], pq[:, 0:QCW]
                                       ).then_inc(sACT.h, 1)

                        @blk.gpsimd
                        def _(g, ec=ec, act0=act0):
                            for qc in range(NQC):
                                g.wait_ge(sACT.h, act0 + qc + 1)
                                g.dma_start(Q_d.ap()[ec * 128:(ec + 1) * 128,
                                            qc * QCW:(qc + 1) * QCW],
                                            qtile[qc % 2][:]).then_inc(sST.h, 16)
                    sMM.inc(NQC)
                    sACT.inc(NQC)
                    sST.inc(16 * NQC)
                with nc.Block() as blk:
                    @blk.sync
                    def _(sync):
                        sync.wait_ge(sST.h, sST.n)
                ld0, mm0, act0, dv0 = sLD.n, sMM.n, sACT.n, sDVE.n
                with nc.Block() as blk:
                    @blk.sync
                    def _(sync, ld0=ld0, act0=act0):
                        for h in range(NH):
                            if h >= 2:
                                sync.wait_ge(sACT.h, act0 + 6 * (h - 2) + 1)
                            sync.dma_start(qh_l[h % 2][:],
                                           Q_d.ap()[h * 64:(h + 1) * 64, :]
                                           ).then_inc(sLD.h, 16)

                    @blk.scalar
                    def _(s, ld0=ld0, mm0=mm0, dv0=dv0):
                        for h in range(NH):
                            s.wait_ge(sLD.h, ld0 + 16 * (h + 1))
                            s.activation(qh_sq[h % 2][:], qh_l[h % 2][:], AF.Square
                                         ).then_inc(sACT.h, 1)
                            for qc in range(NQC):
                                s.wait_ge(sMM.h, mm0 + 4 * h + qc + 1)
                                s.copy(qn_row[:, qc * QCW:(qc + 1) * QCW],
                                       ps[4 + (h % 2) * 0][0:1, 0:QCW] if False
                                       else ps[4 + (h + qc) % 2][0:1, 0:QCW]
                                       ).then_inc(sACT.h, 1)
                            s.wait_ge(sDVE.h, dv0 + 2 * h + 1)
                            s.activation(qm4[:], qm4[:], AF.Sqrt).then_inc(sACT.h, 1)

                    @blk.tensor
                    def _(t, act0=act0):
                        for h in range(NH):
                            t.wait_ge(sACT.h, act0 + 6 * h + 1)
                            for qc in range(NQC):
                                if 4 * h + qc >= 2:
                                    t.wait_ge(sACT.h, act0 + 6 * ((4 * h + qc - 2)
                                              // 4) + ((4 * h + qc - 2) % 4) + 2)
                                t.matmul(ps[4 + (h + qc) % 2][0:1, 0:QCW],
                                         ones1[0:64, :],
                                         qh_sq[h % 2][:, qc * QCW:(qc + 1) * QCW],
                                         start=True, stop=True).then_inc(sMM.h, 1)

                    @blk.vector
                    def _(v, act0=act0):
                        for h in range(NH):
                            v.wait_ge(sACT.h, act0 + 6 * h + 5)
                            for qc in range(NQC):
                                ins = v.reduce_max(qm4[:, qc:qc + 1],
                                                   qn_row[:, qc * QCW:(qc + 1) * QCW],
                                                   axis=AX.X)
                                if qc == NQC - 1:
                                    ins.then_inc(sDVE.h, 1)
                            v.wait_ge(sDVE.h, dv0 + 2 * h + 1)
                            v.wait_ge(sACT.h, act0 + 6 * h + 6)
                            v.tensor_scalar_mul(qm4[:], qm4[:],
                                                kmaxd8[0:1, h:h + 1])
                            v.tensor_scalar_mul(cneg[:, 4 * h:4 * h + 4], qm4[:],
                                                -1.0).then_inc(sDVE.h, 1)
                sLD.inc(16 * NH)
                sMM.inc(4 * NH)
                sACT.inc(6 * NH)
                sDVE.inc(2 * NH)
                with nc.Block() as blk:
                    @blk.gpsimd
                    def _(g):
                        g.dma_start(rows_d.ap()[6:7, 0:64], cneg[:]
                                    ).then_inc(sST.h, 16)
                    sST.inc(16)

                    @blk.sync
                    def _(sync):
                        sync.wait_ge(sST.h, sST.n)
                        sync.dma_start(cbias[:], rows_d.ap()[6:7, 0:64]
                                       .to_broadcast((128, 64))).then_inc(sLD.h, 16)
                        sLD.inc(16)
                        sync.wait_ge(sLD.h, sLD.n)

            with ExitStack() as p6:
                khh2 = [sb(p6, f"khh2_{i}", [128, T // 2], BF16) for i in range(2)]
                qh2 = [sb(p6, f"qh2_{i}", [128, TQ], BF16) for i in range(2)]
                pt = [sb(p6, f"pt{i}", [128, QCW], BF16) for i in range(6)]
                oraw = sb(p6, "oraw", [65, TQ], F32)
                ob_t = sb(p6, "ob_t", [65, QCW], F32)
                reciprow = sb(p6, "reciprow", [1, TQ], F32)
                rb64 = sb(p6, "rb64", [64, TQ], F32)
                o_store = sb(p6, "o_store", [64, TQ], BF16)
                ld0, st0, mm0, act0, dv0 = sLD.n, sST.n, sMM.n, sACT.n, sDVE.n
                lb0, sp0 = sLB1.n, sSP.n
                NJP = NKT // 2

                def cE(h, qc, j):
                    return act0 + 129 * h + 32 * qc + 2 * (j + 1)

                def cS(h, qc, j):
                    return mm0 + 128 * h + 32 * qc + (1 if j == 0 else 2 * j)

                def cAV(h, qc, i):
                    return (mm0 + 128 * h + 32 * qc
                            + (2 * i + 3 if i < NJP - 1 else 32))

                def gV(h, qc):
                    return dv0 + 5 * h + qc + 1

                def prev_jq(h, qc, j):
                    if j >= 3:
                        return (h, qc, j - 3)
                    if qc >= 1:
                        return (h, qc - 1, j + NJP - 3)
                    if h >= 1:
                        return (h - 1, NQC - 1, j + NJP - 3)
                    return None

                with nc.Block() as blk:
                    @blk.sync
                    def _(sync, ld0=ld0, sp0=sp0, mm0=mm0):
                        for h in range(NH):
                            if h >= 1:
                                sync.wait_ge(sSP.h, sp0 + 16 * h)
                                sync.dma_start(rb64[:], rows_d.ap()[7:8, 0:TQ]
                                               .to_broadcast((64, TQ))
                                               ).then_inc(sLB1.h, 16)
                            if h >= 2:
                                sync.wait_ge(sMM.h, mm0 + 128 * (h - 1))
                            sync.dma_start(khh2[h % 2][0:64, :],
                                           K_d.ap()[h * 64:(h + 1) * 64, 0:T // 2]
                                           ).then_inc(sLD.h, 16)
                            sync.dma_start(khh2[h % 2][64:128, :],
                                           K_d.ap()[h * 64:(h + 1) * 64, T // 2:T]
                                           ).then_inc(sLD.h, 16)
                            sync.dma_start(qh2[h % 2][0:64, :],
                                           Q_d.ap()[h * 64:(h + 1) * 64, :]
                                           ).then_inc(sLD.h, 16)
                            sync.dma_start(qh2[h % 2][64:128, :],
                                           Q_d.ap()[h * 64:(h + 1) * 64, :]
                                           ).then_inc(sLD.h, 16)
                        sync.wait_ge(sSP.h, sp0 + 16 * NH)
                        sync.dma_start(rb64[:], rows_d.ap()[7:8, 0:TQ]
                                       .to_broadcast((64, TQ))).then_inc(sLB1.h, 16)

                    @blk.tensor
                    def _(t, ld0=ld0, mm0=mm0, act0=act0, dv0=dv0):
                        for h in range(NH):
                            t.wait_ge(sLD.h, ld0 + 64 * (h + 1))
                            for qc in range(NQC):
                                qs = slice(qc * QCW, (qc + 1) * QCW)

                                def spair(j, t=t, h=h, qc=qc, qs=qs):
                                    pv = prev_jq(h, qc, j)
                                    if pv is not None:
                                        t.wait_ge(sACT.h, cE(*pv))
                                    bs = (0, 2, 6)[j % 3]
                                    t.matmul(ps[bs][:, 0:QCW],
                                             khh2[h % 2][0:64,
                                                         j * 128:(j + 1) * 128],
                                             qh2[h % 2][0:64, qs],
                                             start=True, stop=True)
                                    t.matmul(ps[bs + 1][:, 0:QCW],
                                             khh2[h % 2][64:128,
                                                         j * 128:(j + 1) * 128],
                                             qh2[h % 2][64:128, qs],
                                             start=True, stop=True
                                             ).then_inc(sMM.h, 1)

                                def avg(i, t=t, h=h, qc=qc, qs=qs):
                                    t.wait_ge(sACT.h, cE(h, qc, i))
                                    if i == 0:
                                        if qc >= 1:
                                            t.wait_ge(sDVE.h, gV(h, qc - 1))
                                        elif h >= 1:
                                            t.wait_ge(sDVE.h, gV(h - 1, NQC - 1))
                                    for half, ktk in ((0, i), (1, i + NJP)):
                                        vb = V_sb[:, ktk * VROW + 65 * h:
                                                  ktk * VROW + 65 * h + 65]
                                        pm = pt[(2 * i + half) % 6]
                                        t.matmul(
                                            ps[4][0:65, 0:QCW], vb[0:64, :],
                                            pm[0:64, :],
                                            start=(i == 0 and half == 0),
                                            stop=(i == NJP - 1 and half == 1))
                                        ins = t.matmul(
                                            ps[5][0:65, 0:QCW], vb[64:128, :],
                                            pm[64:128, :],
                                            start=(i == 0 and half == 0),
                                            stop=(i == NJP - 1 and half == 1))
                                        if half == 1:
                                            ins.then_inc(sMM.h, 1)

                                for j in range(NJP):
                                    spair(j)
                                    if j >= 1:
                                        avg(j - 1)
                                avg(NJP - 1)

                    @blk.scalar
                    def _(s, sp0=sp0, mm0=mm0, act0=act0, dv0=dv0):
                        for h in range(NH):
                            for qc in range(NQC):
                                for j in range(NJP):
                                    s.wait_ge(sMM.h, cS(h, qc, j))
                                    pv = prev_jq(h, qc, j)
                                    if pv is not None:
                                        s.wait_ge(sMM.h, cAV(*pv))
                                    bs = (0, 2, 6)[j % 3]
                                    s.activation(pt[(2 * j) % 6][:],
                                                 ps[bs][:, 0:QCW], AF.Exp,
                                                 scale=0.125,
                                                 bias=cbias[:, 4 * h + qc:
                                                            4 * h + qc + 1]
                                                 ).then_inc(sACT.h, 1)
                                    s.activation(pt[(2 * j + 1) % 6][:],
                                                 ps[bs + 1][:, 0:QCW], AF.Exp,
                                                 scale=0.125,
                                                 bias=cbias[:, 4 * h + qc:
                                                            4 * h + qc + 1]
                                                 ).then_inc(sACT.h, 1)
                            if h >= 1:
                                s.wait_ge(sSP.h, sp0 + 16 * h)
                            s.wait_ge(sDVE.h, gV(h, NQC - 1))
                            s.activation(reciprow[:], oraw[64:65, :], AF.Ln)
                            s.activation(reciprow[:], reciprow[:], AF.Exp,
                                         scale=-1.0).then_inc(sACT.h, 1)

                    @blk.gpsimd
                    def _(g, act0=act0, dv0=dv0):
                        for h in range(NH):
                            g.wait_ge(sACT.h, act0 + 129 * h + 129)
                            g.dma_start(rows_d.ap()[7:8, 0:TQ], reciprow[:]
                                        ).then_inc(sSP.h, 16)
                            g.wait_ge(sDVE.h, dv0 + 5 * h + 5)
                            g.dma_start(o_d.ap()[h * 64:(h + 1) * 64, :],
                                        o_store[:]).then_inc(sST.h, 16)

                    @blk.vector
                    def _(v, lb0=lb0, st0=st0, mm0=mm0):
                        for h in range(NH):
                            for qc in range(NQC):
                                v.wait_ge(sMM.h, cAV(h, qc, NJP - 1))
                                v.tensor_copy(ob_t[:], ps[5][0:65, 0:QCW])
                                v.tensor_add(oraw[0:65, qc * QCW:(qc + 1) * QCW],
                                             ps[4][0:65, 0:QCW], ob_t[:]
                                             ).then_inc(sDVE.h, 1)
                            v.wait_ge(sLB1.h, lb0 + 16 * (h + 1))
                            if h >= 1:
                                v.wait_ge(sST.h, st0 + 16 * h)
                            v.tensor_mul(o_store[:], oraw[0:64, :], rb64[:]
                                         ).then_inc(sDVE.h, 1)
                sLD.inc(64 * NH)
                sLB1.inc(16 * NH)
                sSP.inc(16 * NH)
                sST.inc(16 * NH)
                sMM.inc(128 * NH)
                sACT.inc(129 * NH)
                sDVE.inc(5 * NH)
                with nc.Block() as blk:
                    @blk.sync
                    def _(sync):
                        sync.wait_ge(sST.h, sST.n)
                        sync.wait_ge(sLD.h, sLD.n)
                        sync.wait_ge(sLB1.h, sLB1.n)
                        sync.wait_ge(sSP.h, sSP.n)

        with ExitStack() as pc:
            acc = [sb(pc, f"acc{i}", [128, TQ], F32) for i in range(NEC)]
            with nc.Block() as blk:
                @blk.sync
                def _(sync):
                    with sync.If_eq(flag_reg, 0):
                        for ec in range(NEC):
                            sync.dma_start(acc[ec][:],
                                           x1_d.ap()[ec * 128:(ec + 1) * 128, 0:TQ]
                                           ).then_inc(sLD.h, 16)
                    with sync.Else():
                        for ec in range(NEC):
                            sync.dma_start(acc[ec][:],
                                           x1_d.ap()[ec * 128:(ec + 1) * 128, TQ:T]
                                           ).then_inc(sLD.h, 16)
                    sLD.inc(16 * NEC)
                    sync.wait_ge(sLD.h, sLD.n)
            with ExitStack() as p7:
                o_sb = [sb(p7, f"o_sb{i}", [128, TQ], BF16) for i in range(NEC)]
                ow_sb = [sb(p7, f"owsb{i}", [128, D], BF16) for i in range(NEC)]
                wo_ev = [sb(p7, f"wo_ev{i}", [128, QCW], F32) for i in range(2)]
                with nc.Block() as blk:
                    @blk.sync
                    def _(sync):
                        for dc in range(NEC):
                            sync.dma_start(o_sb[dc][:],
                                           o_d.ap()[dc * 128:(dc + 1) * 128, :]
                                           ).then_inc(sLD.h, 16)
                            sLD.inc(16)
                        for ec in range(NEC):
                            sync.dma_start(ow_sb[ec][:], wo1_b.ap()[ec]
                                           ).then_inc(sLD.h, 16)
                            sLD.inc(16)
                        sync.wait_ge(sLD.h, sLD.n)
                mm0, act0, dv0 = sMM.n, sACT.n, sDVE.n
                with nc.Block() as blk:
                    @blk.tensor
                    def _(t, act0=act0):
                        i = 0
                        for ec in range(NEC):
                            for qc in range(NQC):
                                if i >= 2:
                                    t.wait_ge(sACT.h, act0 + i - 1)
                                pw = ps[i % 2]
                                for dc in range(NEC):
                                    ins = t.matmul(
                                        pw[:, 0:QCW],
                                        ow_sb[ec][:, dc * 128:(dc + 1) * 128],
                                        o_sb[dc][:, qc * QCW:(qc + 1) * QCW],
                                        start=(dc == 0), stop=(dc == 7))
                                    if dc == 7:
                                        ins.then_inc(sMM.h, 1)
                                i += 1

                    @blk.scalar
                    def _(s, mm0=mm0, dv0=dv0):
                        i = 0
                        for ec in range(NEC):
                            for qc in range(NQC):
                                s.wait_ge(sMM.h, mm0 + i + 1)
                                if i >= 2:
                                    s.wait_ge(sDVE.h, dv0 + i - 1)
                                s.copy(wo_ev[i % 2][:], ps[i % 2][:, 0:QCW]
                                       ).then_inc(sACT.h, 1)
                                i += 1

                    @blk.vector
                    def _(v, act0=act0):
                        i = 0
                        for ec in range(NEC):
                            for qc in range(NQC):
                                v.wait_ge(sACT.h, act0 + i + 1)
                                v.tensor_add(acc[ec][:, qc * QCW:(qc + 1) * QCW],
                                             acc[ec][:, qc * QCW:(qc + 1) * QCW],
                                             wo_ev[i % 2][:]).then_inc(sDVE.h, 1)
                                i += 1
                sMM.inc(NEC * NQC)
                sACT.inc(NEC * NQC)
                sDVE.inc(NEC * NQC)
            with ExitStack() as p8:
                xq1 = [sb(p8, f"xq1_{i}", [128, QCW], BF16) for i in range(NEC)]
                ha = [sb(p8, f"ha{i}", [128, QCW], BF16) for i in range(NHC)]
                hb = [sb(p8, f"hb{i}", [128, QCW], BF16) for i in range(NHC)]
                wt1 = [sb(p8, f"wt1_{i}", [128, D], BF16) for i in range(2)]
                wt2 = [sb(p8, f"wt2_{i}", [128, D], BF16) for i in range(2)]
                wth = [sb(p8, f"wth{i}", [128, NHC * 128], BF16) for i in range(2)]
                sil_t = sb(p8, "sil_t", [128, QCW], BF16)
                u_t = sb(p8, "u_t", [128, QCW], BF16)
                w3_ev = [sb(p8, f"w3_ev{i}", [128, QCW], F32) for i in range(2)]
                for qc in range(NQC):
                    with nc.Block() as blk:
                        @blk.sync
                        def _(sync, qc=qc):
                            with sync.If_eq(flag_reg, 0):
                                for ec in range(NEC):
                                    sync.dma_start(
                                        xq1[ec][:],
                                        xn1_d.ap()[ec * 128:(ec + 1) * 128,
                                                   qc * QCW:(qc + 1) * QCW]
                                    ).then_inc(sLD.h, 16)
                            with sync.Else():
                                for ec in range(NEC):
                                    sync.dma_start(
                                        xq1[ec][:],
                                        xn1_d.ap()[ec * 128:(ec + 1) * 128,
                                                   TQ + qc * QCW:
                                                   TQ + (qc + 1) * QCW]
                                    ).then_inc(sLD.h, 16)
                            sLD.inc(16 * NEC)
                            sync.wait_ge(sLD.h, sLD.n)
                    ld0, mm0, act0, dv0 = sLD.n, sMM.n, sACT.n, sDVE.n
                    with nc.Block() as blk:
                        @blk.sync
                        def _(sync, ld0=ld0, mm0=mm0):
                            for hc in range(NHC):
                                if hc >= 2:
                                    sync.wait_ge(sMM.h, mm0 + 2 * (hc - 1))
                                sync.dma_start(wt1[hc % 2][:], w1t_b.ap()[hc]
                                               ).then_inc(sLD.h, 16)
                                sync.dma_start(wt2[hc % 2][:], w2t_b.ap()[hc]
                                               ).then_inc(sLD.h, 16)

                        @blk.tensor
                        def _(t, ld0=ld0, act0=act0):
                            for hc in range(NHC):
                                t.wait_ge(sLD.h, ld0 + 32 * (hc + 1))
                                if hc >= 2:
                                    t.wait_ge(sACT.h, act0 + 2 * (hc - 1))
                                p1 = ps[2 * (hc % 2)]
                                p2 = ps[2 * (hc % 2) + 1]
                                for dc in range(NEC):
                                    ins = t.matmul(
                                        p1[:, 0:QCW],
                                        wt1[hc % 2][:, dc * 128:(dc + 1) * 128],
                                        xq1[dc][:], start=(dc == 0), stop=(dc == 7))
                                    if dc == 7:
                                        ins.then_inc(sMM.h, 1)
                                for dc in range(NEC):
                                    ins = t.matmul(
                                        p2[:, 0:QCW],
                                        wt2[hc % 2][:, dc * 128:(dc + 1) * 128],
                                        xq1[dc][:], start=(dc == 0), stop=(dc == 7))
                                    if dc == 7:
                                        ins.then_inc(sMM.h, 1)

                        @blk.scalar
                        def _(s, mm0=mm0, dv0=dv0):
                            for hc in range(NHC):
                                p1 = ps[2 * (hc % 2)]
                                p2 = ps[2 * (hc % 2) + 1]
                                s.wait_ge(sMM.h, mm0 + 2 * hc + 1)
                                if hc >= 1:
                                    s.wait_ge(sDVE.h, dv0 + hc)
                                s.activation(sil_t[:], p1[:, 0:QCW], AF.Silu,
                                             scale=V["w1s"][:, hc:hc + 1])
                                s.wait_ge(sMM.h, mm0 + 2 * hc + 2)
                                s.activation(u_t[:], p2[:, 0:QCW], AF.Identity,
                                             scale=V["w2s"][:, hc:hc + 1]
                                             ).then_inc(sACT.h, 2)

                        @blk.vector
                        def _(v, act0=act0):
                            for hc in range(NHC):
                                v.wait_ge(sACT.h, act0 + 2 * hc + 2)
                                v.tensor_mul(ha[hc][:], sil_t[:], u_t[:]
                                             ).then_inc(sDVE.h, 1)
                    sLD.inc(32 * NHC)
                    sMM.inc(2 * NHC)
                    sACT.inc(2 * NHC)
                    sDVE.inc(NHC)
                    for src_t, dst_t, wmat, svec in ((ha, hb, h1t_b, "h1s"),
                                                     (hb, ha, h2t_b, "h2s")):
                        ld0, mm0, act0 = sLD.n, sMM.n, sACT.n
                        with nc.Block() as blk:
                            @blk.sync
                            def _(sync, wmat=wmat, ld0=ld0, mm0=mm0):
                                for oc in range(NHC):
                                    if oc >= 2:
                                        sync.wait_ge(sMM.h, mm0 + oc - 1)
                                    sync.dma_start(wth[oc % 2][:], wmat.ap()[oc]
                                                   ).then_inc(sLD.h, 16)

                            @blk.tensor
                            def _(t, src_t=src_t, ld0=ld0, act0=act0):
                                for oc in range(NHC):
                                    t.wait_ge(sLD.h, ld0 + 16 * (oc + 1))
                                    if oc >= 2:
                                        t.wait_ge(sACT.h, act0 + oc - 1)
                                    po = ps[4 + oc % 2]
                                    for ic in range(NHC):
                                        ins = t.matmul(
                                            po[:, 0:QCW],
                                            wth[oc % 2][:, ic * 128:(ic + 1) * 128],
                                            src_t[ic][:], start=(ic == 0),
                                            stop=(ic == NHC - 1))
                                        if ic == NHC - 1:
                                            ins.then_inc(sMM.h, 1)

                            @blk.scalar
                            def _(s, dst_t=dst_t, svec=svec, mm0=mm0):
                                for oc in range(NHC):
                                    po = ps[4 + oc % 2]
                                    s.wait_ge(sMM.h, mm0 + oc + 1)
                                    s.activation(dst_t[oc][:], po[:, 0:QCW],
                                                 AF.Gelu,
                                                 scale=V[svec][:, oc:oc + 1]
                                                 ).then_inc(sACT.h, 1)
                        sLD.inc(16 * NHC)
                        sMM.inc(NHC)
                        sACT.inc(NHC)
                    ld0, mm0, act0, dv0 = sLD.n, sMM.n, sACT.n, sDVE.n
                    with nc.Block() as blk:
                        @blk.sync
                        def _(sync, ld0=ld0, mm0=mm0):
                            for ec in range(NEC):
                                if ec >= 2:
                                    sync.wait_ge(sMM.h, mm0 + ec - 1)
                                sync.dma_start(wth[ec % 2][:], w3t_b.ap()[ec]
                                               ).then_inc(sLD.h, 16)

                        @blk.tensor
                        def _(t, ld0=ld0, act0=act0):
                            for ec in range(NEC):
                                t.wait_ge(sLD.h, ld0 + 16 * (ec + 1))
                                if ec >= 2:
                                    t.wait_ge(sACT.h, act0 + ec - 1)
                                po = ps[6 + ec % 2]
                                for ic in range(NHC):
                                    ins = t.matmul(
                                        po[:, 0:QCW],
                                        wth[ec % 2][:, ic * 128:(ic + 1) * 128],
                                        ha[ic][:], start=(ic == 0),
                                        stop=(ic == NHC - 1))
                                    if ic == NHC - 1:
                                        ins.then_inc(sMM.h, 1)

                        @blk.scalar
                        def _(s, mm0=mm0, dv0=dv0):
                            for ec in range(NEC):
                                po = ps[6 + ec % 2]
                                s.wait_ge(sMM.h, mm0 + ec + 1)
                                if ec >= 2:
                                    s.wait_ge(sDVE.h, dv0 + ec - 1)
                                s.activation(w3_ev[ec % 2][:], po[:, 0:QCW],
                                             AF.Identity,
                                             scale=V["w3s"][:, ec:ec + 1]
                                             ).then_inc(sACT.h, 1)

                        @blk.vector
                        def _(v, qc=qc, act0=act0):
                            for ec in range(NEC):
                                v.wait_ge(sACT.h, act0 + ec + 1)
                                v.tensor_add(acc[ec][:, qc * QCW:(qc + 1) * QCW],
                                             acc[ec][:, qc * QCW:(qc + 1) * QCW],
                                             w3_ev[ec % 2][:]).then_inc(sDVE.h, 1)
                    sLD.inc(16 * NEC)
                    sMM.inc(NEC)
                    sACT.inc(NEC)
                    sDVE.inc(NEC)
            with nc.Block() as blk:
                @blk.gpsimd
                def _(g):
                    for ec in range(NEC):
                        g.dma_start(y.ap()[ec * 128:(ec + 1) * 128, :], acc[ec][:]
                                    ).then_inc(sST.h, 16)
                    sST.inc(16 * NEC)
                    g.wait_ge(sST.h, sST.n)

                @blk.sync
                def _(sync):
                    sync.wait_ge(sLD.h, sLD.n)
    return nc


def _pack_vec(v, k):
    return np.ascontiguousarray(np.asarray(v, np.float32).reshape(k, 128).T)


def _tile_T(w, noc):
    wT = np.asarray(w, np.float32).T
    nk = wT.shape[0] // 128
    A = wT.reshape(nk, 128, noc, 128).transpose(2, 1, 0, 3)
    return np.ascontiguousarray(A.reshape(noc, 128, nk * 128))


def _rowchunk_T(w):
    wT = np.ascontiguousarray(np.asarray(w, np.float32).T)
    return wT.reshape(wT.shape[0] // 128, 128, wT.shape[1])


def kernel(**inputs):
    xin = np.asarray(inputs["L0_attn_x"], np.float32)
    base = {
        "ident_in": np.eye(128, dtype=np.float32),
        "oneslo_in": np.concatenate([np.ones(64, np.float32),
                                     np.zeros(64, np.float32)]).reshape(128, 1),
        "oneshi_in": np.concatenate([np.zeros(64, np.float32),
                                     np.ones(64, np.float32)]).reshape(128, 1),
        "l0_wkT": np.ascontiguousarray(
            np.asarray(inputs["L0_attn_Wk"], np.float32).T),
        "l0_wvT": np.ascontiguousarray(
            np.asarray(inputs["L0_attn_Wv"], np.float32).T),
        "l0_wrT": np.ascontiguousarray(
            np.asarray(inputs["L0_attn_Wr"], np.float32).T),
        "l0_woT": np.ascontiguousarray(
            np.asarray(inputs["L0_attn_Wo"], np.float32).T),
        "f0_w": np.ascontiguousarray(np.asarray(inputs["L0_ffn_weight"],
                                                np.float32)),
        "wq1": _tile_T(inputs["L1_attn_wq"], 8),
        "wk1": _tile_T(inputs["L1_attn_wk"], 8),
        "wv1": _rowchunk_T(inputs["L1_attn_wv"]),
        "wo1": _tile_T(inputs["L1_attn_wo"], 8),
        "w1t": _tile_T(inputs["L1_ffn_w1_w"], NHC),
        "w2t": _tile_T(inputs["L1_ffn_w2_w"], NHC),
        "h1t": _tile_T(inputs["L1_ffn_h1_w"], NHC),
        "h2t": _tile_T(inputs["L1_ffn_h2_w"], NHC),
        "w3t": _tile_T(inputs["L1_ffn_w3_w"], NEC),
        "vec_u": _pack_vec(inputs["L0_attn_u"], 8),
        "vec_w": _pack_vec(inputs["L0_attn_w"], 8),
        "vec_tmk": _pack_vec(inputs["L0_attn_tmk"], 8),
        "vec_tmv": _pack_vec(inputs["L0_attn_tmv"], 8),
        "vec_tmr": _pack_vec(inputs["L0_attn_tmr"], 8),
        "vec_bo": _pack_vec(inputs["L0_attn_bo"], 8),
        "vec_b0f": _pack_vec(inputs["L0_ffn_bias"], 8),
        "vec_w1s": _pack_vec(inputs["L1_ffn_w1_s"], 32),
        "vec_w2s": _pack_vec(inputs["L1_ffn_w2_s"], 32),
        "vec_h1s": _pack_vec(inputs["L1_ffn_h1_s"], 32),
        "vec_h2s": _pack_vec(inputs["L1_ffn_h2_s"], 32),
        "vec_w3s": _pack_vec(inputs["L1_ffn_w3_s"], 8),
    }
    in_maps = []
    for c in range(8):
        b, qh = divmod(c, 2)
        m = dict(base)
        m["xT"] = np.ascontiguousarray(xin[b].T)
        m["qflag"] = np.array([[qh]], np.int32)
        in_maps.append(m)
    nc = build()
    res = run_bass_kernel_spmd(nc, in_maps, core_ids=list(range(8)))
    out = np.empty((4, T, D), np.float32)
    for c in range(8):
        b, qh = divmod(c, 2)
        out[b, qh * TQ:(qh + 1) * TQ, :] = res.results[c]["y"].T
    return out




# version 17
# speedup vs baseline: 1.0263x; 1.0094x over previous
from contextlib import ExitStack

import numpy as np

import concourse.bass as bass
import concourse.mybir as mybir
from concourse.bass_utils import run_bass_kernel_spmd

F32 = mybir.dt.float32
BF16 = mybir.dt.bfloat16
I32 = mybir.dt.int32
AF = mybir.ActivationFunctionType
ALU = mybir.AluOpType
AX = mybir.AxisListType

D = 1024
T = 4096
TQ = 2048
NEC = 8
TC = 256
NTC = T // TC
NH = 16
DHID = 4096
NHC = DHID // 128
QCW = 512
NQC = TQ // QCW
NKT = T // 128
KCW = 512
NKC = T // KCW
VROW = 1040


class SemC:
    def __init__(self, h):
        self.h = h
        self.n = 0

    def inc(self, k=1):
        self.n += k
        return self.n


def build():
    nc = bass.Bass("TRN2", target_bir_lowering=False)
    nc.detect_race_conditions = False

    xT = nc.dram_tensor("xT", [D, T], F32, kind="ExternalInput")
    qflag = nc.dram_tensor("qflag", [1, 1], I32, kind="ExternalInput")
    ident_in = nc.dram_tensor("ident_in", [128, 128], F32, kind="ExternalInput")
    oneslo_in = nc.dram_tensor("oneslo_in", [128, 1], F32, kind="ExternalInput")
    oneshi_in = nc.dram_tensor("oneshi_in", [128, 1], F32, kind="ExternalInput")
    l0_wkT = nc.dram_tensor("l0_wkT", [D, D], F32, kind="ExternalInput")
    l0_wvT = nc.dram_tensor("l0_wvT", [D, D], F32, kind="ExternalInput")
    l0_wrT = nc.dram_tensor("l0_wrT", [D, D], F32, kind="ExternalInput")
    l0_woT = nc.dram_tensor("l0_woT", [D, D], F32, kind="ExternalInput")
    f0_w = nc.dram_tensor("f0_w", [D, D], F32, kind="ExternalInput")
    wq1 = nc.dram_tensor("wq1", [8, 128, D], F32, kind="ExternalInput")
    wk1 = nc.dram_tensor("wk1", [8, 128, D], F32, kind="ExternalInput")
    wv1 = nc.dram_tensor("wv1", [8, 128, D], F32, kind="ExternalInput")
    wo1 = nc.dram_tensor("wo1", [8, 128, D], F32, kind="ExternalInput")
    w1t = nc.dram_tensor("w1t", [NHC, 128, D], F32, kind="ExternalInput")
    w2t = nc.dram_tensor("w2t", [NHC, 128, D], F32, kind="ExternalInput")
    h1t = nc.dram_tensor("h1t", [NHC, 128, DHID], F32, kind="ExternalInput")
    h2t = nc.dram_tensor("h2t", [NHC, 128, DHID], F32, kind="ExternalInput")
    w3t = nc.dram_tensor("w3t", [NEC, 128, DHID], F32, kind="ExternalInput")
    vin = {}
    for nm, k in (("u", 8), ("w", 8), ("tmk", 8), ("tmv", 8), ("tmr", 8),
                  ("bo", 8), ("b0f", 8), ("w1s", 32), ("w2s", 32), ("h1s", 32),
                  ("h2s", 32), ("w3s", 8)):
        vin[nm] = nc.dram_tensor("vec_" + nm, [128, k], F32, kind="ExternalInput")
    y = nc.dram_tensor("y", [D, TQ], F32, kind="ExternalOutput")

    x1_d = nc.dram_tensor("x1_d", [D, T], F32, kind="Internal")
    xn1_d = nc.dram_tensor("xn1_d", [D, T], BF16, kind="Internal")
    K_d = nc.dram_tensor("K_d", [D, T], BF16, kind="Internal")
    Q_d = nc.dram_tensor("Q_d", [D, TQ], BF16, kind="Internal")
    o_d = nc.dram_tensor("o_d", [D, TQ], BF16, kind="Internal")
    kn2_d = nc.dram_tensor("kn2_d", [NH, T], F32, kind="Internal")
    rows_d = nc.dram_tensor("rows_d", [8, T], F32, kind="Internal")
    wq1_b = nc.dram_tensor("wq1_b", [8, 128, D], BF16, kind="Internal")
    wk1_b = nc.dram_tensor("wk1_b", [8, 128, D], BF16, kind="Internal")
    wv1_b = nc.dram_tensor("wv1_b", [8, 128, D], BF16, kind="Internal")
    wo1_b = nc.dram_tensor("wo1_b", [8, 128, D], BF16, kind="Internal")
    w1t_b = nc.dram_tensor("w1t_b", [NHC, 128, D], BF16, kind="Internal")
    w2t_b = nc.dram_tensor("w2t_b", [NHC, 128, D], BF16, kind="Internal")
    h1t_b = nc.dram_tensor("h1t_b", [NHC, 128, DHID], BF16, kind="Internal")
    h2t_b = nc.dram_tensor("h2t_b", [NHC, 128, DHID], BF16, kind="Internal")
    w3t_b = nc.dram_tensor("w3t_b", [NEC, 128, DHID], BF16, kind="Internal")

    ctx = ExitStack()
    with ctx:
        mksem = lambda name: SemC(ctx.enter_context(nc.semaphore(name)))
        sLD = mksem("sLD")
        sLDo = mksem("sLDo")
        sWL = [mksem(f"sWL{i}") for i in range(6)]
        sWS = [mksem(f"sWS{i}") for i in range(6)]
        sCC = mksem("sCC")
        sLB1 = mksem("sLB1")
        sLB2 = mksem("sLB2")
        sST = mksem("sST")
        sSC = mksem("sSC")
        sSP = mksem("sSP")
        sSX = mksem("sSX")
        sSR = [mksem("sSRe"), mksem("sSRo")]
        sMM = mksem("sMM")
        sACT = mksem("sACT")
        sDVE = mksem("sDVE")

        def sb(st, name, shape, dt):
            return st.enter_context(nc.sbuf_tensor(name, shape, dt))

        ident = sb(ctx, "ident", [128, 128], F32)
        epsc = sb(ctx, "epsc", [1, 1], F32)
        ln127c = sb(ctx, "ln127c", [128, 1], F32)
        eps8c = sb(ctx, "eps8c", [128, 1], F32)
        neps8c = sb(ctx, "neps8c", [128, 1], F32)
        ones1 = sb(ctx, "ones1", [128, 1], F32)
        oneslo = sb(ctx, "oneslo", [128, 1], F32)
        oneshi = sb(ctx, "oneshi", [128, 1], F32)
        V = {nm: sb(ctx, "v_" + nm, [128, k], F32)
             for nm, k in (("u", 8), ("w", 8), ("negw", 8), ("umw", 8),
                           ("wmu", 8), ("m2w", 8), ("tmk", 8), ("tmv", 8),
                           ("tmr", 8), ("bo", 8), ("b0f", 8), ("w1s", 32),
                           ("w2s", 32), ("h1s", 32), ("h2s", 32), ("w3s", 8),
                           ("wscale0", 8), ("stA", 8), ("stB", 8), ("prevcol", 8))}
        ps = [ctx.enter_context(nc.psum_tensor(f"ps{i}", [128, 512], F32))
              for i in range(8)]
        flag_reg = ctx.enter_context(nc.sync.register("flag_reg"))

        with nc.Block() as blk:
            @blk.sync
            def _(sync):
                sync.dma_start(ident[:], ident_in.ap()).then_inc(sLD.h, 16)
                sync.dma_start(oneslo[:], oneslo_in.ap()).then_inc(sLD.h, 16)
                sync.dma_start(oneshi[:], oneshi_in.ap()).then_inc(sLD.h, 16)
                sLD.inc(48)
                for nm in ("u", "w", "tmk", "tmv", "tmr", "bo", "b0f", "w1s",
                           "w2s", "h1s", "h2s", "w3s"):
                    sync.dma_start(V[nm][:], vin[nm].ap()).then_inc(sLD.h, 16)
                    sLD.inc(16)
                sync.wait_ge(sLD.h, sLD.n)
                sync.reg_load(flag_reg, qflag.ap().to_broadcast((1, 1)))

            @blk.vector
            def _(v):
                v.memset(ones1[:], 1.0)
                v.memset(V["stA"][:], 0.0)
                v.memset(V["stB"][:], 0.0)
                v.memset(V["prevcol"][:], 0.0)
                v.memset(epsc[:], 1e-5)
                v.memset(ln127c[:], 4.844187086458591)
                v.memset(eps8c[:], 1e-8)
                v.memset(neps8c[:], -1e-8)
        with nc.Block() as blk:
            @blk.vector
            def _(v):
                v.tensor_scalar_mul(V["negw"][:], V["w"][:], -1.0)
                v.tensor_sub(V["umw"][:], V["u"][:], V["w"][:])
                v.tensor_sub(V["wmu"][:], V["w"][:], V["u"][:])
                v.tensor_scalar_mul(V["m2w"][:], V["w"][:], -2.0)

        with ExitStack() as pc:
            wk_sb = [sb(pc, f"wk_sb{i}", [128, D], BF16) for i in range(NEC)]
            wv_sb = [sb(pc, f"wv_sb{i}", [128, D], BF16) for i in range(NEC)]
            wr_sb = [sb(pc, f"wr_sb{i}", [128, D], BF16) for i in range(NEC)]
            wo_sb = [sb(pc, f"wo_sb{i}", [128, D], BF16) for i in range(NEC)]
            wq0_sb = [sb(pc, f"wq0_sb{i}", [128, D], BF16) for i in range(NEC)]
            wld = [sb(pc, f"wld{i}", [128, D], F32) for i in range(3)]

            p2t = [(mat, dstl, dc)
                   for mat, dstl in ((l0_wkT, wk_sb), (l0_wvT, wv_sb),
                                     (l0_wrT, wr_sb), (l0_woT, wo_sb))
                   for dc in range(NEC)]
            w20 = [sWL[b].n for b in range(3)]
            cc20 = sCC.n
            with nc.Block() as blk:
                @blk.sync
                def _(sync):
                    for ti, (mat, dstl, dc) in enumerate(p2t):
                        if ti % 2 != 0:
                            continue
                        if ti >= 3:
                            sync.wait_ge(sCC.h, cc20 + ti - 2)
                        sync.dma_start(wld[ti % 3][:],
                                       mat.ap()[dc * 128:(dc + 1) * 128, :]
                                       ).then_inc(sWL[ti % 3].h, 16)

                @blk.scalar
                def _(s):
                    for ti, (mat, dstl, dc) in enumerate(p2t):
                        if ti % 2 != 1:
                            continue
                        if ti >= 3:
                            s.wait_ge(sCC.h, cc20 + ti - 2)
                        s.dma_start(wld[ti % 3][:],
                                    mat.ap()[dc * 128:(dc + 1) * 128, :]
                                    ).then_inc(sWL[ti % 3].h, 16)

                @blk.vector
                def _(v):
                    for ti, (mat, dstl, dc) in enumerate(p2t):
                        v.wait_ge(sWL[ti % 3].h,
                                  w20[ti % 3] + 16 * (ti // 3 + 1))
                        v.tensor_copy(dstl[dc][:], wld[ti % 3][:]
                                      ).then_inc(sCC.h, 1)
            for b in range(3):
                sWL[b].n = w20[b] + 16 * len(
                    [t for t in range(len(p2t)) if t % 3 == b])
            sCC.inc(len(p2t))
            with nc.Block() as blk:
                @blk.sync
                def _(sync):
                    sync.wait_ge(sCC.h, sCC.n)

            with ExitStack() as p2bs:
                wqf = sb(p2bs, "wqf", [128, D], F32)
                wamax = sb(p2bs, "wamax", [128, 1], F32)
                wr127 = sb(p2bs, "wr127", [128, 1], F32)
                rnd_i = sb(p2bs, "rnd_i", [128, D], I32)
                rnd_f = sb(p2bs, "rnd_f", [128, D], F32)
                rnd_d = sb(p2bs, "rnd_d", [128, D], F32)
                rnd_c = sb(p2bs, "rnd_c", [128, D], F32)
                for ec in range(NEC):
                    with nc.Block() as blk:
                        @blk.sync
                        def _(sync, ec=ec):
                            sync.dma_start(wld[0][:], f0_w.ap()[ec * 128:(ec + 1) * 128, :]
                                           ).then_inc(sLD.h, 16)
                            sLD.inc(16)
                            sync.wait_ge(sLD.h, sLD.n)
                    with nc.Block() as blk:
                        @blk.vector
                        def _(v, ec=ec):
                            v.reduce_max(wamax[:, 0:1], wld[0][:], axis=AX.X,
                                         apply_absolute_value=True
                                         ).then_inc(sDVE.h, 1)
                            sDVE.inc(1)
                            v.wait_ge(sDVE.h, sDVE.n)
                            v.tensor_scalar_max(wamax[:, 0:1], wamax[:, 0:1], 1e-8)
                            v.tensor_scalar_mul(V["wscale0"][:, ec:ec + 1],
                                                wamax[:, 0:1], 1.0 / 127.0)
                    with nc.Block() as blk:
                        @blk.vector
                        def _(v, ec=ec):
                            v.reciprocal(wr127[:, 0:1], V["wscale0"][:, ec:ec + 1])
                    with nc.Block() as blk:
                        @blk.vector
                        def _(v):
                            v.tensor_scalar_mul(wqf[:], wld[0][:], wr127[:, 0:1])
                            v.tensor_copy(rnd_i[:], wqf[:])
                            v.tensor_copy(rnd_f[:], rnd_i[:])
                            v.tensor_sub(rnd_d[:], wqf[:], rnd_f[:])
                            v.scalar_tensor_tensor(rnd_f[:], rnd_d[:], 0.5, rnd_f[:],
                                                   ALU.is_gt, ALU.add)
                            v.tensor_scalar(rnd_c[:], rnd_d[:], -0.5, None, ALU.is_lt)
                            v.tensor_sub(wqf[:], rnd_f[:], rnd_c[:])
                    mm0, act0 = sMM.n, sACT.n
                    with nc.Block() as blk:
                        @blk.tensor
                        def _(t, mm0=mm0, act0=act0):
                            for dc in range(NEC):
                                if dc >= 2:
                                    t.wait_ge(sACT.h, act0 + dc - 1)
                                t.transpose(ps[dc % 2][0:128, 0:128],
                                            wqf[:, dc * 128:(dc + 1) * 128], ident[:]
                                            ).then_inc(sMM.h, 1)

                        @blk.scalar
                        def _(s, ec=ec, mm0=mm0):
                            for dc in range(NEC):
                                s.wait_ge(sMM.h, mm0 + dc + 1)
                                s.copy(wq0_sb[dc][:, ec * 128:(ec + 1) * 128],
                                       ps[dc % 2][0:128, 0:128]).then_inc(sACT.h, 1)
                    sMM.inc(NEC)
                    sACT.inc(NEC)

            with ExitStack() as pa:
                wtmp = [sb(pa, f"wtmp{i}", [128, 2048], F32) for i in range(6)]
                wtmpb = [sb(pa, f"wtmpb{i}", [128, 2048], BF16) for i in range(6)]
                x_a = [[sb(pa, f"x_a{j}_{i}", [128, TC], F32) for i in range(NEC)]
                       for j in range(2)]
                x2a = [sb(pa, f"x2a{i}", [128, TC], F32) for i in range(2)]
                x2d = sb(pa, "x2d", [128, TC], F32)
                mrow = [sb(pa, f"mrow{i}", [1, TC], F32) for i in range(2)]
                vrow = [sb(pa, f"vrow{i}", [1, TC], F32) for i in range(2)]
                rrow = [sb(pa, f"rrow{i}", [1, TC], F32) for i in range(2)]
                nrow = [sb(pa, f"nrow{i}", [1, TC], F32) for i in range(2)]
                tiles = []
                for srcw, dstw, noc, kk in (
                        (wk1, wk1_b, 8, D), (wv1, wv1_b, 8, D),
                        (wq1, wq1_b, 8, D), (wo1, wo1_b, 8, D),
                        (w1t, w1t_b, NHC, D), (w2t, w2t_b, NHC, D),
                        (w3t, w3t_b, NEC, DHID),
                        (h1t, h1t_b, NHC, DHID), (h2t, h2t_b, NHC, DHID)):
                    for oc in range(noc):
                        for off in range(0, kk, 2048):
                            tiles.append((srcw, dstw, oc, off,
                                          min(2048, kk - off)))
                ncast = len(tiles)
                nbsem = [len([c for c in range(ncast) if c % 6 == b])
                         for b in range(6)]
                cbnd = [(i * ncast) // NTC for i in range(NTC + 1)]
                csplit = [list(range(cbnd[i], cbnd[i + 1]))
                          for i in range(NTC)]
                mm0, act0, dv0 = sMM.n, sACT.n, sDVE.n
                xsem = [sLD, sLDo]
                xb = [sLD.n, sLDo.n]
                xtot = [xb[tc % 2] + 128 * (tc // 2 + 1) for tc in range(NTC)]
                lqs = sWL
                sqs = sWS
                lq0 = [s.n for s in lqs]
                sq0 = [s.n for s in sqs]
                cc0 = sCC.n
                NAC = 10
                dvb = [dv0 + 2 * i for i in range(NTC + 1)]
                sr0 = [sSR[0].n, sSR[1].n]
                rrcnt = [0, 0]
                rrtot = [0] * (NTC + 1)
                for i in range(1, NTC + 1):
                    rrcnt[i % 2] += 32
                    rrtot[i] = sr0[i % 2] + rrcnt[i % 2]
                with nc.Block() as blk:
                    @blk.sync
                    def _(sync):
                        for tc in range(NTC):
                            if tc >= 2:
                                sync.wait_ge(sMM.h, mm0 + 8 * (tc - 1))
                            for ec in range(NEC):
                                sync.dma_start(
                                    x_a[tc % 2][ec][:],
                                    xT.ap()[ec * 128:(ec + 1) * 128,
                                            tc * TC:(tc + 1) * TC]
                                ).then_inc(xsem[tc % 2].h, 16)
                            for ci in csplit[tc]:
                                if ci % 2 != 0:
                                    continue
                                srcw, dstw, oc, off, w = tiles[ci]
                                if ci >= 6:
                                    sync.wait_ge(sCC.h, cc0 + ci - 5)
                                sync.dma_start(wtmp[ci % 6][:, 0:w],
                                               srcw.ap()[oc][:, off:off + w]
                                               ).then_inc(lqs[ci % 6].h, 16)

                    @blk.scalar
                    def _(s):
                        for tc in range(NTC):
                            for ec in range(NEC):
                                if ec == 0:
                                    s.wait_ge(xsem[tc % 2].h, xtot[tc])
                                if ec >= 2:
                                    s.wait_ge(sMM.h, mm0 + 8 * tc + ec - 1)
                                elif tc >= 1:
                                    s.wait_ge(sMM.h, mm0 + 8 * (tc - 1) + 7 + ec)
                                ins = s.activation(x2a[ec % 2][:],
                                                   x_a[tc % 2][ec][:], AF.Square)
                                if ec >= 1:
                                    ins.then_inc(sACT.h, 1)
                            s.activation(x2d[:], x_a[tc % 2][7][:], AF.Square
                                         ).then_inc(sACT.h, 1)
                            s.wait_ge(sDVE.h, dvb[tc] + 1)
                            if tc >= 2:
                                s.wait_ge(sSR[(tc - 1) % 2].h,
                                          rrtot[tc - 1])
                            s.activation(rrow[tc % 2][:], vrow[tc % 2][:], AF.Ln,
                                         bias=epsc[0:1, 0:1]).then_inc(sACT.h, 1)
                            s.activation(rrow[tc % 2][:], rrow[tc % 2][:], AF.Exp,
                                         scale=-0.5).then_inc(sACT.h, 1)
                            for ci in csplit[tc]:
                                if ci % 2 != 1:
                                    continue
                                srcw, dstw, oc, off, w = tiles[ci]
                                if ci >= 6:
                                    s.wait_ge(sCC.h, cc0 + ci - 5)
                                s.dma_start(wtmp[ci % 6][:, 0:w],
                                            srcw.ap()[oc][:, off:off + w]
                                            ).then_inc(lqs[ci % 6].h, 16)

                    @blk.tensor
                    def _(t):
                        for tc in range(NTC):
                            if tc >= 1:
                                t.wait_ge(sDVE.h, dvb[tc - 1] + 1)
                            t.wait_ge(xsem[tc % 2].h, xtot[tc])
                            for ec in range(NEC):
                                t.matmul(ps[6][0:1, 0:TC], ones1[:],
                                         x_a[tc % 2][ec][:],
                                         start=(ec == 0), stop=(ec == NEC - 1))
                                t.wait_ge(sACT.h, act0 + NAC * tc + ec + 1)
                                t.matmul(ps[7][0:1, 0:TC], ones1[:],
                                         x2a[ec % 2][:], start=(ec == 0),
                                         stop=(ec == NEC - 1)).then_inc(sMM.h, 1)

                    @blk.vector
                    def _(v):
                        for tc in range(NTC):
                            v.wait_ge(sMM.h, mm0 + 8 * (tc + 1))
                            if tc >= 2:
                                v.wait_ge(sSR[(tc - 1) % 2].h,
                                          rrtot[tc - 1])
                            v.tensor_scalar_mul(mrow[tc % 2][:], ps[6][0:1, 0:TC],
                                                1.0 / D)
                            v.tensor_mul(vrow[tc % 2][:], mrow[tc % 2][:],
                                         mrow[tc % 2][:])
                            v.scalar_tensor_tensor(vrow[tc % 2][:],
                                                   ps[7][0:1, 0:TC], 1.0 / D,
                                                   vrow[tc % 2][:], ALU.mult,
                                                   ALU.subtract).then_inc(sDVE.h, 1)
                            for ci in csplit[tc]:
                                srcw, dstw, oc, off, w = tiles[ci]
                                v.wait_ge(lqs[ci % 6].h,
                                          lq0[ci % 6] + 16 * (ci // 6 + 1))
                                if ci >= 6:
                                    v.wait_ge(sqs[ci % 6].h,
                                              sq0[ci % 6] + 16 * ((ci - 6) // 6 + 1))
                                v.tensor_copy(wtmpb[ci % 6][:, 0:w],
                                              wtmp[ci % 6][:, 0:w]
                                              ).then_inc(sCC.h, 1)
                            v.wait_ge(sACT.h, act0 + NAC * (tc + 1))
                            v.scalar_tensor_tensor(nrow[tc % 2][:],
                                                   mrow[tc % 2][:], -1.0,
                                                   rrow[tc % 2][:], ALU.mult,
                                                   ALU.mult).then_inc(sDVE.h, 1)

                    @blk.gpsimd
                    def _(g):
                        for tc in range(NTC):
                            if tc >= 1:
                                g.wait_ge(sACT.h, act0 + NAC * tc)
                                g.dma_start(
                                    rows_d.ap()[0:1, (tc - 1) * TC:tc * TC],
                                    rrow[(tc - 1) % 2][:]
                                ).then_inc(sSR[tc % 2].h, 16)
                                g.wait_ge(sDVE.h, dvb[tc])
                                g.dma_start(
                                    rows_d.ap()[1:2, (tc - 1) * TC:tc * TC],
                                    nrow[(tc - 1) % 2][:]
                                ).then_inc(sSR[tc % 2].h, 16)
                            for ci in csplit[tc]:
                                srcw, dstw, oc, off, w = tiles[ci]
                                g.wait_ge(sCC.h, cc0 + ci + 1)
                                g.dma_start(dstw.ap()[oc][:, off:off + w],
                                            wtmpb[ci % 6][:, 0:w]
                                            ).then_inc(sqs[ci % 6].h, 16)
                        g.wait_ge(sACT.h, act0 + NAC * NTC)
                        g.dma_start(
                            rows_d.ap()[0:1, (NTC - 1) * TC:NTC * TC],
                            rrow[(NTC - 1) % 2][:]).then_inc(sSR[0].h, 16)
                        g.wait_ge(sDVE.h, dvb[NTC])
                        g.dma_start(
                            rows_d.ap()[1:2, (NTC - 1) * TC:NTC * TC],
                            nrow[(NTC - 1) % 2][:]).then_inc(sSR[0].h, 16)
                sLD.n = xb[0] + 128 * 8
                sLDo.n = xb[1] + 128 * 8
                for b in range(6):
                    lqs[b].n = lq0[b] + 16 * nbsem[b]
                    sqs[b].n = sq0[b] + 16 * nbsem[b]
                sCC.n = cc0 + ncast
                sSR[0].n = sr0[0] + rrcnt[0]
                sSR[1].n = sr0[1] + rrcnt[1]
                sMM.inc(8 * NTC)
                sACT.inc(NAC * NTC)
                sDVE.n = dvb[NTC]
                with nc.Block() as blk:
                    @blk.sync
                    def _(sync):
                        sync.wait_ge(sLD.h, sLD.n)
                        sync.wait_ge(sLDo.h, sLDo.n)
                        for b in range(6):
                            sync.wait_ge(lqs[b].h, lqs[b].n)
                            sync.wait_ge(sqs[b].h, sqs[b].n)
                        sync.wait_ge(sSR[0].h, sSR[0].n)
                        sync.wait_ge(sSR[1].h, sSR[1].n)

            x_in = [[sb(pc, f"x_in{j}_{i}", [128, TC], F32) for i in range(NEC)]
                    for j in range(2)]
            xn = [sb(pc, f"xn{i}", [128, TC + 1], F32) for i in range(NEC)]
            dif = [sb(pc, f"dif{i}", [128, TC], F32) for i in range(NEC)]
            x1 = [sb(pc, f"x1_{i}", [128, TC], F32) for i in range(NEC)]
            kb = [sb(pc, f"kb{i}", [128, TC], BF16) for i in range(NEC)]
            vb = [sb(pc, f"vb{i}", [128, TC], BF16) for i in range(NEC)]
            rb = [sb(pc, f"rb{i}", [128, TC], BF16) for i in range(NEC)]
            wkvr = [sb(pc, f"wkvr{i}", [128, TC], BF16) for i in range(NEC)]
            xq = [sb(pc, f"xq{i}", [128, TC], BF16) for i in range(NEC)]
            x2t = [sb(pc, f"x2t{i}", [128, TC], F32) for i in range(2)]
            rm_t = [sb(pc, f"rm_t{i}", [128, TC], F32) for i in range(2)]
            att_t = [sb(pc, f"att_t{i}", [128, TC], F32) for i in range(2)]
            kf = sb(pc, "kf", [128, TC], F32)
            vf = [sb(pc, f"vf{i}", [128, TC], F32) for i in range(4)]
            rfp = [sb(pc, f"rfp{i}", [128, TC], F32) for i in range(4)]
            e1 = [sb(pc, f"e1_{i}", [128, TC], F32) for i in range(4)]
            e2 = [sb(pc, f"e2_{i}", [128, TC], F32) for i in range(4)]
            rcb = [sb(pc, f"rcb{i}", [128, TC], F32) for i in range(4)]
            sc_b = [sb(pc, f"sc_b{i}", [128, TC], F32) for i in range(4)]
            rr_t = sb(pc, "rr_t", [128, TC], F32)
            s1_t = sb(pc, "s1_t", [128, TC], F32)
            e2v = sb(pc, "e2v", [128, TC], F32)
            sc_a = sb(pc, "sc_a", [128, TC], F32)
            lnb = sb(pc, "lnb", [128, TC], F32)
            wkv = sb(pc, "wkv", [128, TC], F32)
            run_amax = sb(pc, "run_amax", [128, TC], F32)
            amax_tm = sb(pc, "amax_tm", [128, 2], F32)
            pack_tm = sb(pc, "pack_tm", [128, 4], F32)
            pack_row = sb(pc, "pack_row", [4, 128], F32)
            rnd_qi = sb(pc, "rnd_qi", [128, TC], I32)
            m_row = sb(pc, "m_row", [1, TC], F32)
            var_row = sb(pc, "var_row", [1, TC], F32)
            rstd_row = sb(pc, "rstd_row", [1, TC], F32)
            nmr_row = sb(pc, "nmr_row", [1, TC], F32)
            rstd_b = [sb(pc, f"rstd_b{i}", [128, TC], F32) for i in range(2)]
            nmr_b = [sb(pc, f"nmr_b{i}", [128, TC], F32) for i in range(2)]
            rstd1_b = sb(pc, "rstd1_b", [128, TC], F32)
            nmr1_b = sb(pc, "nmr1_b", [128, TC], F32)
            r127_b = sb(pc, "r127_b", [128, TC], F32)
            s_b = sb(pc, "s_b", [128, TC], F32)
            xn1_t = [sb(pc, f"xn1_t{i}", [128, TC], BF16) for i in range(2)]

            def ln_stats1(src_tiles, row_a, row_b, seg):
                mm0, act0 = sMM.n, sACT.n
                with nc.Block() as blk:
                    @blk.scalar
                    def _(s, mm0=mm0):
                        for ec in range(NEC):
                            if ec >= 2:
                                s.wait_ge(sMM.h, mm0 + 2 * (ec - 1))
                            s.activation(x2t[ec % 2][:], src_tiles[ec][:, 0:TC],
                                         AF.Square).then_inc(sACT.h, 1)

                    @blk.tensor
                    def _(t, mm0=mm0, act0=act0):
                        for ec in range(NEC):
                            t.matmul(ps[6][0:1, 0:TC], ones1[:],
                                     src_tiles[ec][:, 0:TC],
                                     start=(ec == 0), stop=(ec == NEC - 1))
                            t.wait_ge(sACT.h, act0 + ec + 1)
                            t.matmul(ps[7][0:1, 0:TC], ones1[:], x2t[ec % 2][:],
                                     start=(ec == 0), stop=(ec == NEC - 1)
                                     ).then_inc(sMM.h, 2)
                sACT.inc(NEC)
                sMM.inc(2 * NEC)
                with nc.Block() as blk:
                    @blk.vector
                    def _(v):
                        v.tensor_scalar_mul(m_row[:], ps[6][0:1, 0:TC], 1.0 / D)
                        v.tensor_mul(var_row[:], m_row[:], m_row[:])
                        v.scalar_tensor_tensor(var_row[:], ps[7][0:1, 0:TC],
                                               1.0 / D, var_row[:], ALU.mult,
                                               ALU.subtract).then_inc(sDVE.h, 1)
                    sDVE.inc(1)

                    @blk.scalar
                    def _(s):
                        s.wait_ge(sDVE.h, sDVE.n)
                        s.activation(rstd_row[:], var_row[:], AF.Ln,
                                     bias=epsc[0:1, 0:1])
                        s.activation(rstd_row[:], rstd_row[:], AF.Exp, scale=-0.5
                                     ).then_inc(sACT.h, 1)
                    sACT.inc(1)

                    @blk.vector
                    def _(v):
                        v.wait_ge(sACT.h, sACT.n)
                        v.scalar_tensor_tensor(nmr_row[:], m_row[:], -1.0,
                                               rstd_row[:], ALU.mult, ALU.mult)
                with nc.Block() as blk:
                    @blk.gpsimd
                    def _(g):
                        g.dma_start(rows_d.ap()[row_a:row_a + 1, seg:seg + TC],
                                    rstd_row[:]).then_inc(sSP.h, 16)
                        g.dma_start(rows_d.ap()[row_b:row_b + 1, seg:seg + TC],
                                    nmr_row[:]).then_inc(sSP.h, 16)
                    sSP.inc(32)

                    @blk.sync
                    def _(sync):
                        sync.wait_ge(sSP.h, sSP.n)
                        sync.dma_start(
                            rstd1_b[:], rows_d.ap()[row_a:row_a + 1, seg:seg + TC]
                            .to_broadcast((128, TC))).then_inc(sLB1.h, 16)
                        sync.dma_start(
                            nmr1_b[:], rows_d.ap()[row_b:row_b + 1, seg:seg + TC]
                            .to_broadcast((128, TC))).then_inc(sLB1.h, 16)
                        sLB1.inc(32)
                        sync.wait_ge(sLB1.h, sLB1.n)

            xcnt = [sLD.n, sLDo.n]
            xint = [0] * (NTC + 1)
            b2tot = [0] * (NTC + 1)
            with nc.Block() as blk:
                @blk.sync
                def _(sync):
                    for ec in range(NEC):
                        sync.dma_start(x_in[0][ec][:],
                                       xT.ap()[ec * 128:(ec + 1) * 128, 0:TC]
                                       ).then_inc(sLD.h, 16)
                        sLD.inc(16)
                    sync.dma_start(rstd_b[0][:], rows_d.ap()[0:1, 0:TC]
                                   .to_broadcast((128, TC))).then_inc(sLB2.h, 16)
                    sync.dma_start(nmr_b[0][:], rows_d.ap()[1:2, 0:TC]
                                   .to_broadcast((128, TC))).then_inc(sLB2.h, 16)
                    sLB2.inc(32)
                    sync.wait_ge(sLD.h, sLD.n)
                    sync.wait_ge(sLB2.h, sLB2.n)
            xcnt[0] += 128
            xint[0] = xcnt[0]
            b2tot[0] = sLB2.n

            VBl, ABl, TBl = [], [], []
            for tc in range(NTC):
                seg = tc * TC
                cp = tc % 2
                v0, a0, t0 = sDVE.n, sACT.n, sMM.n
                b1t = sLB1.n + 32
                spt = sSP.n + 32
                x1p = sST.n
                sxp = sSX.n
                VBl.append(v0); ABl.append(a0); TBl.append(t0)
                pv0 = VBl[tc - 1] if tc >= 1 else None
                pa0 = ABl[tc - 1] if tc >= 1 else None
                pt0 = TBl[tc - 1] if tc >= 1 else None
                Vdif = lambda e, b=v0: b + e + 1
                Vrm = lambda e, b=v0: b + 8 + e + 1
                Vkb = lambda e, b=v0: b + 16 + e + 1
                Vvb = lambda e, b=v0: b + 24 + e + 1
                Vb_ = lambda e, b=v0: b + 32 + 2 * e + 1
                Vwkvr = lambda e, b=v0: b + 32 + 2 * e + 2
                Vabsm = lambda e, b=v0: b + 48 + e
                Vpack = lambda b=v0: b + 56
                Vxq = lambda e, b=v0: b + 56 + e + 1
                Vx1a = lambda e, b=v0: b + 64 + e + 1
                Vfd = lambda e, b=v0: b + 72 + 2 * e + 1
                Vx1b = lambda e, b=v0: b + 72 + 2 * e + 2
                Asig = lambda e, b=a0: b + e + 1
                Akv = lambda e, b=a0: b + 8 + 4 * e + 1
                Ae = lambda e, b=a0: b + 8 + 4 * e + 2
                Ar = lambda e, b=a0: b + 8 + 4 * e + 3
                Arcb = lambda e, b=a0: b + 8 + 4 * e + 4
                Aabs = lambda e, b=a0: b + 41 + e
                Arec = lambda b=a0: b + 49
                Apack = lambda b=a0: b + 50
                Awo = lambda e, b=a0: b + 50 + e + 1
                Afb = lambda e, b=a0: b + 58 + e + 1
                Tk = lambda e, b=t0: b + 3 * e + 1
                Tv = lambda e, b=t0: b + 3 * e + 2
                Tr = lambda e, b=t0: b + 3 * e + 3
                Ttr = lambda g2, b=t0: b + 24 + g2 + 1
                Tpk = lambda b=t0: b + 27
                Two = lambda e, b=t0: b + 27 + e + 1
                Tf = lambda e, b=t0: b + 35 + e + 1

                with nc.Block() as blk:
                    @blk.vector
                    def _(v, tc=tc, seg=seg, cp=cp, v0=v0, a0=a0, t0=t0,
                          b1t=b1t, x1p=x1p,
                          pv0=pv0, pa0=pa0, pt0=pt0, Vdif=Vdif, Asig=Asig,
                          Aabs=Aabs, Ae=Ae, Arcb=Arcb, Ar=Ar, Ttr=Ttr, Awo=Awo,
                          Tf=Tf, Afb=Afb):
                        for ec in range(NEC):
                            if tc >= 1:
                                if ec == 0:
                                    v.wait_ge(xsem[tc % 2].h, xint[tc])
                                    v.wait_ge(sLB2.h, b2tot[tc])
                                v.wait_ge(sACT.h, pa0 + 41 + ec)
                            v.tensor_copy(xn[ec][:, 0:1], V["prevcol"][:, ec:ec + 1])
                            v.tensor_mul(xn[ec][:, 1:TC + 1], x_in[cp][ec][:],
                                         rstd_b[cp][:])
                            v.tensor_add(xn[ec][:, 1:TC + 1], xn[ec][:, 1:TC + 1],
                                         nmr_b[cp][:])
                            v.tensor_sub(dif[ec][:], xn[ec][:, 1:TC + 1],
                                         xn[ec][:, 0:TC]).then_inc(sDVE.h, 1)
                        for ec in range(NEC):
                            if ec >= 2:
                                v.wait_ge(sACT.h, Asig(ec - 2))
                            elif tc >= 1:
                                v.wait_ge(sACT.h, pa0 + 58 + 7 + ec)
                            v.scalar_tensor_tensor(rm_t[ec % 2][:], dif[ec][:],
                                                   V["tmr"][:, ec:ec + 1],
                                                   xn[ec][:, 0:TC], ALU.mult,
                                                   ALU.add).then_inc(sDVE.h, 1)
                        if tc >= 1:
                            v.wait_ge(sMM.h, pt0 + 24)
                        for ec in range(NEC):
                            v.scalar_tensor_tensor(kb[ec][:], dif[ec][:],
                                                   V["tmk"][:, ec:ec + 1],
                                                   xn[ec][:, 0:TC], ALU.mult,
                                                   ALU.add).then_inc(sDVE.h, 1)
                        for ec in range(NEC):
                            v.scalar_tensor_tensor(vb[ec][:], dif[ec][:],
                                                   V["tmv"][:, ec:ec + 1],
                                                   xn[ec][:, 0:TC], ALU.mult,
                                                   ALU.add).then_inc(sDVE.h, 1)
                        for ec in range(NEC):
                            v.tensor_copy(V["prevcol"][:, ec:ec + 1],
                                          xn[ec][:, TC:TC + 1])
                        for ec in range(NEC):
                            v.wait_ge(sACT.h, Ae(ec))
                            if ec >= 4:
                                v.wait_ge(sACT.h, Arcb(ec - 4))
                            elif tc >= 1:
                                v.wait_ge(sACT.h, pa0 + 8 + 4 * (4 + ec) + 4)
                            v.tensor_mul(e2v[:], e2[ec % 4][:], vf[ec % 4][:])
                            v.tensor_tensor_scan(sc_a[:], e1[ec % 4][:], e2v[:],
                                                 V["stA"][:, ec:ec + 1],
                                                 ALU.mult, ALU.add)
                            v.tensor_tensor_scan(sc_b[ec % 4][:], e1[ec % 4][:],
                                                 e2[ec % 4][:],
                                                 V["stB"][:, ec:ec + 1],
                                                 ALU.mult, ALU.add
                                                 ).then_inc(sDVE.h, 1)
                            v.tensor_copy(V["stA"][:, ec:ec + 1],
                                          sc_a[:, TC - 1:TC])
                            v.tensor_copy(V["stB"][:, ec:ec + 1],
                                          sc_b[ec % 4][:, TC - 1:TC])
                            v.wait_ge(sACT.h, Arcb(ec))
                            v.tensor_mul(wkv[:], sc_a[:], rcb[ec % 4][:])
                            v.wait_ge(sACT.h, Ar(ec))
                            if tc >= 1 and ec == 0:
                                v.wait_ge(sMM.h, pt0 + 35)
                            v.tensor_mul(wkvr[ec][:], wkv[:], rfp[ec % 4][:]
                                         ).then_inc(sDVE.h, 1)
                        for ec in range(1, NEC):
                            v.wait_ge(sACT.h, Aabs(ec))
                            v.tensor_max(run_amax[:], run_amax[:], x2t[ec % 2][:]
                                         ).then_inc(sDVE.h, 1)
                        for g2 in range(2):
                            v.wait_ge(sMM.h, Ttr(g2))
                            ins = v.reduce_max(amax_tm[:, g2:g2 + 1],
                                               ps[6 + g2][0:128, 0:128],
                                               axis=AX.X)
                            if g2 == 1:
                                ins.then_inc(sDVE.h, 1)
                        for ec in range(NEC):
                            if ec == 0:
                                v.wait_ge(sLB1.h, b1t)
                                if tc >= 1:
                                    v.wait_ge(sMM.h, pt0 + 43)
                            v.tensor_mul(dif[0][:], xn[ec][:, 1:TC + 1], r127_b[:])
                            v.tensor_copy(rnd_qi[:], dif[0][:])
                            v.tensor_copy(dif[1][:], rnd_qi[:])
                            v.tensor_sub(dif[2][:], dif[0][:], dif[1][:])
                            v.scalar_tensor_tensor(dif[1][:], dif[2][:], 0.5,
                                                   dif[1][:], ALU.is_gt, ALU.add)
                            v.tensor_scalar(dif[3][:], dif[2][:], -0.5, None,
                                            ALU.is_lt)
                            v.tensor_sub(xq[ec][:], dif[1][:], dif[3][:]
                                         ).then_inc(sDVE.h, 1)
                        for ec in range(NEC):
                            if ec == 0 and tc >= 1:
                                v.wait_ge(sST.h, x1p)
                            v.wait_ge(sACT.h, Awo(ec))
                            v.tensor_add(x1[ec][:], x_in[cp][ec][:],
                                         att_t[ec % 2][:]).then_inc(sDVE.h, 1)
                        for ec in range(NEC):
                            v.wait_ge(sMM.h, Tf(ec))
                            v.scalar_tensor_tensor(
                                rm_t[ec % 2][:],
                                ps[2 if ec % 2 == 0 else 5][:, 0:TC],
                                V["wscale0"][:, ec:ec + 1], s_b[:], ALU.mult,
                                ALU.mult).then_inc(sDVE.h, 1)
                            v.wait_ge(sACT.h, Afb(ec))
                            v.tensor_add(x1[ec][:], x1[ec][:], att_t[ec % 2][:]
                                         ).then_inc(sDVE.h, 1)

                    @blk.scalar
                    def _(s, tc=tc, cp=cp, v0=v0, a0=a0, t0=t0, pv0=pv0, pa0=pa0,
                          Tk=Tk, Tv=Tv, Tr=Tr, Vb_=Vb_, Vrm=Vrm, Vwkvr=Vwkvr,
                          Vdif=Vdif, Vabsm=Vabsm, Tpk=Tpk, Two=Two, Vx1a=Vx1a,
                          Vfd=Vfd, Vx1b=Vx1b, Vpack=Vpack):
                        for ec in range(NEC):
                            s.wait_ge(sDVE.h, Vrm(ec))
                            s.activation(rb[ec][:], rm_t[ec % 2][:], AF.Sigmoid
                                         ).then_inc(sACT.h, 1)
                        for ec in range(NEC):
                            p = ec % 2
                            s.wait_ge(sMM.h, Tk(ec))
                            s.copy(kf[:], ps[0 if p == 0 else 3][:, 0:TC])
                            s.wait_ge(sMM.h, Tv(ec))
                            if ec >= 4:
                                s.wait_ge(sDVE.h, Vb_(ec - 4))
                            elif tc >= 1:
                                s.wait_ge(sDVE.h, pv0 + 32 + 2 * (4 + ec) + 1)
                            s.copy(vf[ec % 4][:], ps[1 if p == 0 else 4][:, 0:TC]
                                   ).then_inc(sACT.h, 1)
                            s.activation(rr_t[:], kf[:], AF.Relu,
                                         bias=V["umw"][:, ec:ec + 1])
                            s.activation(e1[ec % 4][:], rr_t[:], AF.Exp,
                                         scale=-1.0, bias=V["m2w"][:, ec:ec + 1])
                            s.activation(s1_t[:], kf[:], AF.Relu, scale=-1.0,
                                         bias=V["wmu"][:, ec:ec + 1])
                            s.activation(e2[ec % 4][:], s1_t[:], AF.Exp,
                                         scale=-1.0).then_inc(sACT.h, 1)
                            s.wait_ge(sMM.h, Tr(ec))
                            if ec >= 4:
                                s.wait_ge(sDVE.h, Vwkvr(ec - 4))
                            elif tc >= 1:
                                s.wait_ge(sDVE.h, pv0 + 32 + 2 * (4 + ec) + 2)
                            s.copy(rfp[ec % 4][:], ps[2 if p == 0 else 5][:, 0:TC]
                                   ).then_inc(sACT.h, 1)
                            s.wait_ge(sDVE.h, Vb_(ec))
                            s.activation(lnb[:], sc_b[ec % 4][:], AF.Ln)
                            s.activation(rcb[ec % 4][:], lnb[:], AF.Exp,
                                         scale=-1.0).then_inc(sACT.h, 1)
                        s.wait_ge(sDVE.h, Vdif(0))
                        s.activation(run_amax[:], xn[0][:, 1:TC + 1], AF.Abs
                                     ).then_inc(sACT.h, 1)
                        for ec in range(1, NEC):
                            s.wait_ge(sDVE.h, Vdif(ec))
                            if ec >= 3:
                                s.wait_ge(sDVE.h, Vabsm(ec - 2))
                            s.activation(x2t[ec % 2][:], xn[ec][:, 1:TC + 1],
                                         AF.Abs).then_inc(sACT.h, 1)
                        s.wait_ge(sDVE.h, Vpack())
                        s.activation(rr_t[:, 0:2], amax_tm[:], AF.Relu,
                                     bias=neps8c[:, 0:1])
                        s.activation(s1_t[:, 0:2], rr_t[:, 0:2], AF.Identity,
                                     bias=eps8c[:, 0:1])
                        s.activation(pack_tm[:, 2:4], s1_t[:, 0:2], AF.Identity,
                                     scale=1.0 / 127.0)
                        s.activation(rr_t[:, 0:2], s1_t[:, 0:2], AF.Ln)
                        s.activation(pack_tm[:, 0:2], rr_t[:, 0:2], AF.Exp,
                                     scale=-1.0, bias=ln127c[:, 0:1]
                                     ).then_inc(sACT.h, 1)
                        s.wait_ge(sMM.h, Tpk())
                        s.copy(pack_row[:], ps[6][0:4, 0:128]).then_inc(sACT.h, 1)
                        for ec in range(NEC):
                            s.wait_ge(sMM.h, Two(ec))
                            if ec >= 2:
                                s.wait_ge(sDVE.h, Vx1a(ec - 2))
                            elif tc >= 1:
                                s.wait_ge(sDVE.h, pv0 + 72 + 2 * (6 + ec) + 2)
                            s.activation(att_t[ec % 2][:],
                                         ps[6 if ec % 2 == 0 else 7][:, 0:TC],
                                         AF.Identity, bias=V["bo"][:, ec:ec + 1]
                                         ).then_inc(sACT.h, 1)
                        for ec in range(NEC):
                            s.wait_ge(sDVE.h, Vfd(ec))
                            if ec >= 2:
                                s.wait_ge(sDVE.h, Vx1b(ec - 2))
                            s.activation(att_t[ec % 2][:], rm_t[ec % 2][:],
                                         AF.Identity, bias=V["b0f"][:, ec:ec + 1]
                                         ).then_inc(sACT.h, 1)

                    @blk.tensor
                    def _(t, tc=tc, v0=v0, a0=a0, t0=t0, pv0=pv0, pa0=pa0,
                          Vkb=Vkb, Vvb=Vvb, Akv=Akv, Asig=Asig, Ar=Ar,
                          Vabsm=Vabsm, Vpack=Vpack, Vwkvr=Vwkvr, Apack=Apack,
                          Arec=Arec,
                          Awo=Awo, Vxq=Vxq, Vfd=Vfd):
                        for ec in range(NEC):
                            p = ec % 2
                            if ec == 0:
                                t.wait_ge(sDVE.h, Vkb(7))
                            if ec >= 2:
                                t.wait_ge(sACT.h, Akv(ec - 2))
                            for dc in range(NEC):
                                ins = t.matmul(
                                    ps[0 if p == 0 else 3][:, 0:TC],
                                    wk_sb[dc][:, ec * 128:(ec + 1) * 128],
                                    kb[dc][:], start=(dc == 0), stop=(dc == 7))
                                if dc == 7:
                                    ins.then_inc(sMM.h, 1)
                            if ec == 0:
                                t.wait_ge(sDVE.h, Vvb(7))
                            for dc in range(NEC):
                                ins = t.matmul(
                                    ps[1 if p == 0 else 4][:, 0:TC],
                                    wv_sb[dc][:, ec * 128:(ec + 1) * 128],
                                    vb[dc][:], start=(dc == 0), stop=(dc == 7))
                                if dc == 7:
                                    ins.then_inc(sMM.h, 1)
                            if ec == 0:
                                t.wait_ge(sACT.h, Asig(7))
                            if ec >= 2:
                                t.wait_ge(sACT.h, Ar(ec - 2))
                            elif tc >= 1:
                                t.wait_ge(sDVE.h, pv0 + 72 + 2 * (6 + ec) + 1)
                            for dc in range(NEC):
                                ins = t.matmul(
                                    ps[2 if p == 0 else 5][:, 0:TC],
                                    wr_sb[dc][:, ec * 128:(ec + 1) * 128],
                                    rb[dc][:], start=(dc == 0), stop=(dc == 7))
                                if dc == 7:
                                    ins.then_inc(sMM.h, 1)
                        t.wait_ge(sDVE.h, Vabsm(7))
                        for g2 in range(2):
                            t.transpose(ps[6 + g2][0:128, 0:128],
                                        run_amax[:, g2 * 128:(g2 + 1) * 128],
                                        ident[:]).then_inc(sMM.h, 1)
                        t.wait_ge(sACT.h, Arec())
                        t.transpose(ps[6][0:4, 0:128], pack_tm[:], ident[:]
                                    ).then_inc(sMM.h, 1)
                        for ec in range(NEC):
                            if ec == 0:
                                t.wait_ge(sDVE.h, Vwkvr(7))
                                t.wait_ge(sACT.h, Apack())
                            if ec >= 2:
                                t.wait_ge(sACT.h, Awo(ec - 2))
                            for dc in range(NEC):
                                ins = t.matmul(
                                    ps[6 if ec % 2 == 0 else 7][:, 0:TC],
                                    wo_sb[dc][:, ec * 128:(ec + 1) * 128],
                                    wkvr[dc][:], start=(dc == 0), stop=(dc == 7))
                                if dc == 7:
                                    ins.then_inc(sMM.h, 1)
                        for ec in range(NEC):
                            if ec == 0:
                                t.wait_ge(sDVE.h, Vxq(7))
                            if ec >= 2:
                                t.wait_ge(sDVE.h, Vfd(ec - 2))
                            else:
                                t.wait_ge(sACT.h, Ar(6 + ec))
                            for dc in range(NEC):
                                ins = t.matmul(
                                    ps[2 if ec % 2 == 0 else 5][:, 0:TC],
                                    wq0_sb[dc][:, ec * 128:(ec + 1) * 128],
                                    xq[dc][:], start=(dc == 0), stop=(dc == 7))
                                if dc == 7:
                                    ins.then_inc(sMM.h, 1)

                    @blk.sync
                    def _(sync, tc=tc, seg=seg, spt=spt, pv0=pv0):
                        sync.wait_ge(sSP.h, spt)
                        if tc >= 1:
                            sync.wait_ge(sDVE.h, pv0 + 72 + 15)
                        sync.dma_start(r127_b[:], rows_d.ap()[2:3, seg:seg + TC]
                                       .to_broadcast((128, TC))).then_inc(sLB1.h, 16)
                        sync.dma_start(s_b[:], rows_d.ap()[3:4, seg:seg + TC]
                                       .to_broadcast((128, TC))).then_inc(sLB1.h, 16)
                        if tc < NTC - 1:
                            np_ = (tc + 1) % 2
                            if tc >= 1:
                                sync.wait_ge(sDVE.h, pv0 + 64 + 8)
                            for ec in range(NEC):
                                sync.dma_start(
                                    x_in[np_][ec][:],
                                    xT.ap()[ec * 128:(ec + 1) * 128,
                                            seg + TC:seg + 2 * TC]
                                ).then_inc(xsem[np_].h, 16)
                            sync.dma_start(
                                rstd_b[np_][:],
                                rows_d.ap()[0:1, seg + TC:seg + 2 * TC]
                                .to_broadcast((128, TC))).then_inc(sLB2.h, 16)
                            sync.dma_start(
                                nmr_b[np_][:],
                                rows_d.ap()[1:2, seg + TC:seg + 2 * TC]
                                .to_broadcast((128, TC))).then_inc(sLB2.h, 16)

                    @blk.gpsimd
                    def _(g, seg=seg, a0=a0, v0=v0, Apack=Apack, Vx1b=Vx1b):
                        g.wait_ge(sACT.h, Apack())
                        g.dma_start(rows_d.ap()[2:3, seg:seg + TC],
                                    pack_row[0:2, :]).then_inc(sSP.h, 16)
                        g.dma_start(rows_d.ap()[3:4, seg:seg + TC],
                                    pack_row[2:4, :]).then_inc(sSP.h, 16)
                        for ec in range(NEC):
                            g.wait_ge(sDVE.h, Vx1b(ec))
                            g.dma_start(x1_d.ap()[ec * 128:(ec + 1) * 128,
                                        seg:seg + TC], x1[ec][:]
                                        ).then_inc(sST.h, 16)
                sDVE.inc(88)
                sACT.inc(66)
                sMM.inc(43)
                sLB1.inc(32)
                sSP.inc(32)
                sST.inc(128)
                if tc < NTC - 1:
                    xsem[(tc + 1) % 2].inc(128)
                    xcnt[(tc + 1) % 2] += 128
                    xint[tc + 1] = xcnt[(tc + 1) % 2]
                    sLB2.inc(32)
                    b2tot[tc + 1] = sLB2.n
                ln_stats1(x1, 4, 5, seg)
                dv0b = sDVE.n
                with nc.Block() as blk:
                    @blk.vector
                    def _(v, dv0b=dv0b, sxp=sxp):
                        for ec in range(NEC):
                            if ec >= 2:
                                v.wait_ge(sSX.h, sxp + 16 * ec)
                            else:
                                v.wait_ge(sSX.h, sxp)
                            v.tensor_mul(dif[0][:], x1[ec][:], rstd1_b[:])
                            v.tensor_add(xn1_t[ec % 2][:], dif[0][:], nmr1_b[:]
                                         ).then_inc(sDVE.h, 1)

                    @blk.gpsimd
                    def _(g, seg=seg, dv0b=dv0b):
                        for ec in range(NEC):
                            g.wait_ge(sDVE.h, dv0b + ec + 1)
                            g.dma_start(xn1_d.ap()[ec * 128:(ec + 1) * 128,
                                        seg:seg + TC], xn1_t[ec % 2][:]
                                        ).then_inc(sSX.h, 16)
                sDVE.inc(NEC)
                sSX.inc(16 * NEC)
            with nc.Block() as blk:
                @blk.sync
                def _(sync):
                    sync.wait_ge(sLD.h, sLD.n)
                    sync.wait_ge(sLDo.h, sLDo.n)
                    sync.wait_ge(sLB1.h, sLB1.n)
                    sync.wait_ge(sLB2.h, sLB2.n)
                    sync.wait_ge(sST.h, sST.n)
                    sync.wait_ge(sSP.h, sSP.n)
                    sync.wait_ge(sSX.h, sSX.n)

        with ExitStack() as pc:
            V_sb = sb(pc, "V_sb", [128, NKT * VROW], BF16)
            cbias = sb(pc, "cbias", [128, 64], F32)
            with nc.Block() as blk:
                @blk.vector
                def _(v):
                    v.memset(V_sb[:], 1.0)

            with ExitStack() as p4:
                kw_sb = [sb(p4, f"kw1sb{i}", [128, D], BF16) for i in range(NEC)]
                vw_sb = [sb(p4, f"vw1sb{i}", [128, D], BF16) for i in range(NEC)]
                xn1c = [sb(p4, f"xn1c{i}", [128, KCW], BF16) for i in range(NEC)]
                ktile = [sb(p4, f"ktile{i}", [128, KCW], BF16) for i in range(2)]
                k2t = [sb(p4, f"k2t{i}", [128, KCW], F32) for i in range(2)]
                knrow = [sb(p4, f"knrow{i}", [1, KCW], F32) for i in range(4)]
                with nc.Block() as blk:
                    @blk.sync
                    def _(sync):
                        for ec in range(NEC):
                            sync.dma_start(kw_sb[ec][:], wk1_b.ap()[ec]
                                           ).then_inc(sLD.h, 16)
                            sync.dma_start(vw_sb[ec][:], wv1_b.ap()[ec]
                                           ).then_inc(sLD.h, 16)
                            sLD.inc(32)
                        sync.wait_ge(sLD.h, sLD.n)
                for kc in range(NKC):
                    seg = kc * KCW
                    with nc.Block() as blk:
                        @blk.sync
                        def _(sync, seg=seg):
                            for ec in range(NEC):
                                sync.dma_start(xn1c[ec][:],
                                               xn1_d.ap()[ec * 128:(ec + 1) * 128,
                                               seg:seg + KCW]).then_inc(sLD.h, 16)
                                sLD.inc(16)
                            sync.wait_ge(sLD.h, sLD.n)
                    mm0, act0, st0 = sMM.n, sACT.n, sST.n
                    with nc.Block() as blk:
                        @blk.tensor
                        def _(t, mm0=mm0, act0=act0):
                            for ec in range(NEC):
                                if ec >= 2:
                                    t.wait_ge(sACT.h, act0 + 4 * (ec - 2) + 2)
                                pk = ps[ec % 2]
                                for dc in range(NEC):
                                    ins = t.matmul(
                                        pk[:, 0:KCW],
                                        kw_sb[ec][:, dc * 128:(dc + 1) * 128],
                                        xn1c[dc][:], start=(dc == 0), stop=(dc == 7))
                                    if dc == 7:
                                        ins.then_inc(sMM.h, 1)
                                t.wait_ge(sACT.h, act0 + 4 * ec + 2)
                                if ec >= 1:
                                    t.wait_ge(sACT.h, act0 + 4 * (ec - 1) + 4)
                                t.matmul(ps[2][0:1, 0:KCW], oneslo[:],
                                         k2t[ec % 2][:], start=True, stop=True)
                                t.matmul(ps[3][0:1, 0:KCW], oneshi[:],
                                         k2t[ec % 2][:], start=True, stop=True
                                         ).then_inc(sMM.h, 1)

                        @blk.scalar
                        def _(s, mm0=mm0, act0=act0, st0=st0):
                            for ec in range(NEC):
                                pk = ps[ec % 2]
                                s.wait_ge(sMM.h, mm0 + 2 * ec + 1)
                                if ec >= 2:
                                    s.wait_ge(sST.h, st0 + 48 * ec)
                                s.copy(ktile[ec % 2][:], pk[:, 0:KCW])
                                s.activation(k2t[ec % 2][:], pk[:, 0:KCW],
                                             AF.Square).then_inc(sACT.h, 2)
                                s.wait_ge(sMM.h, mm0 + 2 * ec + 2)
                                s.copy(knrow[(2 * ec) % 4][:], ps[2][0:1, 0:KCW])
                                s.copy(knrow[(2 * ec + 1) % 4][:],
                                       ps[3][0:1, 0:KCW]).then_inc(sACT.h, 2)

                        @blk.gpsimd
                        def _(g, seg=seg, act0=act0):
                            for ec in range(NEC):
                                g.wait_ge(sACT.h, act0 + 4 * ec + 2)
                                g.dma_start(K_d.ap()[ec * 128:(ec + 1) * 128,
                                            seg:seg + KCW],
                                            ktile[ec % 2][:]).then_inc(sST.h, 16)
                                g.wait_ge(sACT.h, act0 + 4 * ec + 4)
                                g.dma_start(kn2_d.ap()[2 * ec:2 * ec + 1,
                                            seg:seg + KCW],
                                            knrow[(2 * ec) % 4][:]
                                            ).then_inc(sST.h, 16)
                                g.dma_start(kn2_d.ap()[2 * ec + 1:2 * ec + 2,
                                            seg:seg + KCW],
                                            knrow[(2 * ec + 1) % 4][:]
                                            ).then_inc(sST.h, 16)
                    sMM.inc(2 * NEC)
                    sACT.inc(4 * NEC)
                    sST.inc(48 * NEC)
                    mm0, act0 = sMM.n, sACT.n
                    with nc.Block() as blk:
                        @blk.tensor
                        def _(t, mm0=mm0, act0=act0):
                            for idx in range(2 * (KCW // 128)):
                                g2, nh2 = divmod(idx, 2)
                                if idx >= 2:
                                    t.wait_ge(sACT.h, act0 + idx - 1)
                                pv = ps[4 + idx % 2]
                                for dc in range(NEC):
                                    ins = t.matmul(
                                        pv[:, 0:512],
                                        xn1c[dc][:, g2 * 128:(g2 + 1) * 128],
                                        vw_sb[dc][:, nh2 * 512:(nh2 + 1) * 512],
                                        start=(dc == 0), stop=(dc == 7))
                                    if dc == 7:
                                        ins.then_inc(sMM.h, 1)

                        @blk.scalar
                        def _(s, kc=kc, mm0=mm0):
                            for idx in range(2 * (KCW // 128)):
                                g2, nh2 = divmod(idx, 2)
                                gg = kc * (KCW // 128) + g2
                                base = gg * VROW
                                pv = ps[4 + idx % 2]
                                s.wait_ge(sMM.h, mm0 + idx + 1)
                                dst = V_sb[:, base:base + VROW].rearrange(
                                    "p (h c) -> p h c", h=16)[:, 8 * nh2:8 * nh2 + 8,
                                                              0:64]
                                src = pv[:, 0:512].rearrange("p (h c) -> p h c", h=8)
                                s.copy(dst, src).then_inc(sACT.h, 1)
                    sMM.inc(2 * (KCW // 128))
                    sACT.inc(2 * (KCW // 128))
                with nc.Block() as blk:
                    @blk.sync
                    def _(sync):
                        sync.wait_ge(sST.h, sST.n)

            with ExitStack() as p5:
                qw_sb = [sb(p5, f"qw1sb{i}", [128, D], BF16) for i in range(NEC)]
                xn1q = [sb(p5, f"xn1q{i}", [128, TQ], BF16) for i in range(NEC)]
                qtile = [sb(p5, f"qtile{i}", [128, QCW], BF16) for i in range(2)]
                knrow_l = sb(p5, "knrow_l", [1, T], F32)
                kmax_row = sb(p5, "kmax_row", [1, NH], F32)
                kmaxd8 = sb(p5, "kmaxd8", [1, NH], F32)
                qh_l = [sb(p5, f"qh_l{i}", [64, TQ], BF16) for i in range(2)]
                qh_sq = [sb(p5, f"qh_sq{i}", [64, TQ], F32) for i in range(2)]
                qn_row = sb(p5, "qn_row", [1, TQ], F32)
                qm4 = sb(p5, "qm4", [1, NQC], F32)
                cneg = sb(p5, "cneg", [1, 64], F32)
                with nc.Block() as blk:
                    @blk.sync
                    def _(sync):
                        for ec in range(NEC):
                            sync.dma_start(qw_sb[ec][:], wq1_b.ap()[ec]
                                           ).then_inc(sLD.h, 16)
                            sLD.inc(16)
                        with sync.If_eq(flag_reg, 0):
                            for ec in range(NEC):
                                sync.dma_start(xn1q[ec][:],
                                               xn1_d.ap()[ec * 128:(ec + 1) * 128,
                                               0:TQ]).then_inc(sLD.h, 16)
                        with sync.Else():
                            for ec in range(NEC):
                                sync.dma_start(xn1q[ec][:],
                                               xn1_d.ap()[ec * 128:(ec + 1) * 128,
                                               TQ:T]).then_inc(sLD.h, 16)
                        sLD.inc(16 * NEC)
                        sync.wait_ge(sLD.h, sLD.n)
                ld0, dv0 = sLD.n, sDVE.n
                with nc.Block() as blk:
                    @blk.sync
                    def _(sync, ld0=ld0, dv0=dv0):
                        for h in range(NH):
                            if h >= 1:
                                sync.wait_ge(sDVE.h, dv0 + h)
                            sync.dma_start(knrow_l[:], kn2_d.ap()[h:h + 1, :]
                                           ).then_inc(sLD.h, 16)

                    @blk.vector
                    def _(v, ld0=ld0):
                        for h in range(NH):
                            v.wait_ge(sLD.h, ld0 + 16 * (h + 1))
                            v.reduce_max(kmax_row[:, h:h + 1], knrow_l[:],
                                         axis=AX.X).then_inc(sDVE.h, 1)
                sLD.inc(16 * NH)
                sDVE.inc(NH)
                with nc.Block() as blk:
                    @blk.scalar
                    def _(s):
                        s.activation(kmaxd8[:], kmax_row[:], AF.Sqrt
                                     ).then_inc(sACT.h, 1)
                    sACT.inc(1)

                    @blk.vector
                    def _(v):
                        v.wait_ge(sACT.h, sACT.n)
                        v.tensor_scalar_mul(kmaxd8[:], kmaxd8[:], 1.0 / 8.0)
                for ec in range(NEC):
                    mm0, act0, st0 = sMM.n, sACT.n, sST.n
                    with nc.Block() as blk:
                        @blk.tensor
                        def _(t, ec=ec, act0=act0):
                            for qc in range(NQC):
                                if qc >= 2:
                                    t.wait_ge(sACT.h, act0 + qc - 1)
                                pq = ps[qc % 2]
                                for dc in range(NEC):
                                    ins = t.matmul(
                                        pq[:, 0:QCW],
                                        qw_sb[ec][:, dc * 128:(dc + 1) * 128],
                                        xn1q[dc][:, qc * QCW:(qc + 1) * QCW],
                                        start=(dc == 0), stop=(dc == 7))
                                    if dc == 7:
                                        ins.then_inc(sMM.h, 1)

                        @blk.scalar
                        def _(s, mm0=mm0, st0=st0):
                            for qc in range(NQC):
                                pq = ps[qc % 2]
                                s.wait_ge(sMM.h, mm0 + qc + 1)
                                if qc >= 2:
                                    s.wait_ge(sST.h, st0 + 16 * qc)
                                s.copy(qtile[qc % 2][:], pq[:, 0:QCW]
                                       ).then_inc(sACT.h, 1)

                        @blk.gpsimd
                        def _(g, ec=ec, act0=act0):
                            for qc in range(NQC):
                                g.wait_ge(sACT.h, act0 + qc + 1)
                                g.dma_start(Q_d.ap()[ec * 128:(ec + 1) * 128,
                                            qc * QCW:(qc + 1) * QCW],
                                            qtile[qc % 2][:]).then_inc(sST.h, 16)
                    sMM.inc(NQC)
                    sACT.inc(NQC)
                    sST.inc(16 * NQC)
                with nc.Block() as blk:
                    @blk.sync
                    def _(sync):
                        sync.wait_ge(sST.h, sST.n)
                ld0, mm0, act0, dv0 = sLD.n, sMM.n, sACT.n, sDVE.n
                with nc.Block() as blk:
                    @blk.sync
                    def _(sync, ld0=ld0, act0=act0):
                        for h in range(NH):
                            if h >= 2:
                                sync.wait_ge(sACT.h, act0 + 6 * (h - 2) + 1)
                            sync.dma_start(qh_l[h % 2][:],
                                           Q_d.ap()[h * 64:(h + 1) * 64, :]
                                           ).then_inc(sLD.h, 16)

                    @blk.scalar
                    def _(s, ld0=ld0, mm0=mm0, dv0=dv0):
                        for h in range(NH):
                            s.wait_ge(sLD.h, ld0 + 16 * (h + 1))
                            s.activation(qh_sq[h % 2][:], qh_l[h % 2][:], AF.Square
                                         ).then_inc(sACT.h, 1)
                            for qc in range(NQC):
                                s.wait_ge(sMM.h, mm0 + 4 * h + qc + 1)
                                s.copy(qn_row[:, qc * QCW:(qc + 1) * QCW],
                                       ps[4 + (h % 2) * 0][0:1, 0:QCW] if False
                                       else ps[4 + (h + qc) % 2][0:1, 0:QCW]
                                       ).then_inc(sACT.h, 1)
                            s.wait_ge(sDVE.h, dv0 + 2 * h + 1)
                            s.activation(qm4[:], qm4[:], AF.Sqrt).then_inc(sACT.h, 1)

                    @blk.tensor
                    def _(t, act0=act0):
                        for h in range(NH):
                            t.wait_ge(sACT.h, act0 + 6 * h + 1)
                            for qc in range(NQC):
                                if 4 * h + qc >= 2:
                                    t.wait_ge(sACT.h, act0 + 6 * ((4 * h + qc - 2)
                                              // 4) + ((4 * h + qc - 2) % 4) + 2)
                                t.matmul(ps[4 + (h + qc) % 2][0:1, 0:QCW],
                                         ones1[0:64, :],
                                         qh_sq[h % 2][:, qc * QCW:(qc + 1) * QCW],
                                         start=True, stop=True).then_inc(sMM.h, 1)

                    @blk.vector
                    def _(v, act0=act0):
                        for h in range(NH):
                            v.wait_ge(sACT.h, act0 + 6 * h + 5)
                            for qc in range(NQC):
                                ins = v.reduce_max(qm4[:, qc:qc + 1],
                                                   qn_row[:, qc * QCW:(qc + 1) * QCW],
                                                   axis=AX.X)
                                if qc == NQC - 1:
                                    ins.then_inc(sDVE.h, 1)
                            v.wait_ge(sDVE.h, dv0 + 2 * h + 1)
                            v.wait_ge(sACT.h, act0 + 6 * h + 6)
                            v.tensor_scalar_mul(qm4[:], qm4[:],
                                                kmaxd8[0:1, h:h + 1])
                            v.tensor_scalar_mul(cneg[:, 4 * h:4 * h + 4], qm4[:],
                                                -1.0).then_inc(sDVE.h, 1)
                sLD.inc(16 * NH)
                sMM.inc(4 * NH)
                sACT.inc(6 * NH)
                sDVE.inc(2 * NH)
                with nc.Block() as blk:
                    @blk.gpsimd
                    def _(g):
                        g.dma_start(rows_d.ap()[6:7, 0:64], cneg[:]
                                    ).then_inc(sST.h, 16)
                    sST.inc(16)

                    @blk.sync
                    def _(sync):
                        sync.wait_ge(sST.h, sST.n)
                        sync.dma_start(cbias[:], rows_d.ap()[6:7, 0:64]
                                       .to_broadcast((128, 64))).then_inc(sLD.h, 16)
                        sLD.inc(16)
                        sync.wait_ge(sLD.h, sLD.n)

            with ExitStack() as p6:
                khh2 = [sb(p6, f"khh2_{i}", [128, T // 2], BF16) for i in range(2)]
                qh2 = [sb(p6, f"qh2_{i}", [128, TQ], BF16) for i in range(2)]
                pt = [sb(p6, f"pt{i}", [128, QCW], BF16) for i in range(4)]
                oraw = sb(p6, "oraw", [65, TQ], F32)
                ob_t = sb(p6, "ob_t", [65, QCW], F32)
                reciprow = sb(p6, "reciprow", [1, TQ], F32)
                rb64 = sb(p6, "rb64", [64, TQ], F32)
                o_store = sb(p6, "o_store", [64, TQ], BF16)
                ld0, st0, mm0, act0, dv0 = sLD.n, sST.n, sMM.n, sACT.n, sDVE.n
                lb0, sp0 = sLB1.n, sSP.n
                NJP = NKT // 2

                def cE(h, qc, j):
                    return act0 + 129 * h + 32 * qc + 2 * (j + 1)

                def cS(h, qc, j):
                    return mm0 + 128 * h + 32 * qc + (1 if j == 0 else 2 * j)

                def cAV(h, qc, i):
                    return (mm0 + 128 * h + 32 * qc
                            + (2 * i + 3 if i < NJP - 1 else 32))

                def gV(h, qc):
                    return dv0 + 5 * h + qc + 1

                def prev_jq(h, qc, j):
                    if j >= 2:
                        return (h, qc, j - 2)
                    if qc >= 1:
                        return (h, qc - 1, j + NJP - 2)
                    if h >= 1:
                        return (h - 1, NQC - 1, j + NJP - 2)
                    return None

                with nc.Block() as blk:
                    @blk.sync
                    def _(sync, ld0=ld0, sp0=sp0, mm0=mm0):
                        for h in range(NH):
                            if h >= 1:
                                sync.wait_ge(sSP.h, sp0 + 16 * h)
                                sync.dma_start(rb64[:], rows_d.ap()[7:8, 0:TQ]
                                               .to_broadcast((64, TQ))
                                               ).then_inc(sLB1.h, 16)
                            if h >= 2:
                                sync.wait_ge(sMM.h, mm0 + 128 * (h - 1))
                            sync.dma_start(khh2[h % 2][0:64, :],
                                           K_d.ap()[h * 64:(h + 1) * 64, 0:T // 2]
                                           ).then_inc(sLD.h, 16)
                            sync.dma_start(khh2[h % 2][64:128, :],
                                           K_d.ap()[h * 64:(h + 1) * 64, T // 2:T]
                                           ).then_inc(sLD.h, 16)
                            sync.dma_start(qh2[h % 2][0:64, :],
                                           Q_d.ap()[h * 64:(h + 1) * 64, :]
                                           ).then_inc(sLD.h, 16)
                            sync.dma_start(qh2[h % 2][64:128, :],
                                           Q_d.ap()[h * 64:(h + 1) * 64, :]
                                           ).then_inc(sLD.h, 16)
                        sync.wait_ge(sSP.h, sp0 + 16 * NH)
                        sync.dma_start(rb64[:], rows_d.ap()[7:8, 0:TQ]
                                       .to_broadcast((64, TQ))).then_inc(sLB1.h, 16)

                    @blk.tensor
                    def _(t, ld0=ld0, mm0=mm0, act0=act0, dv0=dv0):
                        for h in range(NH):
                            t.wait_ge(sLD.h, ld0 + 64 * (h + 1))
                            for qc in range(NQC):
                                qs = slice(qc * QCW, (qc + 1) * QCW)

                                def spair(j, t=t, h=h, qc=qc, qs=qs):
                                    pv = prev_jq(h, qc, j)
                                    if pv is not None:
                                        t.wait_ge(sACT.h, cE(*pv))
                                    bs = 2 * (j % 2)
                                    t.matmul(ps[bs][:, 0:QCW],
                                             khh2[h % 2][0:64,
                                                         j * 128:(j + 1) * 128],
                                             qh2[h % 2][0:64, qs],
                                             start=True, stop=True)
                                    t.matmul(ps[bs + 1][:, 0:QCW],
                                             khh2[h % 2][64:128,
                                                         j * 128:(j + 1) * 128],
                                             qh2[h % 2][64:128, qs],
                                             start=True, stop=True
                                             ).then_inc(sMM.h, 1)

                                def avg(i, t=t, h=h, qc=qc, qs=qs):
                                    t.wait_ge(sACT.h, cE(h, qc, i))
                                    if i == 0:
                                        if qc >= 1:
                                            t.wait_ge(sDVE.h, gV(h, qc - 1))
                                        elif h >= 1:
                                            t.wait_ge(sDVE.h, gV(h - 1, NQC - 1))
                                    for half, ktk in ((0, i), (1, i + NJP)):
                                        vb = V_sb[:, ktk * VROW + 65 * h:
                                                  ktk * VROW + 65 * h + 65]
                                        pm = pt[(2 * i + half) % 4]
                                        t.matmul(
                                            ps[4][0:65, 0:QCW], vb[0:64, :],
                                            pm[0:64, :],
                                            start=(i == 0 and half == 0),
                                            stop=(i == NJP - 1 and half == 1))
                                        ins = t.matmul(
                                            ps[5][0:65, 0:QCW], vb[64:128, :],
                                            pm[64:128, :],
                                            start=(i == 0 and half == 0),
                                            stop=(i == NJP - 1 and half == 1))
                                        if half == 1:
                                            ins.then_inc(sMM.h, 1)

                                for j in range(NJP):
                                    spair(j)
                                    if j >= 1:
                                        avg(j - 1)
                                avg(NJP - 1)

                    @blk.scalar
                    def _(s, sp0=sp0, mm0=mm0, act0=act0, dv0=dv0):
                        for h in range(NH):
                            for qc in range(NQC):
                                for j in range(NJP):
                                    s.wait_ge(sMM.h, cS(h, qc, j))
                                    pv = prev_jq(h, qc, j)
                                    if pv is not None:
                                        s.wait_ge(sMM.h, cAV(*pv))
                                    bs = 2 * (j % 2)
                                    s.activation(pt[(2 * j) % 4][:],
                                                 ps[bs][:, 0:QCW], AF.Exp,
                                                 scale=0.125,
                                                 bias=cbias[:, 4 * h + qc:
                                                            4 * h + qc + 1]
                                                 ).then_inc(sACT.h, 1)
                                    s.activation(pt[(2 * j + 1) % 4][:],
                                                 ps[bs + 1][:, 0:QCW], AF.Exp,
                                                 scale=0.125,
                                                 bias=cbias[:, 4 * h + qc:
                                                            4 * h + qc + 1]
                                                 ).then_inc(sACT.h, 1)
                            if h >= 1:
                                s.wait_ge(sSP.h, sp0 + 16 * h)
                            s.wait_ge(sDVE.h, gV(h, NQC - 1))
                            s.activation(reciprow[:], oraw[64:65, :], AF.Ln)
                            s.activation(reciprow[:], reciprow[:], AF.Exp,
                                         scale=-1.0).then_inc(sACT.h, 1)

                    @blk.gpsimd
                    def _(g, act0=act0, dv0=dv0):
                        for h in range(NH):
                            g.wait_ge(sACT.h, act0 + 129 * h + 129)
                            g.dma_start(rows_d.ap()[7:8, 0:TQ], reciprow[:]
                                        ).then_inc(sSP.h, 16)
                            g.wait_ge(sDVE.h, dv0 + 5 * h + 5)
                            g.dma_start(o_d.ap()[h * 64:(h + 1) * 64, :],
                                        o_store[:]).then_inc(sST.h, 16)

                    @blk.vector
                    def _(v, lb0=lb0, st0=st0, mm0=mm0):
                        for h in range(NH):
                            for qc in range(NQC):
                                v.wait_ge(sMM.h, cAV(h, qc, NJP - 1))
                                v.tensor_copy(ob_t[:], ps[5][0:65, 0:QCW])
                                v.tensor_add(oraw[0:65, qc * QCW:(qc + 1) * QCW],
                                             ps[4][0:65, 0:QCW], ob_t[:]
                                             ).then_inc(sDVE.h, 1)
                            v.wait_ge(sLB1.h, lb0 + 16 * (h + 1))
                            if h >= 1:
                                v.wait_ge(sST.h, st0 + 16 * h)
                            v.tensor_mul(o_store[:], oraw[0:64, :], rb64[:]
                                         ).then_inc(sDVE.h, 1)
                sLD.inc(64 * NH)
                sLB1.inc(16 * NH)
                sSP.inc(16 * NH)
                sST.inc(16 * NH)
                sMM.inc(128 * NH)
                sACT.inc(129 * NH)
                sDVE.inc(5 * NH)
                with nc.Block() as blk:
                    @blk.sync
                    def _(sync):
                        sync.wait_ge(sST.h, sST.n)
                        sync.wait_ge(sLD.h, sLD.n)
                        sync.wait_ge(sLB1.h, sLB1.n)
                        sync.wait_ge(sSP.h, sSP.n)

        with ExitStack() as pc:
            acc = [sb(pc, f"acc{i}", [128, TQ], F32) for i in range(NEC)]
            with nc.Block() as blk:
                @blk.sync
                def _(sync):
                    with sync.If_eq(flag_reg, 0):
                        for ec in range(NEC):
                            sync.dma_start(acc[ec][:],
                                           x1_d.ap()[ec * 128:(ec + 1) * 128, 0:TQ]
                                           ).then_inc(sLD.h, 16)
                    with sync.Else():
                        for ec in range(NEC):
                            sync.dma_start(acc[ec][:],
                                           x1_d.ap()[ec * 128:(ec + 1) * 128, TQ:T]
                                           ).then_inc(sLD.h, 16)
                    sLD.inc(16 * NEC)
                    sync.wait_ge(sLD.h, sLD.n)
            with ExitStack() as p7:
                o_sb = [sb(p7, f"o_sb{i}", [128, TQ], BF16) for i in range(NEC)]
                ow_sb = [sb(p7, f"owsb{i}", [128, D], BF16) for i in range(NEC)]
                wo_ev = [sb(p7, f"wo_ev{i}", [128, QCW], F32) for i in range(2)]
                with nc.Block() as blk:
                    @blk.sync
                    def _(sync):
                        for dc in range(NEC):
                            sync.dma_start(o_sb[dc][:],
                                           o_d.ap()[dc * 128:(dc + 1) * 128, :]
                                           ).then_inc(sLD.h, 16)
                            sLD.inc(16)
                        for ec in range(NEC):
                            sync.dma_start(ow_sb[ec][:], wo1_b.ap()[ec]
                                           ).then_inc(sLD.h, 16)
                            sLD.inc(16)
                        sync.wait_ge(sLD.h, sLD.n)
                mm0, act0, dv0 = sMM.n, sACT.n, sDVE.n
                with nc.Block() as blk:
                    @blk.tensor
                    def _(t, act0=act0):
                        i = 0
                        for ec in range(NEC):
                            for qc in range(NQC):
                                if i >= 2:
                                    t.wait_ge(sACT.h, act0 + i - 1)
                                pw = ps[i % 2]
                                for dc in range(NEC):
                                    ins = t.matmul(
                                        pw[:, 0:QCW],
                                        ow_sb[ec][:, dc * 128:(dc + 1) * 128],
                                        o_sb[dc][:, qc * QCW:(qc + 1) * QCW],
                                        start=(dc == 0), stop=(dc == 7))
                                    if dc == 7:
                                        ins.then_inc(sMM.h, 1)
                                i += 1

                    @blk.scalar
                    def _(s, mm0=mm0, dv0=dv0):
                        i = 0
                        for ec in range(NEC):
                            for qc in range(NQC):
                                s.wait_ge(sMM.h, mm0 + i + 1)
                                if i >= 2:
                                    s.wait_ge(sDVE.h, dv0 + i - 1)
                                s.copy(wo_ev[i % 2][:], ps[i % 2][:, 0:QCW]
                                       ).then_inc(sACT.h, 1)
                                i += 1

                    @blk.vector
                    def _(v, act0=act0):
                        i = 0
                        for ec in range(NEC):
                            for qc in range(NQC):
                                v.wait_ge(sACT.h, act0 + i + 1)
                                v.tensor_add(acc[ec][:, qc * QCW:(qc + 1) * QCW],
                                             acc[ec][:, qc * QCW:(qc + 1) * QCW],
                                             wo_ev[i % 2][:]).then_inc(sDVE.h, 1)
                                i += 1
                sMM.inc(NEC * NQC)
                sACT.inc(NEC * NQC)
                sDVE.inc(NEC * NQC)
            with ExitStack() as p8:
                xq1 = [sb(p8, f"xq1_{i}", [128, QCW], BF16) for i in range(NEC)]
                ha = [sb(p8, f"ha{i}", [128, QCW], BF16) for i in range(NHC)]
                hb = [sb(p8, f"hb{i}", [128, QCW], BF16) for i in range(NHC)]
                wt1 = [sb(p8, f"wt1_{i}", [128, D], BF16) for i in range(2)]
                wt2 = [sb(p8, f"wt2_{i}", [128, D], BF16) for i in range(2)]
                wth = [sb(p8, f"wth{i}", [128, NHC * 128], BF16) for i in range(2)]
                sil_t = sb(p8, "sil_t", [128, QCW], BF16)
                u_t = sb(p8, "u_t", [128, QCW], BF16)
                w3_ev = [sb(p8, f"w3_ev{i}", [128, QCW], F32) for i in range(2)]
                for qc in range(NQC):
                    with nc.Block() as blk:
                        @blk.sync
                        def _(sync, qc=qc):
                            with sync.If_eq(flag_reg, 0):
                                for ec in range(NEC):
                                    sync.dma_start(
                                        xq1[ec][:],
                                        xn1_d.ap()[ec * 128:(ec + 1) * 128,
                                                   qc * QCW:(qc + 1) * QCW]
                                    ).then_inc(sLD.h, 16)
                            with sync.Else():
                                for ec in range(NEC):
                                    sync.dma_start(
                                        xq1[ec][:],
                                        xn1_d.ap()[ec * 128:(ec + 1) * 128,
                                                   TQ + qc * QCW:
                                                   TQ + (qc + 1) * QCW]
                                    ).then_inc(sLD.h, 16)
                            sLD.inc(16 * NEC)
                            sync.wait_ge(sLD.h, sLD.n)
                    ld0, mm0, act0, dv0 = sLD.n, sMM.n, sACT.n, sDVE.n
                    with nc.Block() as blk:
                        @blk.sync
                        def _(sync, ld0=ld0, mm0=mm0):
                            for hc in range(NHC):
                                if hc >= 2:
                                    sync.wait_ge(sMM.h, mm0 + 2 * (hc - 1))
                                sync.dma_start(wt1[hc % 2][:], w1t_b.ap()[hc]
                                               ).then_inc(sLD.h, 16)
                                sync.dma_start(wt2[hc % 2][:], w2t_b.ap()[hc]
                                               ).then_inc(sLD.h, 16)

                        @blk.tensor
                        def _(t, ld0=ld0, act0=act0):
                            for hc in range(NHC):
                                t.wait_ge(sLD.h, ld0 + 32 * (hc + 1))
                                if hc >= 2:
                                    t.wait_ge(sACT.h, act0 + 2 * (hc - 1))
                                p1 = ps[2 * (hc % 2)]
                                p2 = ps[2 * (hc % 2) + 1]
                                for dc in range(NEC):
                                    ins = t.matmul(
                                        p1[:, 0:QCW],
                                        wt1[hc % 2][:, dc * 128:(dc + 1) * 128],
                                        xq1[dc][:], start=(dc == 0), stop=(dc == 7))
                                    if dc == 7:
                                        ins.then_inc(sMM.h, 1)
                                for dc in range(NEC):
                                    ins = t.matmul(
                                        p2[:, 0:QCW],
                                        wt2[hc % 2][:, dc * 128:(dc + 1) * 128],
                                        xq1[dc][:], start=(dc == 0), stop=(dc == 7))
                                    if dc == 7:
                                        ins.then_inc(sMM.h, 1)

                        @blk.scalar
                        def _(s, mm0=mm0, dv0=dv0):
                            for hc in range(NHC):
                                p1 = ps[2 * (hc % 2)]
                                p2 = ps[2 * (hc % 2) + 1]
                                s.wait_ge(sMM.h, mm0 + 2 * hc + 1)
                                if hc >= 1:
                                    s.wait_ge(sDVE.h, dv0 + hc)
                                s.activation(sil_t[:], p1[:, 0:QCW], AF.Silu,
                                             scale=V["w1s"][:, hc:hc + 1])
                                s.wait_ge(sMM.h, mm0 + 2 * hc + 2)
                                s.activation(u_t[:], p2[:, 0:QCW], AF.Identity,
                                             scale=V["w2s"][:, hc:hc + 1]
                                             ).then_inc(sACT.h, 2)

                        @blk.vector
                        def _(v, act0=act0):
                            for hc in range(NHC):
                                v.wait_ge(sACT.h, act0 + 2 * hc + 2)
                                v.tensor_mul(ha[hc][:], sil_t[:], u_t[:]
                                             ).then_inc(sDVE.h, 1)
                    sLD.inc(32 * NHC)
                    sMM.inc(2 * NHC)
                    sACT.inc(2 * NHC)
                    sDVE.inc(NHC)
                    for src_t, dst_t, wmat, svec in ((ha, hb, h1t_b, "h1s"),
                                                     (hb, ha, h2t_b, "h2s")):
                        ld0, mm0, act0 = sLD.n, sMM.n, sACT.n
                        with nc.Block() as blk:
                            @blk.sync
                            def _(sync, wmat=wmat, ld0=ld0, mm0=mm0):
                                for oc in range(NHC):
                                    if oc >= 2:
                                        sync.wait_ge(sMM.h, mm0 + oc - 1)
                                    sync.dma_start(wth[oc % 2][:], wmat.ap()[oc]
                                                   ).then_inc(sLD.h, 16)

                            @blk.tensor
                            def _(t, src_t=src_t, ld0=ld0, act0=act0):
                                for oc in range(NHC):
                                    t.wait_ge(sLD.h, ld0 + 16 * (oc + 1))
                                    if oc >= 2:
                                        t.wait_ge(sACT.h, act0 + oc - 1)
                                    po = ps[4 + oc % 2]
                                    for ic in range(NHC):
                                        ins = t.matmul(
                                            po[:, 0:QCW],
                                            wth[oc % 2][:, ic * 128:(ic + 1) * 128],
                                            src_t[ic][:], start=(ic == 0),
                                            stop=(ic == NHC - 1))
                                        if ic == NHC - 1:
                                            ins.then_inc(sMM.h, 1)

                            @blk.scalar
                            def _(s, dst_t=dst_t, svec=svec, mm0=mm0):
                                for oc in range(NHC):
                                    po = ps[4 + oc % 2]
                                    s.wait_ge(sMM.h, mm0 + oc + 1)
                                    s.activation(dst_t[oc][:], po[:, 0:QCW],
                                                 AF.Gelu,
                                                 scale=V[svec][:, oc:oc + 1]
                                                 ).then_inc(sACT.h, 1)
                        sLD.inc(16 * NHC)
                        sMM.inc(NHC)
                        sACT.inc(NHC)
                    ld0, mm0, act0, dv0 = sLD.n, sMM.n, sACT.n, sDVE.n
                    with nc.Block() as blk:
                        @blk.sync
                        def _(sync, ld0=ld0, mm0=mm0):
                            for ec in range(NEC):
                                if ec >= 2:
                                    sync.wait_ge(sMM.h, mm0 + ec - 1)
                                sync.dma_start(wth[ec % 2][:], w3t_b.ap()[ec]
                                               ).then_inc(sLD.h, 16)

                        @blk.tensor
                        def _(t, ld0=ld0, act0=act0):
                            for ec in range(NEC):
                                t.wait_ge(sLD.h, ld0 + 16 * (ec + 1))
                                if ec >= 2:
                                    t.wait_ge(sACT.h, act0 + ec - 1)
                                po = ps[6 + ec % 2]
                                for ic in range(NHC):
                                    ins = t.matmul(
                                        po[:, 0:QCW],
                                        wth[ec % 2][:, ic * 128:(ic + 1) * 128],
                                        ha[ic][:], start=(ic == 0),
                                        stop=(ic == NHC - 1))
                                    if ic == NHC - 1:
                                        ins.then_inc(sMM.h, 1)

                        @blk.scalar
                        def _(s, mm0=mm0, dv0=dv0):
                            for ec in range(NEC):
                                po = ps[6 + ec % 2]
                                s.wait_ge(sMM.h, mm0 + ec + 1)
                                if ec >= 2:
                                    s.wait_ge(sDVE.h, dv0 + ec - 1)
                                s.activation(w3_ev[ec % 2][:], po[:, 0:QCW],
                                             AF.Identity,
                                             scale=V["w3s"][:, ec:ec + 1]
                                             ).then_inc(sACT.h, 1)

                        @blk.vector
                        def _(v, qc=qc, act0=act0):
                            for ec in range(NEC):
                                v.wait_ge(sACT.h, act0 + ec + 1)
                                v.tensor_add(acc[ec][:, qc * QCW:(qc + 1) * QCW],
                                             acc[ec][:, qc * QCW:(qc + 1) * QCW],
                                             w3_ev[ec % 2][:]).then_inc(sDVE.h, 1)
                    sLD.inc(16 * NEC)
                    sMM.inc(NEC)
                    sACT.inc(NEC)
                    sDVE.inc(NEC)
            with nc.Block() as blk:
                @blk.gpsimd
                def _(g):
                    for ec in range(NEC):
                        g.dma_start(y.ap()[ec * 128:(ec + 1) * 128, :], acc[ec][:]
                                    ).then_inc(sST.h, 16)
                    sST.inc(16 * NEC)
                    g.wait_ge(sST.h, sST.n)

                @blk.sync
                def _(sync):
                    sync.wait_ge(sLD.h, sLD.n)
    return nc


def _pack_vec(v, k):
    return np.ascontiguousarray(np.asarray(v, np.float32).reshape(k, 128).T)


def _tile_T(w, noc):
    wT = np.asarray(w, np.float32).T
    nk = wT.shape[0] // 128
    A = wT.reshape(nk, 128, noc, 128).transpose(2, 1, 0, 3)
    return np.ascontiguousarray(A.reshape(noc, 128, nk * 128))


def _rowchunk_T(w):
    wT = np.ascontiguousarray(np.asarray(w, np.float32).T)
    return wT.reshape(wT.shape[0] // 128, 128, wT.shape[1])


def kernel(**inputs):
    xin = np.asarray(inputs["L0_attn_x"], np.float32)
    base = {
        "ident_in": np.eye(128, dtype=np.float32),
        "oneslo_in": np.concatenate([np.ones(64, np.float32),
                                     np.zeros(64, np.float32)]).reshape(128, 1),
        "oneshi_in": np.concatenate([np.zeros(64, np.float32),
                                     np.ones(64, np.float32)]).reshape(128, 1),
        "l0_wkT": np.ascontiguousarray(
            np.asarray(inputs["L0_attn_Wk"], np.float32).T),
        "l0_wvT": np.ascontiguousarray(
            np.asarray(inputs["L0_attn_Wv"], np.float32).T),
        "l0_wrT": np.ascontiguousarray(
            np.asarray(inputs["L0_attn_Wr"], np.float32).T),
        "l0_woT": np.ascontiguousarray(
            np.asarray(inputs["L0_attn_Wo"], np.float32).T),
        "f0_w": np.ascontiguousarray(np.asarray(inputs["L0_ffn_weight"],
                                                np.float32)),
        "wq1": _tile_T(inputs["L1_attn_wq"], 8),
        "wk1": _tile_T(inputs["L1_attn_wk"], 8),
        "wv1": _rowchunk_T(inputs["L1_attn_wv"]),
        "wo1": _tile_T(inputs["L1_attn_wo"], 8),
        "w1t": _tile_T(inputs["L1_ffn_w1_w"], NHC),
        "w2t": _tile_T(inputs["L1_ffn_w2_w"], NHC),
        "h1t": _tile_T(inputs["L1_ffn_h1_w"], NHC),
        "h2t": _tile_T(inputs["L1_ffn_h2_w"], NHC),
        "w3t": _tile_T(inputs["L1_ffn_w3_w"], NEC),
        "vec_u": _pack_vec(inputs["L0_attn_u"], 8),
        "vec_w": _pack_vec(inputs["L0_attn_w"], 8),
        "vec_tmk": _pack_vec(inputs["L0_attn_tmk"], 8),
        "vec_tmv": _pack_vec(inputs["L0_attn_tmv"], 8),
        "vec_tmr": _pack_vec(inputs["L0_attn_tmr"], 8),
        "vec_bo": _pack_vec(inputs["L0_attn_bo"], 8),
        "vec_b0f": _pack_vec(inputs["L0_ffn_bias"], 8),
        "vec_w1s": _pack_vec(inputs["L1_ffn_w1_s"], 32),
        "vec_w2s": _pack_vec(inputs["L1_ffn_w2_s"], 32),
        "vec_h1s": _pack_vec(inputs["L1_ffn_h1_s"], 32),
        "vec_h2s": _pack_vec(inputs["L1_ffn_h2_s"], 32),
        "vec_w3s": _pack_vec(inputs["L1_ffn_w3_s"], 8),
    }
    in_maps = []
    for c in range(8):
        b, qh = divmod(c, 2)
        m = dict(base)
        m["xT"] = np.ascontiguousarray(xin[b].T)
        m["qflag"] = np.array([[qh]], np.int32)
        in_maps.append(m)
    nc = build()
    res = run_bass_kernel_spmd(nc, in_maps, core_ids=list(range(8)))
    out = np.empty((4, T, D), np.float32)
    for c in range(8):
        b, qh = divmod(c, 2)
        out[b, qh * TQ:(qh + 1) * TQ, :] = res.results[c]["y"].T
    return out


